# version 03
# baseline (speedup 1.0000x reference)
import numpy as np
from contextlib import ExitStack
import concourse.bass as bass
import concourse.mybir as mybir
from concourse.bass_utils import run_bass_kernel_spmd

F32 = mybir.dt.float32
AF = mybir.ActivationFunctionType
ALU = mybir.AluOpType
AX = mybir.AxisListType

NCORES = 8
T, D = 2048, 1024
SPC = 4
NQ = 4
QW = 512
HW = 256
ND = 8
NHC = 4
NKC = 16
H = 8
SCALE = 0.0625
LNEPS = 1e-5
GEPS = 1e-6
INV_D = 1.0 / 1024.0

ESEM = {'T': 't', 'V': 'v', 'S': 's', 'G': 'g'}


class Prog:

    def __init__(self):
        self.items = []
        self.val = {}
        self.w = {}
        self.r = {}

    def emit(self, eng, fn, reads=(), writes=(), extra=(), dma=None,
             sreads=()):
        waits = {}

        def add(mk, force=False):
            if mk is None:
                return
            sn, vv = mk
            if not force and eng in ESEM and sn == ESEM[eng]:
                return
            if waits.get(sn, -1) < vv:
                waits[sn] = vv

        force_all = eng in ('V', 'S')
        for k in reads:
            add(self.w.get(k), force=force_all)
        for k in sreads:
            add(self.w.get(k), force=True)
        for k in writes:
            add(self.w.get(k))
            for sn, vv in self.r.get(k, {}).items():
                add((sn, vv))
        for mk in extra:
            add(mk)
        if dma is not None:
            sem, step = dma, 16
        else:
            sem, step = ESEM[eng], 1
        self.items.append((eng, tuple(waits.items()), fn, sem, step))
        self.val[sem] = self.val.get(sem, 0) + step
        mk = (sem, self.val[sem])
        for k in list(reads) + list(sreads):
            d = self.r.setdefault(k, {})
            if d.get(mk[0], -1) < mk[1]:
                d[mk[0]] = mk[1]
        for k in writes:
            self.w[k] = mk
            self.r[k] = {}
        return mk

    def barrier_mark(self):
        return [(sn, vv) for sn, vv in self.val.items()]


def build(debug=False):
    nc = bass.Bass("TRN2", target_bir_lowering=False, debug=False)

    xin = nc.dram_tensor("xin", [SPC, D, T], F32, kind="ExternalInput").ap()
    wqd = nc.dram_tensor("wq", [2048, 1024], F32, kind="ExternalInput").ap()
    wkupd = nc.dram_tensor("wkup", [2048, 64], F32, kind="ExternalInput").ap()
    wvupd = nc.dram_tensor("wvup", [2048, 64], F32, kind="ExternalInput").ap()
    woTd = nc.dram_tensor("woT", [2048, 1024], F32, kind="ExternalInput").ap()
    wkvT2d = nc.dram_tensor("wkvT2", [1024, 128], F32, kind="ExternalInput").ap()
    wTd = nc.dram_tensor("wTd", [1024, 1024], F32, kind="ExternalInput").ap()
    rTd = nc.dram_tensor("rTd", [1024, 8], F32, kind="ExternalInput").ap()
    aTd = nc.dram_tensor("aTd", [1024, 256], F32, kind="ExternalInput").ap()
    bTd = nc.dram_tensor("bTd", [256, 1024], F32, kind="ExternalInput").ap()
    bcold = nc.dram_tensor("bcold", [1024, 1], F32, kind="ExternalInput").ap()
    identd = nc.dram_tensor("identd", [128, 128], F32, kind="ExternalInput").ap()
    onescd = nc.dram_tensor("onescd", [128, 1], F32, kind="ExternalInput").ap()
    onesrd = nc.dram_tensor("onesrd", [1, 128], F32, kind="ExternalInput").ap()
    selLd = nc.dram_tensor("selLd", [8, 512], F32, kind="ExternalInput").ap()
    selEd = nc.dram_tensor("selEd", [8, 256], F32, kind="ExternalInput").ap()
    epsd = nc.dram_tensor("epsd", [128, 1], F32, kind="ExternalInput").ap()
    xout = nc.dram_tensor("xout", [SPC, D, T], F32, kind="ExternalOutput").ap()
    dbg = {}
    if debug:
        for nm, shp in [("d_kvl", [128, T]), ("d_kvt", [128, NKC * 65]),
                        ("d_qt", [128, 4 * QW]), ("d_lr", [8, QW]),
                        ("d_ng", [128, 8]), ("d_on", [128, 4 * QW]),
                        ("d_ab", [128, 8 * HW]), ("d_lnrb", [8, 512]),
                        ("d_gf", [128, 16]), ("d_hs", [128, 2 * HW]),
                        ("d_wqk", [128, 512]), ("d_wov", [128, 1024]),
                        ("d_kvm2", [128, 2]), ("d_gsc", [128, 8]),
                        ("d_kvm_a", [128, 1]), ("d_kvm_b", [128, 1]),
                        ("d_xa", [128, T]), ("d_ltk", [128, 16]),
                        ("d_geg", [128, 8]), ("d_gt2", [128, 8]),
                        ("d_gmsk", [128, 8]), ("d_gem", [128, 8]),
                        ("d_gsc2", [128, 8]),
                        ("d_rr", [128, 512]), ("d_st", [34, 512]),
                        ("d_psb", [128, 512]), ("d_sqs", [128, 512])]:
            dbg[nm] = nc.dram_tensor(nm, shp, F32, kind="ExternalOutput").ap()

    P = Prog()

    with ExitStack() as stk:
        en = stk.enter_context

        def sb(name, shape):
            return en(nc.sbuf_tensor(name, shape, F32))

        xb = [sb(f"x{i}", [128, T]) for i in range(ND)]
        wqkT = [sb(f"wqk{i}", [128, 512]) for i in range(ND)]
        wkv2 = [sb(f"wkv{i}", [128, 128]) for i in range(ND)]
        wovT = [sb(f"wov{j}", [128, 1024]) for j in range(NHC)]
        wTs = [sb(f"wt{i}", [128, 1024]) for i in range(ND)]
        aTs = [sb(f"aa{i}", [128, 256]) for i in range(ND)]
        bTs = [sb(f"bb{g}", [128, 1024]) for g in range(2)]
        rTs = [sb(f"rr{i}", [128, 8]) for i in range(ND)]
        bcols = sb("bcols", [128, 8])
        negws = sb("negws", [128, 4])
        negwskv = sb("negwskv", [128, 1])
        negwsW = sb("negwsw", [128, 8])
        negwsA = sb("negwsa", [128, 2])
        negwsR = sb("negwsr", [8, 1])
        kvl = sb("kvl", [128, T])
        kvt = sb("kvt", [128, NKC * 65])
        lnrb = sb("lnrb", [8, 512])
        ident = sb("ident", [128, 128])
        onesc = sb("onesc", [128, 1])
        onesr = sb("onesr", [1, 128])
        selL = sb("selL", [8, 512])
        selE = sb("selE", [8, 256])
        negG = sb("negg", [128, 8])
        kvm = sb("kvm", [128, 1])
        kvm2 = sb("kvm2", [128, 2])
        epsc = sb("epsc", [128, 1])
        qt = [sb(f"q{j}", [128, 512]) for j in range(NHC)]
        pbuf = [sb(f"p{k}", [128, 512]) for k in range(2)]
        pbuf2 = [sb(f"pp{k}", [128, 512]) for k in range(2)]
        pball = [pbuf[0], pbuf[1], pbuf2[0], pbuf2[1]]
        sqs = sb("sqs", [128, 512])
        rrep = sb("rrep", [128, 512])
        brep = sb("brep", [128, 512])
        ab = [sb(f"ab{i}", [128, 256]) for i in range(ND)]
        stat = sb("stat", [34, 512])
        lrows = sb("lrows", [8, 512])
        rl = sb("rl", [8, 512])
        gate8 = sb("gate8", [8, 256])
        hsb = [sb(f"h{g}", [128, 256]) for g in range(2)]
        ltk = sb("ltk", [128, 16])
        geg = sb("geg", [128, 8])
        gt2 = sb("gt2", [128, 8])
        gmsk = sb("gmsk", [128, 8])
        gem = sb("gem", [128, 8])
        gsc = sb("gsc", [128, 8])
        gfin = sb("gfin", [128, 16])
        m1t = sb("m1t", [128, 1])
        nm1t = sb("nm1t", [128, 1])
        m2t = sb("m2t", [128, 1])
        lsum = sb("lsum", [128, 1])
        s2t = sb("s2t", [128, 1])
        dent = sb("dent", [128, 1])
        rdent = sb("rdent", [128, 1])
        zcol = sb("zcol", [128, 1])

        ps = [en(nc.psum_tensor(f"ps{k}", [128, 512], F32)) for k in range(8)]
        PK = [f"ps{k}" for k in range(8)]

        sems = {}
        for sn in ('t', 'v', 's', 'g', 'ld', 'ot', 'gd'):
            sems[sn] = en(nc.semaphore(sn))
        block = en(nc.Block())

        def ld(fn, writes):
            return P.emit('Y', fn, (), writes, dma='ld')

        def st(fn, reads):
            return P.emit('Y', fn, reads, (), dma='ot')

        def gd(fn, reads=(), writes=()):
            return P.emit('G', fn, reads, writes, dma='gd')

        ld(lambda E: E.dma_start(out=ident[:], in_=identd[:, :]), ['ident'])
        ld(lambda E: E.dma_start(out=onesc[:], in_=onescd[:, :]), ['onesc'])
        ld(lambda E: E.dma_start(out=onesr[:], in_=onesrd[:, :]), ['onesr'])
        ld(lambda E: E.dma_start(out=selL[:], in_=selLd[:, :]), ['selL'])
        ld(lambda E: E.dma_start(out=selE[:], in_=selEd[:, :]), ['selE'])
        ld(lambda E: E.dma_start(out=epsc[:], in_=epsd[:, :]), ['epsc'])
        for i in range(ND):
            ld(lambda E, i=i: E.dma_start(
                out=wkv2[i][:], in_=wkvT2d[i * 128:(i + 1) * 128, :]), [f'wkv{i}'])
            ld(lambda E, i=i: E.dma_start(
                out=wTs[i][:], in_=wTd[i * 128:(i + 1) * 128, :]), [f'wt{i}'])
            ld(lambda E, i=i: E.dma_start(
                out=aTs[i][:], in_=aTd[i * 128:(i + 1) * 128, :]), [f'aa{i}'])
            ld(lambda E, i=i: E.dma_start(
                out=rTs[i][:], in_=rTd[i * 128:(i + 1) * 128, :]), [f'rr{i}'])
        for g in range(2):
            ld(lambda E, g=g: E.dma_start(
                out=bTs[g][:], in_=bTd[g * 128:(g + 1) * 128, :]), [f'bb{g}'])
        for oc in range(ND):
            ld(lambda E, oc=oc: E.dma_start(
                out=bcols[:, oc:oc + 1], in_=bcold[oc * 128:(oc + 1) * 128, :]),
                ['bcols'])
        for i in range(ND):
            for q in range(NQ):
                ld(lambda E, i=i, q=q: E.dma_start(
                    out=xb[i][:, q * QW:(q + 1) * QW],
                    in_=xin[0, i * 128:(i + 1) * 128, q * QW:(q + 1) * QW]),
                    [f'x{i}q{q}'])

        P.emit('V', lambda E: E.memset(stat[:, :], 0.0), (), ['stat'])
        P.emit('V', lambda E: E.memset(lnrb[:, :], 0.0), (), ['lnrb'])
        P.emit('G', lambda E: E.memset(kvt[:, :], 1.0), (), ['kvt'])
        P.emit('V', lambda E: E.memset(lrows[:, :], 1.0), (), ['lrows'])
        P.emit('V', lambda E: E.memset(gsc[:, :], 0.0), (), ['gsc'])
        P.emit('V', lambda E: E.memset(zcol[:, :], 0.0), (), ['zcol'])

        for h in range(H):
            ld(lambda E, h=h: E.dma_start(
                out=qt[0][:, :], in_=wqd[h * 256:h * 256 + 128, 0:512]),
                ['qt0h0', 'qt0h1'])
            ld(lambda E, h=h: E.dma_start(
                out=qt[1][:, :], in_=wqd[h * 256:h * 256 + 128, 512:1024]),
                ['qt1h0', 'qt1h1'])
            ld(lambda E, h=h: E.dma_start(
                out=qt[2][:, :], in_=wqd[h * 256 + 128:h * 256 + 256, 0:512]),
                ['qt2h0', 'qt2h1'])
            ld(lambda E, h=h: E.dma_start(
                out=qt[3][:, :], in_=wqd[h * 256 + 128:h * 256 + 256, 512:1024]),
                ['qt3h0', 'qt3h1'])
            ld(lambda E, h=h: E.dma_start(
                out=pbuf[0][:, 0:64], in_=wkupd[h * 256:h * 256 + 128, :]),
                ['pb0'])
            ld(lambda E, h=h: E.dma_start(
                out=pbuf[0][:, 64:128],
                in_=wkupd[h * 256 + 128:h * 256 + 256, :]), ['pb0'])
            for i in range(ND):
                bk = i % 2
                q01 = i // 4
                c0 = (i % 4) * 128
                P.emit('T', lambda E, bk=bk, q01=q01, c0=c0: E.matmul(
                    ps[bk][0:128, 0:64], qt[q01][:, c0:c0 + 128],
                    pbuf[0][:, 0:64], start=True, stop=False),
                    [f'qt{q01}h0', f'qt{q01}h1', 'pb0'], [PK[bk]])
                P.emit('T', lambda E, bk=bk, q01=q01, c0=c0: E.matmul(
                    ps[bk][0:128, 0:64], qt[2 + q01][:, c0:c0 + 128],
                    pbuf[0][:, 64:128], start=False, stop=True),
                    [f'qt{2+q01}h0', f'qt{2+q01}h1', 'pb0'], [PK[bk]])
                P.emit('V', lambda E, bk=bk, i=i, h=h: E.tensor_copy(
                    wqkT[i][:, h * 64:(h + 1) * 64], ps[bk][0:128, 0:64]),
                    [PK[bk]], [f'wqkT{i}'])
        for jp in range(NHC):
            ld(lambda E, jp=jp: E.dma_start(
                out=pbuf[1][:, 0:64], in_=wvupd[jp * 512:jp * 512 + 128, :]),
                ['pb1'])
            ld(lambda E, jp=jp: E.dma_start(
                out=pbuf[1][:, 64:128],
                in_=wvupd[jp * 512 + 128:jp * 512 + 256, :]), ['pb1'])
            ld(lambda E, jp=jp: E.dma_start(
                out=pbuf[1][:, 128:192],
                in_=wvupd[jp * 512 + 256:jp * 512 + 384, :]), ['pb1'])
            ld(lambda E, jp=jp: E.dma_start(
                out=pbuf[1][:, 192:256],
                in_=wvupd[jp * 512 + 384:jp * 512 + 512, :]), ['pb1'])
            for ohalf in range(2):
                for k2 in range(2):
                    ld(lambda E, jp=jp, k2=k2, ohalf=ohalf: E.dma_start(
                        out=qt[k2][:, :],
                        in_=woTd[jp * 512 + k2 * 128:jp * 512 + (k2 + 1) * 128,
                                 ohalf * 512:(ohalf + 1) * 512]),
                        [f'qt{k2}h0', f'qt{k2}h1'])
                    ld(lambda E, jp=jp, k2=k2, ohalf=ohalf: E.dma_start(
                        out=qt[2 + k2][:, :],
                        in_=woTd[jp * 512 + 256 + k2 * 128:
                                 jp * 512 + 256 + (k2 + 1) * 128,
                                 ohalf * 512:(ohalf + 1) * 512]),
                        [f'qt{2+k2}h0', f'qt{2+k2}h1'])
                bk = 4 + ohalf
                P.emit('T', lambda E, bk=bk: E.matmul(
                    ps[bk][0:64, :], pbuf[1][:, 0:64], qt[0][:, :],
                    start=True, stop=False, tile_position=(0, 0)),
                    ['pb1', 'qt0h0', 'qt0h1'], [PK[bk]])
                P.emit('T', lambda E, bk=bk: E.matmul(
                    ps[bk][0:64, :], pbuf[1][:, 64:128], qt[1][:, :],
                    start=False, stop=True, tile_position=(0, 0)),
                    ['pb1', 'qt1h0', 'qt1h1'], [PK[bk]])
                P.emit('T', lambda E, bk=bk: E.matmul(
                    ps[bk][64:128, :], pbuf[1][:, 128:192], qt[2][:, :],
                    start=True, stop=False, tile_position=(0, 64)),
                    ['pb1', 'qt2h0', 'qt2h1'], [PK[bk]])
                P.emit('T', lambda E, bk=bk: E.matmul(
                    ps[bk][64:128, :], pbuf[1][:, 192:256], qt[3][:, :],
                    start=False, stop=True, tile_position=(0, 64)),
                    ['pb1', 'qt3h0', 'qt3h1'], [PK[bk]])
                P.emit('V', lambda E, bk=bk, jp=jp, ohalf=ohalf: E.tensor_copy(
                    wovT[jp][:, ohalf * 512:(ohalf + 1) * 512], ps[bk][:, :]),
                    [PK[bk]], [f'wovT{jp}'])
        if debug:
            st(lambda E: E.dma_start(out=dbg['d_wqk'][:, :], in_=wqkT[0][:, :]),
               ['wqkT0'])
            st(lambda E: E.dma_start(out=dbg['d_wov'][:, :], in_=wovT[0][:, :]),
               ['wovT0'])

        def colsum_neg(dst_fn, dst_key, lhs):
            n = len(lhs)
            for i, (lf, rkeys, kp) in enumerate(lhs):
                P.emit('T', lambda E, lf=lf, i=i, n=n, kp=kp: E.matmul(
                    ps[6][0:kp, 0:1], lf(), onesc[:, :],
                    start=(i == 0), stop=(i == n - 1)),
                    list(rkeys) + ['onesc'], [PK[6]])
            P.emit('S', lambda E, kp=lhs[0][2]: E.mul(
                dst_fn(), ps[6][0:kp, 0:1], -1.0), [PK[6]], [dst_key])

        for j in range(NHC):
            colsum_neg(lambda j=j: negws[:, j:j + 1], 'negws',
                       [(lambda i=i, j=j: wqkT[i][:, j * 128:(j + 1) * 128],
                         [f'wqkT{i}'], 128) for i in range(ND)])
        colsum_neg(lambda: negwskv[:, 0:1], 'negwskv',
                   [(lambda i=i: wkv2[i][:, :], [f'wkv{i}'], 128)
                    for i in range(ND)])
        for oc in range(ND):
            colsum_neg(lambda oc=oc: negwsW[:, oc:oc + 1], 'negwsW',
                       [(lambda i=i, oc=oc: wTs[i][:, oc * 128:(oc + 1) * 128],
                         [f'wt{i}'], 128) for i in range(ND)])
        for g in range(2):
            colsum_neg(lambda g=g: negwsA[:, g:g + 1], 'negwsA',
                       [(lambda i=i, g=g: aTs[i][:, g * 128:(g + 1) * 128],
                         [f'aa{i}'], 128) for i in range(ND)])
        colsum_neg(lambda: negwsR[:, 0:1], 'negwsR',
                   [(lambda i=i: rTs[i][:, :], [f'rr{i}'], 8)
                    for i in range(ND)])

        def ln_stats(w, bS, bQ, store_q=None):
            sqr = pbuf[0][0:1, 0:w]
            msq = pbuf[0][0:1, 256:256 + w] if w <= 256 else pbuf[1][0:1, 0:w]
            rrow = qt[0][0:1, 0:w] if w > 256 else pbuf[1][0:1, 0:w]
            brow = qt[1][0:1, 0:w] if w > 256 else pbuf[1][0:1, 256:256 + w]
            rk = 'qt0h0' if w > 256 else 'pb1'
            bk_ = 'qt1h0' if w > 256 else 'pb1'
            P.emit('V', lambda E, w=w: E.tensor_copy(
                stat[0:1, 0:w], ps[bS][0:1, 0:w]), [PK[bS]], ['stat'])
            P.emit('V', lambda E, w=w: E.tensor_copy(
                stat[32:33, 0:w], ps[bQ][32:33, 0:w]), [PK[bQ]], ['stat'])
            P.emit('T', lambda E, w=w: E.matmul(
                ps[6][0:1, 0:w], ident[0:33, 32:33], stat[0:33, 0:w],
                start=True, stop=True), ['stat', 'ident'], [PK[6]])
            P.emit('V', lambda E, w=w, sqr=sqr: E.tensor_copy(
                sqr, ps[6][0:1, 0:w]), [PK[6]], ['pb0'])
            P.emit('S', lambda E, w=w, msq=msq: E.activation(
                msq, stat[0:1, 0:w], AF.Square, scale=INV_D),
                ['stat'], ['pb0' if w <= 256 else 'pb1'])
            P.emit('V', lambda E, sqr=sqr, msq=msq: E.scalar_tensor_tensor(
                msq, sqr, INV_D, msq, ALU.mult, ALU.subtract),
                [], ['pb0' if w <= 256 else 'pb1'], sreads=['pb0', 'pb1'])
            P.emit('S', lambda E, msq=msq: E.activation(
                msq, msq, AF.Ln, bias=epsc[0:1, 0:1]),
                ['pb0', 'pb1', 'epsc'], ['pb0' if w <= 256 else 'pb1'])
            P.emit('S', lambda E, msq=msq, rrow=rrow: E.activation(
                rrow, msq, AF.Exp, scale=-0.5),
                ['pb0', 'pb1'], [rk])
            P.emit('V', lambda E, rrow=rrow, brow=brow: E.scalar_tensor_tensor(
                brow, stat[0:1, 0:w], INV_D, rrow,
                ALU.mult, ALU.mult), [rk], [bk_], sreads=['stat'])
            P.emit('T', lambda E, w=w, rrow=rrow: E.matmul(
                ps[6][:, 0:w], onesr[:, :], rrow, start=True, stop=True),
                [rk, 'onesr'], [PK[6]])
            P.emit('V', lambda E, w=w: E.tensor_copy(
                rrep[:, 0:w], ps[6][:, 0:w]), [PK[6]], ['rrep'])
            P.emit('T', lambda E, w=w, brow=brow: E.matmul(
                ps[6][:, 0:w], onesr[:, :], brow, start=True, stop=True),
                [bk_, 'onesr'], [PK[6]])
            P.emit('V', lambda E, w=w: E.tensor_copy(
                brep[:, 0:w], ps[6][:, 0:w]), [PK[6]], ['brep'])
            if store_q is not None:
                gd(lambda E, q=store_q, rrow=rrow: E.dma_start(
                    out=lnrb[q:q + 1, :], in_=rrow), [rk], ['lnrb'])
                gd(lambda E, q=store_q, brow=brow: E.dma_start(
                    out=lnrb[4 + q:5 + q, :], in_=brow), [bk_], ['lnrb'])

        for sq_i in range(SPC):
            XK = lambda i, q: f'x{i}q{q}'
            for q in range(NQ):
                qs = slice(q * QW, (q + 1) * QW)
                for i in range(ND):
                    sqt = sqs if i % 2 == 0 else pbuf[1]
                    sk = ['sqsA', 'sqsB'] if i % 2 == 0 else ['pb1']
                    P.emit('S', lambda E, i=i, qs=qs, sqt=sqt: E.activation(
                        sqt[:, :], xb[i][:, qs], AF.Square),
                        [XK(i, q)], sk)
                    P.emit('T', lambda E, i=i, qs=qs: E.matmul(
                        ps[0][0:1, 0:QW], onesc[:, :], xb[i][:, qs],
                        start=(i == 0), stop=(i == ND - 1),
                        tile_position=(0, 0)),
                        [XK(i, q), 'onesc'], [PK[0]])
                    P.emit('T', lambda E, sqt=sqt, i=i: E.matmul(
                        ps[1][32:33, 0:QW], onesc[:, :], sqt[:, :],
                        start=(i == 0), stop=(i == ND - 1),
                        tile_position=(0, 32)), sk + ['onesc'], [PK[1]])
                ln_stats(QW, 0, 1, store_q=q)
                for i in range(ND):
                    P.emit('T', lambda E, i=i, qs=qs: E.matmul(
                        ps[2][:, 0:QW], wkv2[i][:, :], xb[i][:, qs],
                        start=(i == 0), stop=(i == ND - 1)),
                        [f'wkv{i}', XK(i, q)], [PK[2]])
                P.emit('V', lambda E, qs=qs: E.tensor_tensor(
                    kvl[:, qs], ps[2][:, 0:QW], rrep[:, :], ALU.mult),
                    [PK[2], 'rrep'], [f'kvl{q}'])
                P.emit('V', lambda E, qs=qs: E.scalar_tensor_tensor(
                    kvl[:, qs], brep[:, :], negwskv[:, 0:1], kvl[:, qs],
                    ALU.mult, ALU.add),
                    [f'kvl{q}'], [f'kvl{q}'], sreads=['brep', 'negwskv'])
            KVLK = [f'kvl{q}' for q in range(NQ)]
            for cb in range(2):
                bk = 3 + cb
                for ci in range(8):
                    c = cb * 8 + ci
                    P.emit('T', lambda E, bk=bk, c=c, ci=ci: E.transpose(
                        ps[bk][:, ci * 64:(ci + 1) * 64],
                        kvl[0:64, c * 128:(c + 1) * 128], ident[0:64, 0:64]),
                        KVLK + ['ident'], [PK[bk]] if ci == 0 else [PK[bk]])
                for ci in range(8):
                    c = cb * 8 + ci
                    P.emit('V', lambda E, bk=bk, c=c, ci=ci: E.tensor_copy(
                        kvt[:, c * 65:c * 65 + 64],
                        ps[bk][:, ci * 64:(ci + 1) * 64]), [PK[bk]], ['kvt'])
            P.emit('V', lambda E: E.memset(kvm2[:, :], 0.0), (), ['kvm2'])
            P.emit('V', lambda E: E.reduce_max(
                kvm2[0:64, 0:1], kvl[0:64, :], axis=AX.X,
                apply_absolute_value=True), KVLK, ['kvm2'])
            P.emit('V', lambda E: E.reduce_max(
                kvm2[64:128, 1:2], kvl[64:128, :], axis=AX.X,
                apply_absolute_value=True), KVLK, ['kvm2'])
            if debug and sq_i == 0:
                st(lambda E: E.dma_start(out=dbg['d_kvl'][:, :], in_=kvl[:, :]),
                   KVLK)
                st(lambda E: E.dma_start(out=dbg['d_kvt'][:, :], in_=kvt[:, :]),
                   ['kvt'])
                st(lambda E: E.dma_start(out=dbg['d_lnrb'][:, :],
                                         in_=lnrb[:, :]), ['lnrb'])

            for q in range(NQ):
                qs = slice(q * QW, (q + 1) * QW)
                P.emit('T', lambda E, q=q: E.matmul(
                    ps[0][0:1, 0:QW], ident[0:8, q:q + 1], lnrb[0:8, :],
                    start=True, stop=True), ['lnrb', 'ident'], [PK[0]])
                P.emit('V', lambda E: E.tensor_copy(
                    qt[0][0:1, :], ps[0][0:1, 0:QW]), [PK[0]], ['qt0h0'])
                P.emit('T', lambda E: E.matmul(
                    ps[1][:, 0:QW], onesr[:, :], qt[0][0:1, :],
                    start=True, stop=True), ['qt0h0', 'onesr'], [PK[1]])
                P.emit('V', lambda E: E.tensor_copy(
                    rrep[:, :], ps[1][:, 0:QW]), [PK[1]], ['rrep'])
                P.emit('T', lambda E, q=q: E.matmul(
                    ps[0][0:1, 0:QW], ident[0:8, 4 + q:5 + q], lnrb[0:8, :],
                    start=True, stop=True), ['lnrb', 'ident'], [PK[0]])
                P.emit('V', lambda E: E.tensor_copy(
                    qt[0][0:1, :], ps[0][0:1, 0:QW]), [PK[0]], ['qt0h0'])
                P.emit('T', lambda E: E.matmul(
                    ps[1][:, 0:QW], onesr[:, :], qt[0][0:1, :],
                    start=True, stop=True), ['qt0h0', 'onesr'], [PK[1]])
                P.emit('V', lambda E: E.tensor_copy(
                    brep[:, :], ps[1][:, 0:QW]), [PK[1]], ['brep'])
                for j in range(NHC):
                    bk = 2 + j % 2
                    for i in range(ND):
                        P.emit('T', lambda E, bk=bk, i=i, j=j, qs=qs: E.matmul(
                            ps[bk][:, 0:QW], wqkT[i][:, j * 128:(j + 1) * 128],
                            xb[i][:, qs], start=(i == 0), stop=(i == ND - 1)),
                            [f'wqkT{i}', XK(i, q)], [PK[bk]])
                    P.emit('V', lambda E, bk=bk, j=j: E.tensor_tensor(
                        qt[j][:, :], ps[bk][:, 0:QW], rrep[:, :], ALU.mult),
                        [PK[bk], 'rrep'], [f'qt{j}h0', f'qt{j}h1'])
                    P.emit('V', lambda E, j=j: E.scalar_tensor_tensor(
                        qt[j][:, :], brep[:, :], negws[:, j:j + 1], qt[j][:, :],
                        ALU.mult, ALU.add),
                        [f'qt{j}h0', f'qt{j}h1'],
                        [f'qt{j}h0', f'qt{j}h1'], sreads=['brep', 'negws'])
                P.emit('V', lambda E: E.memset(ps[5][:, :], -1e30),
                       (), [PK[5]])
                for j in range(NHC):
                    P.emit('S', lambda E, j=j: E.activation(
                        sqs[:, :], qt[j][:, :], AF.Abs),
                        [f'qt{j}h0', f'qt{j}h1'], ['sqsA', 'sqsB'])
                    P.emit('T', lambda E, j=j: E.matmul(
                        ps[5][32 * j:32 * j + 2, 0:QW], kvm2[:, :], sqs[:, :],
                        start=True, stop=True,
                        tile_position=(0, 32 * j)),
                        ['kvm2', 'sqsA', 'sqsB'], [PK[5]])
                if debug and sq_i == 0 and q == 0:
                    P.emit('V', lambda E: E.tensor_copy(
                        pbuf[0][:, :], ps[5][:, 0:QW]), [PK[5]], ['pb0'])
                    st(lambda E: E.dma_start(out=dbg['d_psb'][:, :],
                                             in_=pbuf[0][:, :]), ['pb0'])
                    st(lambda E: E.dma_start(out=dbg['d_sqs'][:, :],
                                             in_=sqs[:, :]), ['sqsA', 'sqsB'])
                    st(lambda E: E.dma_start(out=dbg['d_kvm2'][:, :],
                                             in_=kvm2[:, :]), ['kvm2'])
                P.emit('V', lambda E: E.reduce_max(
                    gsc[:, 0:1], ps[5][:, 0:QW], axis=AX.X, negate=True),
                    [PK[5]], ['gsc'])
                P.emit('S', lambda E: E.mul(gsc[:, 1:2], gsc[:, 0:1], SCALE),
                       ['gsc'], ['gsc'])
                if debug and sq_i == 0 and q == 0:
                    st(lambda E: E.dma_start(out=dbg['d_gsc'][:, :],
                                             in_=gsc[:, :]), ['gsc'])
                for h in range(H):
                    r_h = 32 * (h // 2) + (h % 2)
                    P.emit('T', lambda E, r_h=r_h: E.matmul(
                        ps[0][0:1, 0:1], ident[0:128, r_h:r_h + 1],
                        gsc[0:128, 1:2], start=True, stop=True),
                        ['gsc', 'ident'], [PK[0]])
                    P.emit('V', lambda E, h=h: E.tensor_copy(
                        stat[0:1, 100 + h:101 + h], ps[0][0:1, 0:1]),
                        [PK[0]], ['stat'])
                    P.emit('T', lambda E, h=h: E.matmul(
                        ps[1][:, 0:1], onesr[:, :],
                        stat[0:1, 100 + h:101 + h], start=True, stop=True),
                        ['stat', 'onesr'], [PK[1]])
                    P.emit('V', lambda E, h=h: E.tensor_copy(
                        negG[:, h:h + 1], ps[1][:, 0:1]), [PK[1]], ['negG'])
                if debug and sq_i == 0 and q == 0:
                    st(lambda E: E.dma_start(out=dbg['d_ng'][:, :],
                                             in_=negG[:, :]), ['negG'])
                    for j in range(NHC):
                        st(lambda E, j=j: E.dma_start(
                            out=dbg['d_qt'][:, j * QW:(j + 1) * QW],
                            in_=qt[j][:, :]), [f'qt{j}h0', f'qt{j}h1'])

                for hp in range(NHC):
                    j = hp
                    qke, qko = f'qt{j}h0', f'qt{j}h1'
                    obe = 4 + hp % 2
                    obo = 6 + hp % 2

                    def eS(c, hf, j=j):
                        bk = (c % 2) + 2 * hf
                        qk = f'qt{j}h{hf}'
                        return P.emit('T', lambda E, bk=bk, c=c, hf=hf, j=j:
                                      E.matmul(
                            ps[bk][:, 0:QW],
                            kvl[64 * hf:64 * hf + 64, c * 128:(c + 1) * 128],
                            qt[j][64 * hf:64 * hf + 64, :],
                            start=True, stop=True,
                            tile_position=(64 * hf, 0)),
                            KVLK + [qk], [PK[bk]])

                    def eE(c, hf, hp=hp):
                        bk = (c % 2) + 2 * hf
                        pbi = (c % 2) + 2 * hf
                        h = 2 * hp + hf
                        return P.emit('S', lambda E, bk=bk, pbi=pbi, h=h:
                                      E.activation(
                            pball[pbi][:, :], ps[bk][:, 0:QW], AF.Exp,
                            bias=negG[:, h:h + 1], scale=SCALE),
                            [PK[bk], 'negG'], [f'pb{pbi}'])

                    def eO(c, hf, obe=obe, obo=obo):
                        ob = obe if hf == 0 else obo
                        pbi = (c % 2) + 2 * hf
                        return P.emit('T', lambda E, c=c, ob=ob, pbi=pbi:
                                      E.matmul(
                            ps[ob][0:65, 0:QW], kvt[:, c * 65:(c + 1) * 65],
                            pball[pbi][:, :],
                            start=(c == 0), stop=(c == NKC - 1)),
                            [f'pb{pbi}', 'kvt'], [PK[ob]])

                    for c in range(2):
                        eS(c, 0)
                        eS(c, 1)
                        eE(c, 0)
                        eE(c, 1)
                    for c in range(2, NKC):
                        eO(c - 2, 0)
                        eO(c - 2, 1)
                        eS(c, 0)
                        eS(c, 1)
                        eE(c, 0)
                        eE(c, 1)
                    for c in (NKC - 2, NKC - 1):
                        eO(c, 0)
                        eO(c, 1)
                    P.emit('V', lambda E, obe=obe, j=j: E.tensor_copy(
                        qt[j][0:64, :], ps[obe][0:64, 0:QW]),
                        [PK[obe]], [qke])
                    P.emit('V', lambda E, obe=obe: E.tensor_copy(
                        sqs[64:65, :], ps[obe][64:65, 0:QW]),
                        [PK[obe]], ['sqsA', 'sqsB'])
                    gd(lambda E, he=2 * hp: E.dma_start(
                        out=lrows[he:he + 1, :], in_=sqs[64:65, :]),
                        ['sqsA', 'sqsB'], ['lrows'])
                    P.emit('V', lambda E, obo=obo: E.tensor_copy(
                        sqs[0:64, :], ps[obo][0:64, 0:QW]),
                        [PK[obo]], ['sqsA', 'sqsB'])
                    gd(lambda E, j=j: E.dma_start(
                        out=qt[j][64:128, :], in_=sqs[0:64, :]),
                        ['sqsA', 'sqsB'], [qko])
                    P.emit('V', lambda E, obo=obo: E.tensor_copy(
                        sqs[64:65, :], ps[obo][64:65, 0:QW]),
                        [PK[obo]], ['sqsA', 'sqsB'])
                    gd(lambda E, ho=2 * hp + 1: E.dma_start(
                        out=lrows[ho:ho + 1, :], in_=sqs[64:65, :]),
                        ['sqsA', 'sqsB'], ['lrows'])
                P.emit('V', lambda E: E.reciprocal(rl[0:8, :], lrows[0:8, :]),
                       ['lrows'], ['rl'])
                for j in range(NHC):
                    P.emit('T', lambda E, j=j: E.matmul(
                        ps[4][:, 0:QW], selL[:, j * 128:(j + 1) * 128],
                        rl[0:8, :], start=True, stop=True),
                        ['rl', 'selL'], [PK[4]])
                    P.emit('V', lambda E, j=j: E.tensor_tensor(
                        qt[j][:, :], qt[j][:, :], ps[4][:, 0:QW], ALU.mult),
                        [PK[4], f'qt{j}h0', f'qt{j}h1'],
                        [f'qt{j}h0', f'qt{j}h1'])
                if debug and sq_i == 0 and q == 0:
                    st(lambda E: E.dma_start(out=dbg['d_lr'][:, :],
                                             in_=lrows[0:8, :]), ['lrows'])
                    for j in range(NHC):
                        st(lambda E, j=j: E.dma_start(
                            out=dbg['d_on'][:, j * QW:(j + 1) * QW],
                            in_=qt[j][:, :]), [f'qt{j}h0', f'qt{j}h1'])

                for hf2 in range(2):
                    cs = slice(q * QW + hf2 * HW, q * QW + (hf2 + 1) * HW)
                    lsl = slice(hf2 * HW, (hf2 + 1) * HW)
                    for oc in range(ND):
                        bk = 5 + oc % 2
                        for j in range(NHC):
                            P.emit('T', lambda E, bk=bk, j=j, oc=oc, lsl=lsl:
                                   E.matmul(
                                ps[bk][:, 0:HW],
                                wovT[j][:, oc * 128:(oc + 1) * 128],
                                qt[j][:, lsl], start=(j == 0),
                                stop=(j == NHC - 1)),
                                [f'wovT{j}', f'qt{j}h0', f'qt{j}h1'],
                                [PK[bk]])
                        P.emit('S', lambda E, bk=bk, oc=oc: E.activation(
                            ab[oc][:, :], ps[bk][:, 0:HW], AF.Identity),
                            [PK[bk]], [f'ab{oc}'])
                        P.emit('S', lambda E, bk=bk, oc=oc: E.activation(
                            sqs[:, 0:HW], ps[bk][:, 0:HW], AF.Square),
                            [PK[bk]], ['sqsA'])
                        P.emit('T', lambda E, oc=oc: E.matmul(
                            ps[0][0:1, 0:HW], onesc[:, :], ab[oc][:, :],
                            start=(oc == 0), stop=(oc == ND - 1),
                            tile_position=(0, 0)),
                            [f'ab{oc}', 'onesc'], [PK[0]])
                        P.emit('T', lambda E, oc=oc: E.matmul(
                            ps[1][32:33, 0:HW], onesc[:, :], sqs[:, 0:HW],
                            start=(oc == 0), stop=(oc == ND - 1),
                            tile_position=(0, 32)), ['sqsA', 'onesc'],
                            [PK[1]])
                    ln_stats(HW, 0, 1)
                    for i in range(ND):
                        P.emit('V', lambda E, i=i: E.tensor_tensor(
                            ab[i][:, :], ab[i][:, :], rrep[:, 0:HW], ALU.mult),
                            ['rrep', f'ab{i}'], [f'ab{i}'])
                        P.emit('V', lambda E, i=i: E.tensor_tensor(
                            ab[i][:, :], ab[i][:, :], brep[:, 0:HW],
                            ALU.subtract), ['brep', f'ab{i}'], [f'ab{i}'])
                        P.emit('V', lambda E, i=i, cs=cs: E.tensor_tensor(
                            xb[i][:, cs], xb[i][:, cs], ab[i][:, :], ALU.add),
                            [f'ab{i}', XK(i, q)], [XK(i, q)])
                if debug and sq_i == 0 and q == 0:
                    for oc in range(ND):
                        st(lambda E, oc=oc: E.dma_start(
                            out=dbg['d_ab'][:, oc * HW:(oc + 1) * HW],
                            in_=ab[oc][:, :]), [f'ab{oc}'])

            if debug and sq_i == 0:
                st(lambda E: E.dma_start(out=dbg['d_xa'][:, :],
                                         in_=xb[0][:, :]),
                   [XK(0, qq) for qq in range(NQ)])
            for q in range(NQ):
                for hf2 in range(2):
                    cs = slice(q * QW + hf2 * HW, q * QW + (hf2 + 1) * HW)
                    for i in range(ND):
                        sl = slice(0, HW) if i % 2 == 0 else slice(HW, 2 * HW)
                        sk = 'sqsA' if i % 2 == 0 else 'sqsB'
                        P.emit('S', lambda E, i=i, cs=cs, sl=sl: E.activation(
                            sqs[:, sl], xb[i][:, cs], AF.Square),
                            [XK(i, q)], [sk])
                        P.emit('T', lambda E, i=i, cs=cs: E.matmul(
                            ps[0][0:1, 0:HW], onesc[:, :], xb[i][:, cs],
                            start=(i == 0), stop=(i == ND - 1),
                            tile_position=(0, 0)),
                            [XK(i, q), 'onesc'], [PK[0]])
                        P.emit('T', lambda E, sl=sl, i=i: E.matmul(
                            ps[1][32:33, 0:HW], onesc[:, :], sqs[:, sl],
                            start=(i == 0), stop=(i == ND - 1),
                            tile_position=(0, 32)), [sk, 'onesc'], [PK[1]])
                    ln_stats(HW, 0, 1)
                    for i in range(ND):
                        P.emit('T', lambda E, i=i, cs=cs: E.matmul(
                            ps[2][0:8, 0:HW], rTs[i][:, :], xb[i][:, cs],
                            start=(i == 0), stop=(i == ND - 1)),
                            [f'rr{i}', XK(i, q)], [PK[2]])
                    P.emit('V', lambda E: E.tensor_tensor(
                        gate8[0:8, :], ps[2][0:8, 0:HW], rrep[0:8, 0:HW],
                        ALU.mult), [PK[2], 'rrep'], ['g8'])
                    P.emit('V', lambda E: E.scalar_tensor_tensor(
                        gate8[0:8, :], brep[0:8, 0:HW], negwsR[0:8, 0:1],
                        gate8[0:8, :], ALU.mult, ALU.add),
                        ['g8'], ['g8'], sreads=['brep', 'negwsR'])
                    P.emit('T', lambda E: E.transpose(
                        ps[3][:, 0:8], gate8[0:8, 0:128], ident[0:8, 0:8]),
                        ['g8', 'ident'], [PK[3]])
                    P.emit('T', lambda E: E.transpose(
                        ps[3][:, 8:16], gate8[0:8, 128:256], ident[0:8, 0:8]),
                        ['g8', 'ident'], [PK[3]])
                    P.emit('V', lambda E: E.tensor_copy(
                        ltk[:, :], ps[3][:, 0:16]), [PK[3]], ['ltk'])
                    if debug and sq_i == 0 and q == 0 and hf2 == 0:
                        st(lambda E: E.dma_start(out=dbg['d_ltk'][:, :],
                                                 in_=ltk[:, :]), ['ltk'])
                        st(lambda E: E.dma_start(out=dbg['d_rr'][:, :],
                                                 in_=rrep[:, :]), ['rrep'])
                        st(lambda E: E.dma_start(out=dbg['d_st'][:, :],
                                                 in_=stat[:, :]), ['stat'])
                    for tb in range(2):
                        tsl = slice(tb * 8, (tb + 1) * 8)
                        P.emit('V', lambda E, tsl=tsl: E.reduce_max(
                            m1t[:, 0:1], ltk[:, tsl], axis=AX.X),
                            ['ltk'], ['m1t'])
                        P.emit('V', lambda E, tsl=tsl: E.reduce_max(
                            nm1t[:, 0:1], ltk[:, tsl], axis=AX.X, negate=True),
                            ['ltk'], ['nm1t'])
                        P.emit('S', lambda E, tsl=tsl: E.activation(
                            geg[:, :], ltk[:, tsl], AF.Exp, bias=nm1t[:, 0:1]),
                            ['ltk', 'nm1t'], ['geg'])
                        P.emit('V', lambda E: E.reduce_sum(
                            lsum[:, 0:1], geg[:, :], axis=AX.X),
                            ['geg'], ['lsum'])
                        P.emit('V', lambda E, tsl=tsl: E.tensor_scalar(
                            gt2[:, :], ltk[:, tsl], m1t[:, 0:1], None,
                            ALU.subtract), ['ltk'], ['gt2'], sreads=['m1t'])
                        P.emit('V', lambda E: E.tensor_scalar(
                            gmsk[:, :], gt2[:, :], zcol[:, 0:1], None,
                            ALU.is_ge), ['gt2'], ['gmsk'], sreads=['zcol'])
                        P.emit('V', lambda E, tsl=tsl: E.scalar_tensor_tensor(
                            gt2[:, :], gmsk[:, :], -1e30, ltk[:, tsl],
                            ALU.mult, ALU.add), ['ltk'], ['gt2'],
                            sreads=['gmsk'])
                        P.emit('V', lambda E: E.reduce_max(
                            m2t[:, 0:1], gt2[:, :], axis=AX.X),
                            ['gt2'], ['m2t'])
                        P.emit('V', lambda E, tsl=tsl: E.tensor_scalar(
                            gmsk[:, :], ltk[:, tsl], m2t[:, 0:1], None,
                            ALU.subtract), ['ltk'], ['gmsk'], sreads=['m2t'])
                        P.emit('V', lambda E: E.tensor_scalar(
                            gmsk[:, :], gmsk[:, :], zcol[:, 0:1], None,
                            ALU.is_ge), ['gmsk'], ['gmsk'], sreads=['zcol'])
                        P.emit('V', lambda E: E.tensor_tensor(
                            gem[:, :], geg[:, :], gmsk[:, :], ALU.mult),
                            ['geg', 'gmsk'], ['gem'])
                        P.emit('V', lambda E: E.reduce_sum(
                            s2t[:, 0:1], gem[:, :], axis=AX.X),
                            ['gem'], ['s2t'])
                        P.emit('S', lambda E: E.mul(
                            dent[:, 0:1], lsum[:, 0:1], GEPS),
                            ['lsum'], ['dent'])
                        P.emit('V', lambda E: E.tensor_tensor(
                            dent[:, 0:1], dent[:, 0:1], s2t[:, 0:1], ALU.add),
                            ['dent', 's2t'], ['dent'])
                        P.emit('V', lambda E: E.reciprocal(
                            rdent[:, 0:1], dent[:, 0:1]), ['dent'], ['rdent'])
                        P.emit('V', lambda E, tsl=tsl: E.tensor_scalar(
                            gfin[:, tsl], gem[:, :], rdent[:, 0:1], None,
                            ALU.mult), ['gem'], ['gfin'], sreads=['rdent'])
                        if debug and sq_i == 0 and q == 0 and hf2 == 0 and tb == 0:
                            st(lambda E: E.dma_start(out=dbg['d_geg'][:, :],
                                                     in_=geg[:, :]), ['geg'])
                            st(lambda E: E.dma_start(out=dbg['d_gt2'][:, :],
                                                     in_=gt2[:, :]), ['gt2'])
                            st(lambda E: E.dma_start(out=dbg['d_gmsk'][:, :],
                                                     in_=gmsk[:, :]), ['gmsk'])
                            st(lambda E: E.dma_start(out=dbg['d_gem'][:, :],
                                                     in_=gem[:, :]), ['gem'])
                            st(lambda E: E.dma_start(out=dbg['d_gsc2'][:, :],
                                                     in_=gsc[:, :]), ['gsc'])
                    P.emit('T', lambda E: E.transpose(
                        ps[2][0:8, 0:128], gfin[:, 0:8], ident[:, :]),
                        ['gfin', 'ident'], [PK[2]])
                    P.emit('T', lambda E: E.transpose(
                        ps[2][0:8, 128:256], gfin[:, 8:16], ident[:, :]),
                        ['gfin', 'ident'], [PK[2]])
                    P.emit('V', lambda E: E.tensor_copy(
                        gate8[0:8, :], ps[2][0:8, 0:HW]), [PK[2]], ['g8'])
                    if debug and sq_i == 0 and q == 0 and hf2 == 0:
                        st(lambda E: E.dma_start(out=dbg['d_gf'][:, :],
                                                 in_=gfin[:, :]), ['gfin'])
                    for g in range(2):
                        gb = 4 + g
                        hb = 6 + g
                        P.emit('T', lambda E, g=g, gb=gb: E.matmul(
                            ps[gb][:, 0:HW], selE[:, g * 128:(g + 1) * 128],
                            gate8[0:8, :], start=True, stop=True),
                            ['g8', 'selE'], [PK[gb]])
                        for i in range(ND):
                            P.emit('T', lambda E, i=i, g=g, hb=hb, cs=cs:
                                   E.matmul(
                                ps[hb][:, 0:HW],
                                aTs[i][:, g * 128:(g + 1) * 128],
                                xb[i][:, cs], start=(i == 0),
                                stop=(i == ND - 1)),
                                [f'aa{i}', XK(i, q)], [PK[hb]])
                        P.emit('V', lambda E, g=g, hb=hb: E.tensor_tensor(
                            hsb[g][:, :], ps[hb][:, 0:HW], rrep[:, 0:HW],
                            ALU.mult), [PK[hb], 'rrep'], [f'hs{g}'])
                        P.emit('V', lambda E, g=g: E.scalar_tensor_tensor(
                            hsb[g][:, :], brep[:, 0:HW], negwsA[:, g:g + 1],
                            hsb[g][:, :], ALU.mult, ALU.add),
                            [f'hs{g}'], [f'hs{g}'], sreads=['brep', 'negwsA'])
                        P.emit('V', lambda E, g=g, gb=gb: E.tensor_tensor(
                            hsb[g][:, :], hsb[g][:, :], ps[gb][:, 0:HW],
                            ALU.mult), [PK[gb], f'hs{g}'], [f'hs{g}'])
                    if debug and sq_i == 0 and q == 0 and hf2 == 0:
                        for g in range(2):
                            st(lambda E, g=g: E.dma_start(
                                out=dbg['d_hs'][:, g * HW:(g + 1) * HW],
                                in_=hsb[g][:, :]), [f'hs{g}'])
                    for oc in range(ND):
                        bk = oc % 2
                        for i in range(ND):
                            P.emit('T', lambda E, bk=bk, i=i, oc=oc, cs=cs:
                                   E.matmul(
                                ps[bk][:, 0:HW],
                                wTs[i][:, oc * 128:(oc + 1) * 128],
                                xb[i][:, cs], start=(i == 0),
                                stop=(i == ND - 1)),
                                [f'wt{i}', XK(i, q)], [PK[bk]])
                        P.emit('T', lambda E, oc=oc: E.matmul(
                            ps[5][:, 0:HW], bTs[0][:, oc * 128:(oc + 1) * 128],
                            hsb[0][:, :], start=True, stop=False),
                            ['bb0', 'hs0'], [PK[5]])
                        P.emit('T', lambda E, oc=oc: E.matmul(
                            ps[5][:, 0:HW], bTs[1][:, oc * 128:(oc + 1) * 128],
                            hsb[1][:, :], start=False, stop=True),
                            ['bb1', 'hs1'], [PK[5]])
                        P.emit('V', lambda E, bk=bk, oc=oc: E.tensor_tensor(
                            ab[oc][:, :], ps[bk][:, 0:HW], rrep[:, 0:HW],
                            ALU.mult), [PK[bk], 'rrep'], [f'ab{oc}'])
                        P.emit('V', lambda E, oc=oc: E.scalar_tensor_tensor(
                            ab[oc][:, :], brep[:, 0:HW], negwsW[:, oc:oc + 1],
                            ab[oc][:, :], ALU.mult, ALU.add),
                            [f'ab{oc}'], [f'ab{oc}'], sreads=['brep', 'negwsW'])
                        P.emit('V', lambda E, oc=oc: E.tensor_tensor(
                            ab[oc][:, :], ab[oc][:, :], ps[5][:, 0:HW],
                            ALU.add), [PK[5], f'ab{oc}'], [f'ab{oc}'])
                        P.emit('S', lambda E, oc=oc: E.activation(
                            ab[oc][:, :], ab[oc][:, :], AF.Identity,
                            bias=bcols[:, oc:oc + 1]),
                            ['bcols', f'ab{oc}'], [f'ab{oc}'])
                        P.emit('S', lambda E, oc=oc: E.activation(
                            sqs[:, 0:HW], ab[oc][:, :], AF.Square),
                            [f'ab{oc}'], ['sqsA'])
                        P.emit('T', lambda E, oc=oc: E.matmul(
                            ps[2][0:1, 0:HW], onesc[:, :], ab[oc][:, :],
                            start=(oc == 0), stop=(oc == ND - 1),
                            tile_position=(0, 0)),
                            [f'ab{oc}', 'onesc'], [PK[2]])
                        P.emit('T', lambda E, oc=oc: E.matmul(
                            ps[3][32:33, 0:HW], onesc[:, :], sqs[:, 0:HW],
                            start=(oc == 0), stop=(oc == ND - 1),
                            tile_position=(0, 32)), ['sqsA', 'onesc'],
                            [PK[3]])
                    ln_stats(HW, 2, 3)
                    for i in range(ND):
                        P.emit('V', lambda E, i=i: E.tensor_tensor(
                            ab[i][:, :], ab[i][:, :], rrep[:, 0:HW], ALU.mult),
                            ['rrep', f'ab{i}'], [f'ab{i}'])
                        P.emit('V', lambda E, i=i: E.tensor_tensor(
                            ab[i][:, :], ab[i][:, :], brep[:, 0:HW],
                            ALU.subtract), ['brep', f'ab{i}'], [f'ab{i}'])
                        P.emit('V', lambda E, i=i, cs=cs: E.tensor_tensor(
                            xb[i][:, cs], xb[i][:, cs], ab[i][:, :], ALU.add),
                            [f'ab{i}', XK(i, q)], [XK(i, q)])
                        st(lambda E, i=i, cs=cs, sq_i=sq_i: E.dma_start(
                            out=xout[sq_i, i * 128:(i + 1) * 128, cs],
                            in_=xb[i][:, cs]), [XK(i, q)])
                        if sq_i + 1 < SPC and hf2 == 1:
                            ld(lambda E, i=i, q=q, sq_i=sq_i: E.dma_start(
                                out=xb[i][:, q * QW:(q + 1) * QW],
                                in_=xin[sq_i + 1, i * 128:(i + 1) * 128,
                                        q * QW:(q + 1) * QW]), [XK(i, q)])

        P.items.append(('Y', tuple([('ot', P.val.get('ot', 0)),
                                    ('gd', P.val.get('gd', 0))]), None, None, 0))

        def walker(engname):
            def go(E):
                waited = {}
                for (en_, waits, fn, sem, step) in P.items:
                    if en_ != engname:
                        continue
                    for (sn, vv) in waits:
                        if waited.get(sn, -1) < vv:
                            E.wait_ge(sems[sn], vv)
                            waited[sn] = vv
                    if fn is None:
                        continue
                    ins = fn(E)
                    ins.then_inc(sems[sem], step)
            return go

        @block.sync
        def _(E):
            walker('Y')(E)

        @block.tensor
        def _(E):
            walker('T')(E)

        @block.vector
        def _(E):
            walker('V')(E)

        @block.scalar
        def _(E):
            walker('S')(E)

        @block.gpsimd
        def _(E):
            walker('G')(E)

    return nc


def _host_prep(L0_attn_x, L0_attn_Wq, L0_attn_Wkv_down, L0_attn_Wk_up,
               L0_attn_Wv_up, L0_attn_Wo, L0_ffn_W, L0_ffn_b, L0_ffn_R,
               L0_ffn_A, L0_ffn_B):
    f = np.float32
    x = np.ascontiguousarray(np.asarray(L0_attn_x, f).transpose(0, 2, 1))
    wq = np.ascontiguousarray(np.asarray(L0_attn_Wq, f))
    wkup = np.ascontiguousarray(np.asarray(L0_attn_Wk_up, f))
    wvup = np.ascontiguousarray(np.asarray(L0_attn_Wv_up, f))
    woT = np.ascontiguousarray(np.asarray(L0_attn_Wo, f).T)
    wkvT = np.asarray(L0_attn_Wkv_down, f).T
    wkvT2 = np.ascontiguousarray(np.concatenate([wkvT, wkvT], axis=1))
    wT = np.ascontiguousarray(np.asarray(L0_ffn_W, f).T)
    rT = np.ascontiguousarray(np.asarray(L0_ffn_R, f).T)
    A = np.asarray(L0_ffn_A, f)
    aT = np.zeros((1024, 256), f)
    for ex in range(8):
        g, ei = divmod(ex, 4)
        aT[:, g * 128 + ei * 32: g * 128 + ei * 32 + 16] = A[ex].T
    Bw = np.asarray(L0_ffn_B, f)
    bT = np.zeros((256, 1024), f)
    for ex in range(8):
        g, ei = divmod(ex, 4)
        bT[g * 128 + ei * 32: g * 128 + ei * 32 + 16, :] = Bw[ex].T
    bcol = np.ascontiguousarray(np.asarray(L0_ffn_b, f).reshape(1024, 1))
    identc = np.eye(128, dtype=f)
    onesc = np.ones((128, 1), f)
    onesr = np.ones((1, 128), f)
    selL = np.zeros((8, 512), f)
    for j in range(4):
        selL[2 * j, j * 128:j * 128 + 64] = 1.0
        selL[2 * j + 1, j * 128 + 64:j * 128 + 128] = 1.0
    selE = np.zeros((8, 256), f)
    for g in range(2):
        for ex in range(4):
            selE[g * 4 + ex, g * 128 + ex * 32:g * 128 + (ex + 1) * 32] = 1.0
    base = dict(wq=wq, wkup=wkup, wvup=wvup, woT=woT, wkvT2=wkvT2, wTd=wT,
                rTd=rT, aTd=aT, bTd=bT, bcold=bcol, identd=identc,
                onescd=onesc, onesrd=onesr, selLd=selL, selEd=selE,
                epsd=np.full((128, 1), LNEPS, f))
    in_maps = []
    for c in range(NCORES):
        m = dict(base)
        m["xin"] = np.ascontiguousarray(x[c * SPC:(c + 1) * SPC])
        in_maps.append(m)
    return in_maps


def kernel(**inputs):
    debug = bool(inputs.pop("_debug", False))
    in_maps = _host_prep(**inputs)
    nc = build(debug=debug)
    res = run_bass_kernel_spmd(nc, in_maps, core_ids=list(range(NCORES)))
    outs = []
    for c in range(NCORES):
        xo = np.asarray(res.results[c]["xout"])
        outs.append(xo.transpose(0, 2, 1))
    full = np.ascontiguousarray(np.concatenate(outs, axis=0).astype(np.float32))
    if debug:
        kernel._dbg = res.results
    return full


# version 07
# speedup vs baseline: 1.0473x; 1.0473x over previous
import numpy as np
from contextlib import ExitStack
import concourse.bass as bass
import concourse.mybir as mybir
from concourse.bass_utils import run_bass_kernel_spmd

F32 = mybir.dt.float32
AF = mybir.ActivationFunctionType
ALU = mybir.AluOpType
AX = mybir.AxisListType

NCORES = 8
T, D = 2048, 1024
SPC = 4
NQ = 4
QW = 512
HW = 256
ND = 8
NHC = 4
NKC = 16
H = 8
SCALE = 0.0625
LNEPS = 1e-5
GEPS = 1e-6
INV_D = 1.0 / 1024.0

ESEM = {'T': 't', 'V': 'v', 'S': 's', 'G': 'g'}


class Prog:

    def __init__(self):
        self.items = []
        self.val = {}
        self.w = {}
        self.r = {}

    def emit(self, eng, fn, reads=(), writes=(), extra=(), dma=None,
             sreads=()):
        waits = {}

        def add(mk, force=False):
            if mk is None:
                return
            sn, vv = mk
            if not force and eng in ESEM and sn == ESEM[eng]:
                return
            if waits.get(sn, -1) < vv:
                waits[sn] = vv

        force_all = eng in ('V', 'S')
        for k in reads:
            add(self.w.get(k), force=force_all)
        for k in sreads:
            add(self.w.get(k), force=True)
        for k in writes:
            add(self.w.get(k))
            for sn, vv in self.r.get(k, {}).items():
                add((sn, vv))
        for mk in extra:
            add(mk)
        if dma is not None:
            sem, step = dma, 16
        else:
            sem, step = ESEM[eng], 1
        self.items.append((eng, tuple(waits.items()), fn, sem, step))
        self.val[sem] = self.val.get(sem, 0) + step
        mk = (sem, self.val[sem])
        for k in list(reads) + list(sreads):
            d = self.r.setdefault(k, {})
            if d.get(mk[0], -1) < mk[1]:
                d[mk[0]] = mk[1]
        for k in writes:
            self.w[k] = mk
            self.r[k] = {}
        return mk

    def barrier_mark(self):
        return [(sn, vv) for sn, vv in self.val.items()]

    def dma_fence(self, eng, sem, keys=()):
        v = self.val.get(sem, 0)
        self.items.append((eng, ((sem, v),), None, None, 0))
        for k in keys:
            self.w[k] = (sem, v)
            self.r[k] = {}

    def dma_fence_r(self, eng, sem, rkeys=()):
        v = self.val.get(sem, 0)
        self.items.append((eng, ((sem, v),), None, None, 0))
        for k in rkeys:
            self.r[k] = {sem: v}


def build(debug=False, debug2=False):
    nc = bass.Bass("TRN2", target_bir_lowering=False, debug=False)

    xin = nc.dram_tensor("xin", [SPC, D, T], F32, kind="ExternalInput").ap()
    wqd = nc.dram_tensor("wq", [2048, 1024], F32, kind="ExternalInput").ap()
    wkupd = nc.dram_tensor("wkup", [2048, 64], F32, kind="ExternalInput").ap()
    wvupd = nc.dram_tensor("wvup", [2048, 64], F32, kind="ExternalInput").ap()
    woTd = nc.dram_tensor("woT", [2048, 1024], F32, kind="ExternalInput").ap()
    wkvT2d = nc.dram_tensor("wkvT2", [1024, 128], F32, kind="ExternalInput").ap()
    wTd = nc.dram_tensor("wTd", [1024, 1024], F32, kind="ExternalInput").ap()
    rTd = nc.dram_tensor("rTd", [1024, 8], F32, kind="ExternalInput").ap()
    aTd = nc.dram_tensor("aTd", [1024, 256], F32, kind="ExternalInput").ap()
    bTd = nc.dram_tensor("bTd", [256, 1024], F32, kind="ExternalInput").ap()
    bcold = nc.dram_tensor("bcold", [1024, 1], F32, kind="ExternalInput").ap()
    identd = nc.dram_tensor("identd", [128, 128], F32, kind="ExternalInput").ap()
    onescd = nc.dram_tensor("onescd", [128, 1], F32, kind="ExternalInput").ap()
    onesrd = nc.dram_tensor("onesrd", [1, 128], F32, kind="ExternalInput").ap()
    selLd = nc.dram_tensor("selLd", [8, 512], F32, kind="ExternalInput").ap()
    selEd = nc.dram_tensor("selEd", [8, 256], F32, kind="ExternalInput").ap()
    epsd = nc.dram_tensor("epsd", [128, 1], F32, kind="ExternalInput").ap()
    xout = nc.dram_tensor("xout", [SPC, D, T], F32, kind="ExternalOutput").ap()
    dbg = {}
    if debug2:
        for nm, shp in [("pm_ws", [128, 16]), ("pm_wqk", [128, 4096]),
                        ("pm_wov", [128, 4096]), ("pm_kvl", [128, 2048]),
                        ("pm_wov0", [128, 4096]), ("pm_lnrb", [8, 512])]:
            dbg[nm] = nc.dram_tensor(nm, shp, F32, kind="ExternalOutput").ap()
    if debug:
        for nm, shp in [("d_kvl", [128, T]), ("d_kvt", [128, NKC * 65]),
                        ("d_qt", [128, 4 * QW]), ("d_lr", [8, QW]),
                        ("d_ng", [128, 8]), ("d_on", [128, 4 * QW]),
                        ("d_ab", [128, 8 * HW]), ("d_lnrb", [8, 512]),
                        ("d_gf", [128, 16]), ("d_hs", [128, 2 * HW]),
                        ("d_wqk", [128, 512]), ("d_wov", [128, 1024]),
                        ("d_kvm2", [128, 2]), ("d_gsc", [128, 8]),
                        ("d_kvm_a", [128, 1]), ("d_kvm_b", [128, 1]),
                        ("d_xa", [128, T]), ("d_ltk", [128, 16]),
                        ("d_geg", [128, 8]), ("d_gt2", [128, 8]),
                        ("d_gmsk", [128, 8]), ("d_gem", [128, 8]),
                        ("d_gsc2", [128, 8]),
                        ("d_rr", [128, 512]), ("d_st", [34, 512]),
                        ("d_psb", [128, 512]), ("d_sqs", [128, 512])]:
            dbg[nm] = nc.dram_tensor(nm, shp, F32, kind="ExternalOutput").ap()

    P = Prog()

    with ExitStack() as stk:
        en = stk.enter_context

        def sb(name, shape):
            return en(nc.sbuf_tensor(name, shape, F32))

        xb = [sb(f"x{i}", [128, T]) for i in range(ND)]
        wqkT = [sb(f"wqk{i}", [128, 512]) for i in range(ND)]
        wovT = [sb(f"wov{j}", [128, 1024]) for j in range(NHC)]
        wTs = [sb(f"wt{i}", [128, 1024]) for i in range(ND)]
        aTs = [sb(f"aa{i}", [128, 256]) for i in range(ND)]
        bTs = [sb(f"bb{g}", [128, 1024]) for g in range(2)]
        rTs = [sb(f"rr{i}", [128, 8]) for i in range(ND)]
        bcols = sb("bcols", [128, 8])
        negws = sb("negws", [128, 4])
        negwskv = sb("negwskv", [128, 1])
        negwsW = sb("negwsw", [128, 8])
        negwsA = sb("negwsa", [128, 2])
        negwsR = sb("negwsr", [8, 1])
        kvl = sb("kvl", [128, T])
        kvt = sb("kvt", [128, NKC * 65])
        lnrb = sb("lnrb", [8, 512])
        ident = sb("ident", [128, 128])
        onesc = sb("onesc", [128, 1])
        onesr = sb("onesr", [1, 128])
        selL = sb("selL", [8, 512])
        selE = sb("selE", [8, 256])
        negG = sb("negg", [128, 8])
        kvm = sb("kvm", [128, 1])
        kvm2 = sb("kvm2", [128, 2])
        epsc = sb("epsc", [128, 1])
        qt = [sb(f"q{j}", [128, 512]) for j in range(NHC)]
        pbuf = [sb(f"p{k}", [128, 512]) for k in range(2)]
        pbuf2 = [sb(f"pp{k}", [128, 512]) for k in range(2)]
        pball = [pbuf[0], pbuf[1], pbuf2[0], pbuf2[1]]
        sqs = sb("sqs", [128, 512])
        rrep = sb("rrep", [128, 512])
        brep = sb("brep", [128, 512])
        ab = [sb(f"ab{i}", [128, 512]) for i in range(ND)]
        stat = sb("stat", [34, 512])
        lrows = sb("lrows", [8, 512])
        rl = sb("rl", [8, 512])
        gate8 = rl
        hsb = [pbuf2[0], pbuf2[1]]
        HSK = ['pb2', 'pb3']
        ltk = sb("ltk", [128, 32])
        geg = sb("geg", [128, 8])
        gt2 = sb("gt2", [128, 8])
        gmsk = sb("gmsk", [128, 8])
        gem = sb("gem", [128, 8])
        gsc = sb("gsc", [128, 8])
        gfin = sb("gfin", [128, 32])
        m1t = sb("m1t", [128, 1])
        nm1t = sb("nm1t", [128, 1])
        m2t = sb("m2t", [128, 1])
        lsum = sb("lsum", [128, 1])
        s2t = sb("s2t", [128, 1])
        dent = sb("dent", [128, 1])
        rdent = sb("rdent", [128, 1])
        zcol = sb("zcol", [128, 1])

        ps = [en(nc.psum_tensor(f"ps{k}", [128, 512], F32)) for k in range(8)]
        PK = [f"ps{k}" for k in range(8)]

        sems = {}
        for sn in ('t', 'v', 's', 'g', 'ld', 'ot', 'gd'):
            sems[sn] = en(nc.semaphore(sn))
        block = en(nc.Block())

        def ld(fn, writes):
            return P.emit('Y', fn, (), writes, dma='ld')

        def st(fn, reads):
            return P.emit('Y', fn, reads, (), dma='ot')

        def gd(fn, reads=(), writes=()):
            mk = P.emit('G', fn, reads, writes, dma='gd')
            P.items.append(('G', ((mk[0], mk[1]),), None, None, 0))
            return mk

        ld(lambda E: E.dma_start(out=ident[:], in_=identd[:, :]), ['ident'])
        ld(lambda E: E.dma_start(out=onesc[:], in_=onescd[:, :]), ['onesc'])
        ld(lambda E: E.dma_start(out=onesr[:], in_=onesrd[:, :]), ['onesr'])
        ld(lambda E: E.dma_start(out=selL[:], in_=selLd[:, :]), ['selL'])
        ld(lambda E: E.dma_start(out=selE[:], in_=selEd[:, :]), ['selE'])
        ld(lambda E: E.dma_start(out=epsc[:], in_=epsd[:, :]), ['epsc'])
        def wkv_ap(i):
            return (pbuf[0] if i < 4 else pbuf2[0])[:, (i % 4) * 128:
                                                    (i % 4) * 128 + 128]

        def wkv_key(i):
            return 'pb0' if i < 4 else 'pb2'

        def load_wkv():
            for i in range(ND):
                ld(lambda E, i=i: E.dma_start(
                    out=wkv_ap(i), in_=wkvT2d[i * 128:(i + 1) * 128, :]),
                    [wkv_key(i)])
            P.dma_fence('Y', 'ld', ['pb0', 'pb2'])
        load_wkv()
        for i in range(ND):
            ld(lambda E, i=i: E.dma_start(
                out=wTs[i][:], in_=wTd[i * 128:(i + 1) * 128, :]), [f'wt{i}'])
            ld(lambda E, i=i: E.dma_start(
                out=aTs[i][:], in_=aTd[i * 128:(i + 1) * 128, :]), [f'aa{i}'])
            ld(lambda E, i=i: E.dma_start(
                out=rTs[i][:], in_=rTd[i * 128:(i + 1) * 128, :]), [f'rr{i}'])
        for g in range(2):
            ld(lambda E, g=g: E.dma_start(
                out=bTs[g][:], in_=bTd[g * 128:(g + 1) * 128, :]), [f'bb{g}'])
        for oc in range(ND):
            ld(lambda E, oc=oc: E.dma_start(
                out=bcols[:, oc:oc + 1], in_=bcold[oc * 128:(oc + 1) * 128, :]),
                ['bcols'])
        for i in range(ND):
            for q in range(NQ):
                ld(lambda E, i=i, q=q: E.dma_start(
                    out=xb[i][:, q * QW:(q + 1) * QW],
                    in_=xin[0, i * 128:(i + 1) * 128, q * QW:(q + 1) * QW]),
                    [f'x{i}q{q}'])
        P.dma_fence('Y', 'ld',
                    ['ident', 'onesc', 'onesr', 'selL', 'selE', 'epsc',
                     'bcols', 'bb0', 'bb1']
                    + [f'wt{i}' for i in range(ND)]
                    + [f'aa{i}' for i in range(ND)]
                    + [f'rr{i}' for i in range(ND)]
                    + [f'x{i}q{q}' for i in range(ND) for q in range(NQ)])

        P.emit('V', lambda E: E.memset(stat[:, :], 0.0), (), ['stat'])
        P.emit('V', lambda E: E.memset(lnrb[:, :], 0.0), (), ['lnrb'])
        P.emit('G', lambda E: E.memset(kvt[:, :], 1.0), (), ['kvt'])
        P.emit('V', lambda E: E.memset(lrows[:, :], 1.0), (), ['lrows'])
        P.emit('V', lambda E: E.memset(gsc[:, :], 0.0), (), ['gsc'])
        P.emit('V', lambda E: E.memset(zcol[:, :], 0.0), (), ['zcol'])

        for h in range(H):
            ld(lambda E, h=h: E.dma_start(
                out=qt[0][:, :], in_=wqd[h * 256:h * 256 + 128, 0:512]),
                ['qt0h0', 'qt0h1'])
            ld(lambda E, h=h: E.dma_start(
                out=qt[1][:, :], in_=wqd[h * 256:h * 256 + 128, 512:1024]),
                ['qt1h0', 'qt1h1'])
            ld(lambda E, h=h: E.dma_start(
                out=qt[2][:, :], in_=wqd[h * 256 + 128:h * 256 + 256, 0:512]),
                ['qt2h0', 'qt2h1'])
            ld(lambda E, h=h: E.dma_start(
                out=qt[3][:, :], in_=wqd[h * 256 + 128:h * 256 + 256, 512:1024]),
                ['qt3h0', 'qt3h1'])
            ld(lambda E, h=h: E.dma_start(
                out=pbuf[0][:, 0:64], in_=wkupd[h * 256:h * 256 + 128, :]),
                ['pb0'])
            ld(lambda E, h=h: E.dma_start(
                out=pbuf[0][:, 64:128],
                in_=wkupd[h * 256 + 128:h * 256 + 256, :]), ['pb0'])
            P.dma_fence('Y', 'ld', ['pb0'] + [f'qt{k}h0' for k in range(4)]
                        + [f'qt{k}h1' for k in range(4)])
            for i in range(ND):
                bk = i % 2
                q01 = i // 4
                c0 = (i % 4) * 128
                P.emit('T', lambda E, bk=bk, q01=q01, c0=c0: E.matmul(
                    ps[bk][0:128, 0:64], qt[q01][:, c0:c0 + 128],
                    pbuf[0][:, 0:64], start=True, stop=False),
                    [f'qt{q01}h0', f'qt{q01}h1', 'pb0'], [PK[bk]])
                P.emit('T', lambda E, bk=bk, q01=q01, c0=c0: E.matmul(
                    ps[bk][0:128, 0:64], qt[2 + q01][:, c0:c0 + 128],
                    pbuf[0][:, 64:128], start=False, stop=True),
                    [f'qt{2+q01}h0', f'qt{2+q01}h1', 'pb0'], [PK[bk]])
                P.emit('V', lambda E, bk=bk, i=i, h=h: E.tensor_copy(
                    wqkT[i][:, h * 64:(h + 1) * 64], ps[bk][0:128, 0:64]),
                    [PK[bk]], [f'wqkT{i}'])
        for jp in range(NHC):
            ld(lambda E, jp=jp: E.dma_start(
                out=pbuf[1][:, 0:64], in_=wvupd[jp * 512:jp * 512 + 128, :]),
                ['pb1'])
            ld(lambda E, jp=jp: E.dma_start(
                out=pbuf[1][:, 64:128],
                in_=wvupd[jp * 512 + 128:jp * 512 + 256, :]), ['pb1'])
            ld(lambda E, jp=jp: E.dma_start(
                out=pbuf[1][:, 128:192],
                in_=wvupd[jp * 512 + 256:jp * 512 + 384, :]), ['pb1'])
            ld(lambda E, jp=jp: E.dma_start(
                out=pbuf[1][:, 192:256],
                in_=wvupd[jp * 512 + 384:jp * 512 + 512, :]), ['pb1'])
            P.dma_fence('Y', 'ld', ['pb1'])
            for ohalf in range(2):
                stg = qt if ohalf == 0 else ab
                sgk = (lambda k: [f'qt{k}h0', f'qt{k}h1']) if ohalf == 0 \
                    else (lambda k: [f'ab{k}'])
                for k2 in range(2):
                    ld(lambda E, jp=jp, k2=k2, ohalf=ohalf, stg=stg:
                       E.dma_start(
                        out=stg[k2][:, :],
                        in_=woTd[jp * 512 + k2 * 128:jp * 512 + (k2 + 1) * 128,
                                 ohalf * 512:(ohalf + 1) * 512]),
                        sgk(k2))
                    ld(lambda E, jp=jp, k2=k2, ohalf=ohalf, stg=stg:
                       E.dma_start(
                        out=stg[2 + k2][:, :],
                        in_=woTd[jp * 512 + 256 + k2 * 128:
                                 jp * 512 + 256 + (k2 + 1) * 128,
                                 ohalf * 512:(ohalf + 1) * 512]),
                        sgk(2 + k2))
                P.dma_fence('Y', 'ld',
                            sum((sgk(k) for k in range(4)), []))
                for hf in range(2):
                    bk = 4 + (2 * ohalf + hf) % 4
                    P.emit('T', lambda E, bk=bk, hf=hf, stg=stg: E.matmul(
                        ps[bk][0:64, :], pbuf[1][:, hf * 128:hf * 128 + 64],
                        stg[2 * hf][:, :], start=True, stop=False),
                        ['pb1'] + sgk(2 * hf), [PK[bk]])
                    P.emit('T', lambda E, bk=bk, hf=hf, stg=stg: E.matmul(
                        ps[bk][0:64, :],
                        pbuf[1][:, hf * 128 + 64:hf * 128 + 128],
                        stg[2 * hf + 1][:, :], start=False, stop=True),
                        ['pb1'] + sgk(2 * hf + 1), [PK[bk]])
                    if hf == 0:
                        P.emit('V', lambda E, bk=bk, jp=jp, ohalf=ohalf:
                               E.tensor_copy(
                            wovT[jp][0:64, ohalf * 512:(ohalf + 1) * 512],
                            ps[bk][0:64, :]), [PK[bk]], [f'wovT{jp}'])
                    else:
                        P.emit('V', lambda E, bk=bk: E.tensor_copy(
                            sqs[0:64, :], ps[bk][0:64, :]),
                            [PK[bk]], ['sqsA', 'sqsB'])
                        gd(lambda E, jp=jp, ohalf=ohalf: E.dma_start(
                            out=wovT[jp][64:128, ohalf * 512:(ohalf + 1) * 512],
                            in_=sqs[0:64, :]), ['sqsA', 'sqsB'],
                            [f'wovT{jp}'])
        if debug:
            st(lambda E: E.dma_start(out=dbg['d_wqk'][:, :], in_=wqkT[0][:, :]),
               ['wqkT0'])
            st(lambda E: E.dma_start(out=dbg['d_wov'][:, :], in_=wovT[0][:, :]),
               ['wovT0'])

        if debug2:
            for j in range(NHC):
                st(lambda E, j=j: E.dma_start(
                    out=dbg['pm_wov0'][:, j * 1024:(j + 1) * 1024],
                    in_=wovT[j][:, :]), [f'wovT{j}'])
        def colsum_neg(dst_fn, dst_key, lhs):
            n = len(lhs)
            for i, (lf, rkeys, kp) in enumerate(lhs):
                P.emit('T', lambda E, lf=lf, i=i, n=n, kp=kp: E.matmul(
                    ps[6][0:kp, 0:1], lf(), onesc[:, :],
                    start=(i == 0), stop=(i == n - 1)),
                    list(rkeys) + ['onesc'], [PK[6]])
            P.emit('S', lambda E, kp=lhs[0][2]: E.mul(
                dst_fn(), ps[6][0:kp, 0:1], -1.0), [PK[6]], [dst_key])

        for j in range(NHC):
            colsum_neg(lambda j=j: negws[:, j:j + 1], 'negws',
                       [(lambda i=i, j=j: wqkT[i][:, j * 128:(j + 1) * 128],
                         [f'wqkT{i}'], 128) for i in range(ND)])
        load_wkv()
        colsum_neg(lambda: negwskv[:, 0:1], 'negwskv',
                   [(lambda i=i: wkv_ap(i), [wkv_key(i)], 128)
                    for i in range(ND)])
        for oc in range(ND):
            colsum_neg(lambda oc=oc: negwsW[:, oc:oc + 1], 'negwsW',
                       [(lambda i=i, oc=oc: wTs[i][:, oc * 128:(oc + 1) * 128],
                         [f'wt{i}'], 128) for i in range(ND)])
        for g in range(2):
            colsum_neg(lambda g=g: negwsA[:, g:g + 1], 'negwsA',
                       [(lambda i=i, g=g: aTs[i][:, g * 128:(g + 1) * 128],
                         [f'aa{i}'], 128) for i in range(ND)])
        colsum_neg(lambda: negwsR[:, 0:1], 'negwsR',
                   [(lambda i=i: rTs[i][:, :], [f'rr{i}'], 8)
                    for i in range(ND)])

        def ln_stats(w, bS, bQ, store_q=None):
            assert w == QW
            sqr = sqs[0:1, 0:w]
            msq = pbuf[1][0:1, 0:w]
            rrow = qt[0][0:1, 0:w]
            brow = qt[1][0:1, 0:w]
            rk = 'qt0h0'
            bk_ = 'qt1h0'
            P.emit('V', lambda E, w=w: E.tensor_copy(
                stat[0:1, 0:w], ps[bS][0:1, 0:w]), [PK[bS]], ['stat'])
            P.emit('V', lambda E, w=w: E.tensor_copy(
                stat[32:33, 0:w], ps[bQ][32:33, 0:w]), [PK[bQ]], ['stat'])
            P.emit('T', lambda E, w=w: E.matmul(
                ps[6][0:1, 0:w], ident[0:33, 32:33], stat[0:33, 0:w],
                start=True, stop=True), ['stat', 'ident'], [PK[6]])
            P.emit('V', lambda E, w=w, sqr=sqr: E.tensor_copy(
                sqr, ps[6][0:1, 0:w]), [PK[6]], ['sqsA', 'sqsB'])
            P.emit('S', lambda E, w=w, msq=msq: E.activation(
                msq, stat[0:1, 0:w], AF.Square, scale=INV_D),
                ['stat'], ['pb1'])
            P.emit('V', lambda E, sqr=sqr, msq=msq: E.scalar_tensor_tensor(
                msq, sqr, INV_D, msq, ALU.mult, ALU.subtract),
                [], ['pb1'], sreads=['sqsA', 'sqsB', 'pb1'])
            P.emit('S', lambda E, msq=msq: E.activation(
                msq, msq, AF.Ln, bias=epsc[0:1, 0:1]),
                ['pb1', 'epsc'], ['pb1'])
            P.emit('S', lambda E, msq=msq, rrow=rrow: E.activation(
                rrow, msq, AF.Exp, scale=-0.5),
                ['pb1'], [rk])
            P.emit('V', lambda E, rrow=rrow, brow=brow: E.scalar_tensor_tensor(
                brow, stat[0:1, 0:w], INV_D, rrow,
                ALU.mult, ALU.mult), [rk], [bk_], sreads=['stat'])
            P.emit('T', lambda E, w=w, rrow=rrow: E.matmul(
                ps[6][:, 0:w], onesr[:, :], rrow, start=True, stop=True),
                [rk, 'onesr'], [PK[6]])
            P.emit('V', lambda E, w=w: E.tensor_copy(
                rrep[:, 0:w], ps[6][:, 0:w]), [PK[6]], ['rrep'])
            P.emit('T', lambda E, w=w, brow=brow: E.matmul(
                ps[6][:, 0:w], onesr[:, :], brow, start=True, stop=True),
                [bk_, 'onesr'], [PK[6]])
            P.emit('V', lambda E, w=w: E.tensor_copy(
                brep[:, 0:w], ps[6][:, 0:w]), [PK[6]], ['brep'])
            if store_q is not None:
                gd(lambda E, q=store_q, rrow=rrow: E.dma_start(
                    out=lnrb[q:q + 1, :], in_=rrow), [rk], ['lnrb'])
                gd(lambda E, q=store_q, brow=brow: E.dma_start(
                    out=lnrb[4 + q:5 + q, :], in_=brow), [bk_], ['lnrb'])

        for sq_i in range(SPC):
            XK = lambda i, q: f'x{i}q{q}'
            load_wkv()
            for q in range(NQ):
                qs = slice(q * QW, (q + 1) * QW)
                for i in range(ND):
                    sqt = sqs if i % 2 == 0 else pbuf[1]
                    sk = ['sqsA', 'sqsB'] if i % 2 == 0 else ['pb1']
                    P.emit('S', lambda E, i=i, qs=qs, sqt=sqt: E.activation(
                        sqt[:, :], xb[i][:, qs], AF.Square),
                        [XK(i, q)], sk)
                    P.emit('T', lambda E, i=i, qs=qs: E.matmul(
                        ps[0][0:1, 0:QW], onesc[:, :], xb[i][:, qs],
                        start=(i == 0), stop=(i == ND - 1),
                        tile_position=(0, 0)),
                        [XK(i, q), 'onesc'], [PK[0]])
                    P.emit('T', lambda E, sqt=sqt, i=i: E.matmul(
                        ps[1][32:33, 0:QW], onesc[:, :], sqt[:, :],
                        start=(i == 0), stop=(i == ND - 1),
                        tile_position=(0, 32)), sk + ['onesc'], [PK[1]])
                ln_stats(QW, 0, 1, store_q=q)
                for i in range(ND):
                    P.emit('T', lambda E, i=i, qs=qs: E.matmul(
                        ps[2][:, 0:QW], wkv_ap(i), xb[i][:, qs],
                        start=(i == 0), stop=(i == ND - 1)),
                        [wkv_key(i), XK(i, q)], [PK[2]])
                P.emit('V', lambda E, qs=qs: E.tensor_tensor(
                    kvl[:, qs], ps[2][:, 0:QW], rrep[:, :], ALU.mult),
                    [PK[2], 'rrep'], [f'kvl{q}'])
                P.emit('V', lambda E, qs=qs: E.scalar_tensor_tensor(
                    kvl[:, qs], brep[:, :], negwskv[:, 0:1], kvl[:, qs],
                    ALU.mult, ALU.add),
                    [f'kvl{q}'], [f'kvl{q}'], sreads=['brep', 'negwskv'])
            KVLK = [f'kvl{q}' for q in range(NQ)]
            for cb in range(2):
                bk = 3 + cb
                for ci in range(8):
                    c = cb * 8 + ci
                    P.emit('T', lambda E, bk=bk, c=c, ci=ci: E.transpose(
                        ps[bk][:, ci * 64:(ci + 1) * 64],
                        kvl[0:64, c * 128:(c + 1) * 128], ident[0:64, 0:64]),
                        KVLK + ['ident'], [PK[bk]] if ci == 0 else [PK[bk]])
                for ci in range(8):
                    c = cb * 8 + ci
                    P.emit('V', lambda E, bk=bk, c=c, ci=ci: E.tensor_copy(
                        kvt[:, c * 65:c * 65 + 64],
                        ps[bk][:, ci * 64:(ci + 1) * 64]), [PK[bk]], ['kvt'])
            P.emit('V', lambda E: E.memset(kvm2[:, :], 0.0), (), ['kvm2'])
            P.emit('V', lambda E: E.reduce_max(
                kvm2[0:64, 0:1], kvl[0:64, :], axis=AX.X,
                apply_absolute_value=True), KVLK, ['kvm2'])
            P.emit('V', lambda E: E.reduce_max(
                kvm2[64:128, 1:2], kvl[64:128, :], axis=AX.X,
                apply_absolute_value=True), KVLK, ['kvm2'])
            if debug and sq_i == 0:
                st(lambda E: E.dma_start(out=dbg['d_kvl'][:, :], in_=kvl[:, :]),
                   KVLK)
                st(lambda E: E.dma_start(out=dbg['d_kvt'][:, :], in_=kvt[:, :]),
                   ['kvt'])
                st(lambda E: E.dma_start(out=dbg['d_lnrb'][:, :],
                                         in_=lnrb[:, :]), ['lnrb'])

            for q in range(NQ):
                qs = slice(q * QW, (q + 1) * QW)
                P.emit('T', lambda E, q=q: E.matmul(
                    ps[0][0:1, 0:QW], ident[0:8, q:q + 1], lnrb[0:8, :],
                    start=True, stop=True), ['lnrb', 'ident'], [PK[0]])
                P.emit('V', lambda E: E.tensor_copy(
                    qt[0][0:1, :], ps[0][0:1, 0:QW]), [PK[0]], ['qt0h0'])
                P.emit('T', lambda E: E.matmul(
                    ps[1][:, 0:QW], onesr[:, :], qt[0][0:1, :],
                    start=True, stop=True), ['qt0h0', 'onesr'], [PK[1]])
                P.emit('V', lambda E: E.tensor_copy(
                    rrep[:, :], ps[1][:, 0:QW]), [PK[1]], ['rrep'])
                P.emit('T', lambda E, q=q: E.matmul(
                    ps[0][0:1, 0:QW], ident[0:8, 4 + q:5 + q], lnrb[0:8, :],
                    start=True, stop=True), ['lnrb', 'ident'], [PK[0]])
                P.emit('V', lambda E: E.tensor_copy(
                    qt[0][0:1, :], ps[0][0:1, 0:QW]), [PK[0]], ['qt0h0'])
                P.emit('T', lambda E: E.matmul(
                    ps[1][:, 0:QW], onesr[:, :], qt[0][0:1, :],
                    start=True, stop=True), ['qt0h0', 'onesr'], [PK[1]])
                P.emit('V', lambda E: E.tensor_copy(
                    brep[:, :], ps[1][:, 0:QW]), [PK[1]], ['brep'])
                for j in range(NHC):
                    bk = 2 + j % 2
                    for i in range(ND):
                        P.emit('T', lambda E, bk=bk, i=i, j=j, qs=qs: E.matmul(
                            ps[bk][:, 0:QW], wqkT[i][:, j * 128:(j + 1) * 128],
                            xb[i][:, qs], start=(i == 0), stop=(i == ND - 1)),
                            [f'wqkT{i}', XK(i, q)], [PK[bk]])
                    P.emit('V', lambda E, bk=bk, j=j: E.tensor_tensor(
                        qt[j][:, :], ps[bk][:, 0:QW], rrep[:, :], ALU.mult),
                        [PK[bk], 'rrep'], [f'qt{j}h0', f'qt{j}h1'])
                    P.emit('V', lambda E, j=j: E.scalar_tensor_tensor(
                        qt[j][:, :], brep[:, :], negws[:, j:j + 1], qt[j][:, :],
                        ALU.mult, ALU.add),
                        [f'qt{j}h0', f'qt{j}h1'],
                        [f'qt{j}h0', f'qt{j}h1'], sreads=['brep', 'negws'])
                P.emit('V', lambda E: E.memset(ps[5][:, :], -1e30),
                       (), [PK[5]])
                for j in range(NHC):
                    P.emit('S', lambda E, j=j: E.activation(
                        sqs[:, :], qt[j][:, :], AF.Abs),
                        [f'qt{j}h0', f'qt{j}h1'], ['sqsA', 'sqsB'])
                    P.emit('T', lambda E, j=j: E.matmul(
                        ps[5][32 * j:32 * j + 2, 0:QW], kvm2[:, :], sqs[:, :],
                        start=True, stop=True,
                        tile_position=(0, 32 * j)),
                        ['kvm2', 'sqsA', 'sqsB'], [PK[5]])
                if debug and sq_i == 0 and q == 0:
                    P.emit('V', lambda E: E.tensor_copy(
                        pbuf[0][:, :], ps[5][:, 0:QW]), [PK[5]], ['pb0'])
                    st(lambda E: E.dma_start(out=dbg['d_psb'][:, :],
                                             in_=pbuf[0][:, :]), ['pb0'])
                    st(lambda E: E.dma_start(out=dbg['d_sqs'][:, :],
                                             in_=sqs[:, :]), ['sqsA', 'sqsB'])
                    st(lambda E: E.dma_start(out=dbg['d_kvm2'][:, :],
                                             in_=kvm2[:, :]), ['kvm2'])
                P.emit('V', lambda E: E.reduce_max(
                    gsc[:, 0:1], ps[5][:, 0:QW], axis=AX.X, negate=True),
                    [PK[5]], ['gsc'])
                P.emit('S', lambda E: E.mul(gsc[:, 1:2], gsc[:, 0:1], SCALE),
                       ['gsc'], ['gsc'])
                if debug and sq_i == 0 and q == 0:
                    st(lambda E: E.dma_start(out=dbg['d_gsc'][:, :],
                                             in_=gsc[:, :]), ['gsc'])
                for h in range(H):
                    r_h = 32 * (h // 2) + (h % 2)
                    P.emit('T', lambda E, r_h=r_h: E.matmul(
                        ps[0][0:1, 0:1], ident[0:128, r_h:r_h + 1],
                        gsc[0:128, 1:2], start=True, stop=True),
                        ['gsc', 'ident'], [PK[0]])
                    P.emit('V', lambda E, h=h: E.tensor_copy(
                        stat[0:1, 100 + h:101 + h], ps[0][0:1, 0:1]),
                        [PK[0]], ['stat'])
                    P.emit('T', lambda E, h=h: E.matmul(
                        ps[1][:, 0:1], onesr[:, :],
                        stat[0:1, 100 + h:101 + h], start=True, stop=True),
                        ['stat', 'onesr'], [PK[1]])
                    P.emit('V', lambda E, h=h: E.tensor_copy(
                        negG[:, h:h + 1], ps[1][:, 0:1]), [PK[1]], ['negG'])
                if debug and sq_i == 0 and q == 0:
                    st(lambda E: E.dma_start(out=dbg['d_ng'][:, :],
                                             in_=negG[:, :]), ['negG'])
                    for j in range(NHC):
                        st(lambda E, j=j: E.dma_start(
                            out=dbg['d_qt'][:, j * QW:(j + 1) * QW],
                            in_=qt[j][:, :]), [f'qt{j}h0', f'qt{j}h1'])

                for hp in range(NHC):
                    j = hp
                    qke, qko = f'qt{j}h0', f'qt{j}h1'
                    obe = 4 + hp % 2
                    obo = 6 + hp % 2

                    def eS(c, hf, j=j):
                        bk = (c % 2) + 2 * hf
                        qk = f'qt{j}h{hf}'
                        return P.emit('T', lambda E, bk=bk, c=c, hf=hf, j=j:
                                      E.matmul(
                            ps[bk][:, 0:QW],
                            kvl[64 * hf:64 * hf + 64, c * 128:(c + 1) * 128],
                            qt[j][64 * hf:64 * hf + 64, :],
                            start=True, stop=True,
                            tile_position=(64 * hf, 0)),
                            KVLK + [qk], [PK[bk]])

                    def eE(c, hf, hp=hp):
                        bk = (c % 2) + 2 * hf
                        pbi = (c % 2) + 2 * hf
                        h = 2 * hp + hf
                        return P.emit('S', lambda E, bk=bk, pbi=pbi, h=h:
                                      E.activation(
                            pball[pbi][:, :], ps[bk][:, 0:QW], AF.Exp,
                            bias=negG[:, h:h + 1], scale=SCALE),
                            [PK[bk], 'negG'], [f'pb{pbi}'])

                    def eO(c, hf, obe=obe, obo=obo):
                        ob = obe if hf == 0 else obo
                        pbi = (c % 2) + 2 * hf
                        return P.emit('T', lambda E, c=c, ob=ob, pbi=pbi:
                                      E.matmul(
                            ps[ob][0:65, 0:QW], kvt[:, c * 65:(c + 1) * 65],
                            pball[pbi][:, :],
                            start=(c == 0), stop=(c == NKC - 1)),
                            [f'pb{pbi}', 'kvt'], [PK[ob]])

                    for c in range(2):
                        eS(c, 0)
                        eS(c, 1)
                        eE(c, 0)
                        eE(c, 1)
                    for c in range(2, NKC):
                        eO(c - 2, 0)
                        eO(c - 2, 1)
                        eS(c, 0)
                        eS(c, 1)
                        eE(c, 0)
                        eE(c, 1)
                    for c in (NKC - 2, NKC - 1):
                        eO(c, 0)
                        eO(c, 1)
                    P.emit('V', lambda E, obe=obe, j=j: E.tensor_copy(
                        qt[j][0:64, :], ps[obe][0:64, 0:QW]),
                        [PK[obe]], [qke])
                    P.emit('V', lambda E, obe=obe: E.tensor_copy(
                        sqs[64:65, :], ps[obe][64:65, 0:QW]),
                        [PK[obe]], ['sqsA', 'sqsB'])
                    gd(lambda E, he=2 * hp: E.dma_start(
                        out=lrows[he:he + 1, :], in_=sqs[64:65, :]),
                        ['sqsA', 'sqsB'], ['lrows'])
                    P.emit('V', lambda E, obo=obo: E.tensor_copy(
                        sqs[0:64, :], ps[obo][0:64, 0:QW]),
                        [PK[obo]], ['sqsA', 'sqsB'])
                    gd(lambda E, j=j: E.dma_start(
                        out=qt[j][64:128, :], in_=sqs[0:64, :]),
                        ['sqsA', 'sqsB'], [qko])
                    P.emit('V', lambda E, obo=obo: E.tensor_copy(
                        sqs[64:65, :], ps[obo][64:65, 0:QW]),
                        [PK[obo]], ['sqsA', 'sqsB'])
                    gd(lambda E, ho=2 * hp + 1: E.dma_start(
                        out=lrows[ho:ho + 1, :], in_=sqs[64:65, :]),
                        ['sqsA', 'sqsB'], ['lrows'])
                P.emit('V', lambda E: E.reciprocal(rl[0:8, :], lrows[0:8, :]),
                       ['lrows'], ['rl'])
                for j in range(NHC):
                    P.emit('T', lambda E, j=j: E.matmul(
                        ps[4][:, 0:QW], selL[:, j * 128:(j + 1) * 128],
                        rl[0:8, :], start=True, stop=True),
                        ['rl', 'selL'], [PK[4]])
                    P.emit('V', lambda E, j=j: E.tensor_tensor(
                        qt[j][:, :], qt[j][:, :], ps[4][:, 0:QW], ALU.mult),
                        [PK[4], f'qt{j}h0', f'qt{j}h1'],
                        [f'qt{j}h0', f'qt{j}h1'])
                if debug and sq_i == 0 and q == 0:
                    st(lambda E: E.dma_start(out=dbg['d_lr'][:, :],
                                             in_=lrows[0:8, :]), ['lrows'])
                    for j in range(NHC):
                        st(lambda E, j=j: E.dma_start(
                            out=dbg['d_on'][:, j * QW:(j + 1) * QW],
                            in_=qt[j][:, :]), [f'qt{j}h0', f'qt{j}h1'])

                cs = slice(q * QW, (q + 1) * QW)
                for oc in range(ND):
                    bk = 5 + oc % 2
                    sqt = sqs if oc % 2 == 0 else pbuf[1]
                    sk = ['sqsA', 'sqsB'] if oc % 2 == 0 else ['pb1']
                    for j in range(NHC):
                        P.emit('T', lambda E, bk=bk, j=j, oc=oc: E.matmul(
                            ps[bk][:, 0:QW],
                            wovT[j][:, oc * 128:(oc + 1) * 128],
                            qt[j][:, :], start=(j == 0),
                            stop=(j == NHC - 1)),
                            [f'wovT{j}', f'qt{j}h0', f'qt{j}h1'],
                            [PK[bk]])
                    P.emit('S', lambda E, bk=bk, oc=oc: E.activation(
                        ab[oc][:, :], ps[bk][:, 0:QW], AF.Identity),
                        [PK[bk]], [f'ab{oc}'])
                    P.emit('S', lambda E, bk=bk, sqt=sqt: E.activation(
                        sqt[:, :], ps[bk][:, 0:QW], AF.Square),
                        [PK[bk]], sk)
                    P.emit('T', lambda E, oc=oc: E.matmul(
                        ps[0][0:1, 0:QW], onesc[:, :], ab[oc][:, :],
                        start=(oc == 0), stop=(oc == ND - 1),
                        tile_position=(0, 0)),
                        [f'ab{oc}', 'onesc'], [PK[0]])
                    P.emit('T', lambda E, sqt=sqt, oc=oc: E.matmul(
                        ps[1][32:33, 0:QW], onesc[:, :], sqt[:, :],
                        start=(oc == 0), stop=(oc == ND - 1),
                        tile_position=(0, 32)), sk + ['onesc'],
                        [PK[1]])
                ln_stats(QW, 0, 1)
                for i in range(ND):
                    P.emit('V', lambda E, i=i: E.tensor_tensor(
                        ab[i][:, :], ab[i][:, :], rrep[:, :], ALU.mult),
                        ['rrep', f'ab{i}'], [f'ab{i}'])
                    P.emit('V', lambda E, i=i: E.tensor_tensor(
                        ab[i][:, :], ab[i][:, :], brep[:, :],
                        ALU.subtract), ['brep', f'ab{i}'], [f'ab{i}'])
                    P.emit('V', lambda E, i=i, cs=cs: E.tensor_tensor(
                        xb[i][:, cs], xb[i][:, cs], ab[i][:, :], ALU.add),
                        [f'ab{i}', XK(i, q)], [XK(i, q)])
                if debug and sq_i == 0 and q == 0:
                    for oc in range(ND):
                        st(lambda E, oc=oc: E.dma_start(
                            out=dbg['d_ab'][:, oc * HW:(oc + 1) * HW],
                            in_=ab[oc][:, 0:HW]), [f'ab{oc}'])

            if debug and sq_i == 0:
                st(lambda E: E.dma_start(out=dbg['d_xa'][:, :],
                                         in_=xb[0][:, :]),
                   [XK(0, qq) for qq in range(NQ)])
            for q in range(NQ):
                if True:
                    cs = slice(q * QW, (q + 1) * QW)
                    for i in range(ND):
                        sqt = sqs if i % 2 == 0 else pbuf[1]
                        sk = ['sqsA', 'sqsB'] if i % 2 == 0 else ['pb1']
                        P.emit('S', lambda E, i=i, cs=cs, sqt=sqt: E.activation(
                            sqt[:, :], xb[i][:, cs], AF.Square),
                            [XK(i, q)], sk)
                        P.emit('T', lambda E, i=i, cs=cs: E.matmul(
                            ps[0][0:1, 0:QW], onesc[:, :], xb[i][:, cs],
                            start=(i == 0), stop=(i == ND - 1),
                            tile_position=(0, 0)),
                            [XK(i, q), 'onesc'], [PK[0]])
                        P.emit('T', lambda E, sqt=sqt, i=i: E.matmul(
                            ps[1][32:33, 0:QW], onesc[:, :], sqt[:, :],
                            start=(i == 0), stop=(i == ND - 1),
                            tile_position=(0, 32)), sk + ['onesc'], [PK[1]])
                    ln_stats(QW, 0, 1)
                    for i in range(ND):
                        P.emit('T', lambda E, i=i, cs=cs: E.matmul(
                            ps[2][0:8, 0:QW], rTs[i][:, :], xb[i][:, cs],
                            start=(i == 0), stop=(i == ND - 1)),
                            [f'rr{i}', XK(i, q)], [PK[2]])
                    P.emit('V', lambda E: E.tensor_tensor(
                        gate8[0:8, :], ps[2][0:8, 0:QW], rrep[0:8, :],
                        ALU.mult), [PK[2], 'rrep'], ['rl'])
                    P.emit('V', lambda E: E.scalar_tensor_tensor(
                        gate8[0:8, :], brep[0:8, :], negwsR[0:8, 0:1],
                        gate8[0:8, :], ALU.mult, ALU.add),
                        ['rl'], ['rl'], sreads=['brep', 'negwsR'])
                    for tb4 in range(4):
                        P.emit('T', lambda E, tb4=tb4: E.transpose(
                            ps[3][:, tb4 * 8:(tb4 + 1) * 8],
                            gate8[0:8, tb4 * 128:(tb4 + 1) * 128],
                            ident[0:8, 0:8]),
                            ['rl', 'ident'], [PK[3]])
                    P.emit('V', lambda E: E.tensor_copy(
                        ltk[:, :], ps[3][:, 0:32]), [PK[3]], ['ltk'])
                    if debug and sq_i == 0 and q == 0:
                        st(lambda E: E.dma_start(out=dbg['d_ltk'][:, :],
                                                 in_=ltk[:, 0:16]), ['ltk'])
                        st(lambda E: E.dma_start(out=dbg['d_rr'][:, :],
                                                 in_=rrep[:, :]), ['rrep'])
                        st(lambda E: E.dma_start(out=dbg['d_st'][:, :],
                                                 in_=stat[:, :]), ['stat'])
                    for tb in range(4):
                        tsl = slice(tb * 8, (tb + 1) * 8)
                        P.emit('V', lambda E, tsl=tsl: E.reduce_max(
                            m1t[:, 0:1], ltk[:, tsl], axis=AX.X),
                            ['ltk'], ['m1t'])
                        P.emit('V', lambda E, tsl=tsl: E.reduce_max(
                            nm1t[:, 0:1], ltk[:, tsl], axis=AX.X, negate=True),
                            ['ltk'], ['nm1t'])
                        P.emit('S', lambda E, tsl=tsl: E.activation(
                            geg[:, :], ltk[:, tsl], AF.Exp, bias=nm1t[:, 0:1]),
                            ['ltk', 'nm1t'], ['geg'])
                        P.emit('V', lambda E: E.reduce_sum(
                            lsum[:, 0:1], geg[:, :], axis=AX.X),
                            ['geg'], ['lsum'])
                        P.emit('V', lambda E, tsl=tsl: E.tensor_scalar(
                            gt2[:, :], ltk[:, tsl], m1t[:, 0:1], None,
                            ALU.subtract), ['ltk'], ['gt2'], sreads=['m1t'])
                        P.emit('V', lambda E: E.tensor_scalar(
                            gmsk[:, :], gt2[:, :], zcol[:, 0:1], None,
                            ALU.is_ge), ['gt2'], ['gmsk'], sreads=['zcol'])
                        P.emit('V', lambda E, tsl=tsl: E.scalar_tensor_tensor(
                            gt2[:, :], gmsk[:, :], -1e30, ltk[:, tsl],
                            ALU.mult, ALU.add), ['ltk'], ['gt2'],
                            sreads=['gmsk'])
                        P.emit('V', lambda E: E.reduce_max(
                            m2t[:, 0:1], gt2[:, :], axis=AX.X),
                            ['gt2'], ['m2t'])
                        P.emit('V', lambda E, tsl=tsl: E.tensor_scalar(
                            gmsk[:, :], ltk[:, tsl], m2t[:, 0:1], None,
                            ALU.subtract), ['ltk'], ['gmsk'], sreads=['m2t'])
                        P.emit('V', lambda E: E.tensor_scalar(
                            gmsk[:, :], gmsk[:, :], zcol[:, 0:1], None,
                            ALU.is_ge), ['gmsk'], ['gmsk'], sreads=['zcol'])
                        P.emit('V', lambda E: E.tensor_tensor(
                            gem[:, :], geg[:, :], gmsk[:, :], ALU.mult),
                            ['geg', 'gmsk'], ['gem'])
                        P.emit('V', lambda E: E.reduce_sum(
                            s2t[:, 0:1], gem[:, :], axis=AX.X),
                            ['gem'], ['s2t'])
                        P.emit('S', lambda E: E.mul(
                            dent[:, 0:1], lsum[:, 0:1], GEPS),
                            ['lsum'], ['dent'])
                        P.emit('V', lambda E: E.tensor_tensor(
                            dent[:, 0:1], dent[:, 0:1], s2t[:, 0:1], ALU.add),
                            ['dent', 's2t'], ['dent'])
                        P.emit('V', lambda E: E.reciprocal(
                            rdent[:, 0:1], dent[:, 0:1]), ['dent'], ['rdent'])
                        P.emit('V', lambda E, tsl=tsl: E.tensor_scalar(
                            gfin[:, tsl], gem[:, :], rdent[:, 0:1], None,
                            ALU.mult), ['gem'], ['gfin'], sreads=['rdent'])
                        if debug and sq_i == 0 and q == 0 and tb == 0:
                            st(lambda E: E.dma_start(out=dbg['d_geg'][:, :],
                                                     in_=geg[:, :]), ['geg'])
                            st(lambda E: E.dma_start(out=dbg['d_gt2'][:, :],
                                                     in_=gt2[:, :]), ['gt2'])
                            st(lambda E: E.dma_start(out=dbg['d_gmsk'][:, :],
                                                     in_=gmsk[:, :]), ['gmsk'])
                            st(lambda E: E.dma_start(out=dbg['d_gem'][:, :],
                                                     in_=gem[:, :]), ['gem'])
                            st(lambda E: E.dma_start(out=dbg['d_gsc2'][:, :],
                                                     in_=gsc[:, :]), ['gsc'])
                    for tb4 in range(4):
                        P.emit('T', lambda E, tb4=tb4: E.transpose(
                            ps[2][0:8, tb4 * 128:(tb4 + 1) * 128],
                            gfin[:, tb4 * 8:(tb4 + 1) * 8], ident[:, :]),
                            ['gfin', 'ident'], [PK[2]])
                    P.emit('V', lambda E: E.tensor_copy(
                        gate8[0:8, :], ps[2][0:8, 0:QW]), [PK[2]], ['rl'])
                    if debug and sq_i == 0 and q == 0:
                        st(lambda E: E.dma_start(out=dbg['d_gf'][:, :],
                                                 in_=gfin[:, 0:16]), ['gfin'])
                    for g in range(2):
                        gb = 4 + g
                        hb = 6 + g
                        P.emit('T', lambda E, g=g, gb=gb: E.matmul(
                            ps[gb][:, 0:QW], selE[:, g * 128:(g + 1) * 128],
                            gate8[0:8, :], start=True, stop=True),
                            ['rl', 'selE'], [PK[gb]])
                        for i in range(ND):
                            P.emit('T', lambda E, i=i, g=g, hb=hb, cs=cs:
                                   E.matmul(
                                ps[hb][:, 0:QW],
                                aTs[i][:, g * 128:(g + 1) * 128],
                                xb[i][:, cs], start=(i == 0),
                                stop=(i == ND - 1)),
                                [f'aa{i}', XK(i, q)], [PK[hb]])
                        P.emit('V', lambda E, g=g, hb=hb: E.tensor_tensor(
                            hsb[g][:, :], ps[hb][:, 0:QW], rrep[:, :],
                            ALU.mult), [PK[hb], 'rrep'], [HSK[g]])
                        P.emit('V', lambda E, g=g: E.scalar_tensor_tensor(
                            hsb[g][:, :], brep[:, :], negwsA[:, g:g + 1],
                            hsb[g][:, :], ALU.mult, ALU.add),
                            [HSK[g]], [HSK[g]], sreads=['brep', 'negwsA'])
                        P.emit('V', lambda E, g=g, gb=gb: E.tensor_tensor(
                            hsb[g][:, :], hsb[g][:, :], ps[gb][:, 0:QW],
                            ALU.mult), [PK[gb], HSK[g]], [HSK[g]])
                    if debug and sq_i == 0 and q == 0:
                        for g in range(2):
                            st(lambda E, g=g: E.dma_start(
                                out=dbg['d_hs'][:, g * HW:(g + 1) * HW],
                                in_=hsb[g][:, 0:HW]), [HSK[g]])
                    for oc in range(ND):
                        bk = oc % 2
                        sqt = sqs if oc % 2 == 0 else pbuf[1]
                        sk = ['sqsA', 'sqsB'] if oc % 2 == 0 else ['pb1']
                        for i in range(ND):
                            P.emit('T', lambda E, bk=bk, i=i, oc=oc, cs=cs:
                                   E.matmul(
                                ps[bk][:, 0:QW],
                                wTs[i][:, oc * 128:(oc + 1) * 128],
                                xb[i][:, cs], start=(i == 0),
                                stop=(i == ND - 1)),
                                [f'wt{i}', XK(i, q)], [PK[bk]])
                        P.emit('T', lambda E, oc=oc: E.matmul(
                            ps[5][:, 0:QW], bTs[0][:, oc * 128:(oc + 1) * 128],
                            hsb[0][:, :], start=True, stop=False),
                            ['bb0', 'pb2'], [PK[5]])
                        P.emit('T', lambda E, oc=oc: E.matmul(
                            ps[5][:, 0:QW], bTs[1][:, oc * 128:(oc + 1) * 128],
                            hsb[1][:, :], start=False, stop=True),
                            ['bb1', 'pb3'], [PK[5]])
                        P.emit('V', lambda E, bk=bk, oc=oc: E.tensor_tensor(
                            ab[oc][:, :], ps[bk][:, 0:QW], rrep[:, :],
                            ALU.mult), [PK[bk], 'rrep'], [f'ab{oc}'])
                        P.emit('V', lambda E, oc=oc: E.scalar_tensor_tensor(
                            ab[oc][:, :], brep[:, :], negwsW[:, oc:oc + 1],
                            ab[oc][:, :], ALU.mult, ALU.add),
                            [f'ab{oc}'], [f'ab{oc}'], sreads=['brep', 'negwsW'])
                        P.emit('V', lambda E, oc=oc: E.tensor_tensor(
                            ab[oc][:, :], ab[oc][:, :], ps[5][:, 0:QW],
                            ALU.add), [PK[5], f'ab{oc}'], [f'ab{oc}'])
                        P.emit('S', lambda E, oc=oc: E.activation(
                            ab[oc][:, :], ab[oc][:, :], AF.Identity,
                            bias=bcols[:, oc:oc + 1]),
                            ['bcols', f'ab{oc}'], [f'ab{oc}'])
                        P.emit('S', lambda E, oc=oc, sqt=sqt: E.activation(
                            sqt[:, :], ab[oc][:, :], AF.Square),
                            [f'ab{oc}'], sk)
                        P.emit('T', lambda E, oc=oc: E.matmul(
                            ps[2][0:1, 0:QW], onesc[:, :], ab[oc][:, :],
                            start=(oc == 0), stop=(oc == ND - 1),
                            tile_position=(0, 0)),
                            [f'ab{oc}', 'onesc'], [PK[2]])
                        P.emit('T', lambda E, oc=oc, sqt=sqt: E.matmul(
                            ps[3][32:33, 0:QW], onesc[:, :], sqt[:, :],
                            start=(oc == 0), stop=(oc == ND - 1),
                            tile_position=(0, 32)), sk + ['onesc'],
                            [PK[3]])
                    ln_stats(QW, 2, 3)
                    for i in range(ND):
                        P.emit('V', lambda E, i=i: E.tensor_tensor(
                            ab[i][:, :], ab[i][:, :], rrep[:, :], ALU.mult),
                            ['rrep', f'ab{i}'], [f'ab{i}'])
                        P.emit('V', lambda E, i=i: E.tensor_tensor(
                            ab[i][:, :], ab[i][:, :], brep[:, :],
                            ALU.subtract), ['brep', f'ab{i}'], [f'ab{i}'])
                        P.emit('V', lambda E, i=i, cs=cs: E.tensor_tensor(
                            xb[i][:, cs], xb[i][:, cs], ab[i][:, :], ALU.add),
                            [f'ab{i}', XK(i, q)], [XK(i, q)])
                        st(lambda E, i=i, cs=cs, sq_i=sq_i: E.dma_start(
                            out=xout[sq_i, i * 128:(i + 1) * 128, cs],
                            in_=xb[i][:, cs]), [XK(i, q)])
                    P.dma_fence_r('Y', 'ot',
                                  [XK(i, q) for i in range(ND)])
                    if sq_i + 1 < SPC:
                        for i in range(ND):
                            ld(lambda E, i=i, q=q, sq_i=sq_i: E.dma_start(
                                out=xb[i][:, q * QW:(q + 1) * QW],
                                in_=xin[sq_i + 1, i * 128:(i + 1) * 128,
                                        q * QW:(q + 1) * QW]), [XK(i, q)])
                        P.dma_fence('Y', 'ld', [XK(i, q) for i in range(ND)])

        if debug2:
            st(lambda E: E.dma_start(out=dbg['pm_ws'][:, 0:4], in_=negws[:, :], allow_slow_non_contiguous=True),
               ['negws'])
            st(lambda E: E.dma_start(out=dbg['pm_ws'][:, 4:5],
                                     in_=negwskv[:, :], allow_slow_non_contiguous=True), ['negwskv'])
            st(lambda E: E.dma_start(out=dbg['pm_ws'][:, 5:13],
                                     in_=negwsW[:, :], allow_slow_non_contiguous=True), ['negwsW'])
            st(lambda E: E.dma_start(out=dbg['pm_ws'][:, 13:15],
                                     in_=negwsA[:, :], allow_slow_non_contiguous=True), ['negwsA'])
            for i in range(ND):
                st(lambda E, i=i: E.dma_start(
                    out=dbg['pm_wqk'][:, i * 512:(i + 1) * 512],
                    in_=wqkT[i][:, :]), [f'wqkT{i}'])
            for j in range(NHC):
                st(lambda E, j=j: E.dma_start(
                    out=dbg['pm_wov'][:, j * 1024:(j + 1) * 1024],
                    in_=wovT[j][:, :]), [f'wovT{j}'])
            st(lambda E: E.dma_start(out=dbg['pm_kvl'][:, :], in_=kvl[:, :]),
               [f'kvl{qq}' for qq in range(NQ)])
            st(lambda E: E.dma_start(out=dbg['pm_lnrb'][:, :], in_=lnrb[:, :]),
               ['lnrb'])
        P.items.append(('Y', tuple([('ot', P.val.get('ot', 0)),
                                    ('gd', P.val.get('gd', 0))]), None, None, 0))

        def walker(engname):
            def go(E):
                waited = {}
                for (en_, waits, fn, sem, step) in P.items:
                    if en_ != engname:
                        continue
                    for (sn, vv) in waits:
                        if waited.get(sn, -1) < vv:
                            E.wait_ge(sems[sn], vv)
                            waited[sn] = vv
                    if fn is None:
                        continue
                    ins = fn(E)
                    ins.then_inc(sems[sem], step)
            return go

        @block.sync
        def _(E):
            walker('Y')(E)

        @block.tensor
        def _(E):
            walker('T')(E)

        @block.vector
        def _(E):
            walker('V')(E)

        @block.scalar
        def _(E):
            walker('S')(E)

        @block.gpsimd
        def _(E):
            walker('G')(E)

    return nc


def _host_prep(L0_attn_x, L0_attn_Wq, L0_attn_Wkv_down, L0_attn_Wk_up,
               L0_attn_Wv_up, L0_attn_Wo, L0_ffn_W, L0_ffn_b, L0_ffn_R,
               L0_ffn_A, L0_ffn_B):
    f = np.float32
    x = np.ascontiguousarray(np.asarray(L0_attn_x, f).transpose(0, 2, 1))
    wq = np.ascontiguousarray(np.asarray(L0_attn_Wq, f))
    wkup = np.ascontiguousarray(np.asarray(L0_attn_Wk_up, f))
    wvup = np.ascontiguousarray(np.asarray(L0_attn_Wv_up, f))
    woT = np.ascontiguousarray(np.asarray(L0_attn_Wo, f).T)
    wkvT = np.asarray(L0_attn_Wkv_down, f).T
    wkvT2 = np.ascontiguousarray(np.concatenate([wkvT, wkvT], axis=1))
    wT = np.ascontiguousarray(np.asarray(L0_ffn_W, f).T)
    rT = np.ascontiguousarray(np.asarray(L0_ffn_R, f).T)
    A = np.asarray(L0_ffn_A, f)
    aT = np.zeros((1024, 256), f)
    for ex in range(8):
        g, ei = divmod(ex, 4)
        aT[:, g * 128 + ei * 32: g * 128 + ei * 32 + 16] = A[ex].T
    Bw = np.asarray(L0_ffn_B, f)
    bT = np.zeros((256, 1024), f)
    for ex in range(8):
        g, ei = divmod(ex, 4)
        bT[g * 128 + ei * 32: g * 128 + ei * 32 + 16, :] = Bw[ex].T
    bcol = np.ascontiguousarray(np.asarray(L0_ffn_b, f).reshape(1024, 1))
    identc = np.eye(128, dtype=f)
    onesc = np.ones((128, 1), f)
    onesr = np.ones((1, 128), f)
    selL = np.zeros((8, 512), f)
    for j in range(4):
        selL[2 * j, j * 128:j * 128 + 64] = 1.0
        selL[2 * j + 1, j * 128 + 64:j * 128 + 128] = 1.0
    selE = np.zeros((8, 256), f)
    for g in range(2):
        for ex in range(4):
            selE[g * 4 + ex, g * 128 + ex * 32:g * 128 + (ex + 1) * 32] = 1.0
    base = dict(wq=wq, wkup=wkup, wvup=wvup, woT=woT, wkvT2=wkvT2, wTd=wT,
                rTd=rT, aTd=aT, bTd=bT, bcold=bcol, identd=identc,
                onescd=onesc, onesrd=onesr, selLd=selL, selEd=selE,
                epsd=np.full((128, 1), LNEPS, f))
    in_maps = []
    for c in range(NCORES):
        m = dict(base)
        m["xin"] = np.ascontiguousarray(x[c * SPC:(c + 1) * SPC])
        in_maps.append(m)
    return in_maps


def kernel(**inputs):
    debug = bool(inputs.pop("_debug", False))
    debug2 = bool(inputs.pop("_debug2", False))
    in_maps = _host_prep(**inputs)
    nc = build(debug=debug, debug2=debug2)
    res = run_bass_kernel_spmd(nc, in_maps, core_ids=list(range(NCORES)))
    outs = []
    for c in range(NCORES):
        xo = np.asarray(res.results[c]["xout"])
        outs.append(xo.transpose(0, 2, 1))
    full = np.ascontiguousarray(np.concatenate(outs, axis=0).astype(np.float32))
    if debug or debug2:
        kernel._dbg = res.results
    return full


# version 08
# speedup vs baseline: 1.1057x; 1.0558x over previous
import numpy as np
from contextlib import ExitStack
import concourse.bass as bass
import concourse.mybir as mybir
from concourse.bass_utils import run_bass_kernel_spmd

F32 = mybir.dt.float32
AF = mybir.ActivationFunctionType
ALU = mybir.AluOpType
AX = mybir.AxisListType

NCORES = 8
T, D = 2048, 1024
SPC = 4
NQ = 4
QW = 512
HW = 256
ND = 8
NHC = 4
NKC = 16
H = 8
SCALE = 0.0625
LNEPS = 1e-5
GEPS = 1e-6
INV_D = 1.0 / 1024.0

ESEM = {'T': 't', 'V': 'v', 'S': 's', 'G': 'g'}


class Prog:

    def __init__(self):
        self.items = []
        self.val = {}
        self.w = {}
        self.r = {}

    def emit(self, eng, fn, reads=(), writes=(), extra=(), dma=None,
             sreads=()):
        waits = {}

        def add(mk, force=False):
            if mk is None:
                return
            sn, vv = mk
            if not force and eng in ESEM and sn == ESEM[eng]:
                return
            if waits.get(sn, -1) < vv:
                waits[sn] = vv

        force_all = eng in ('V', 'S')
        for k in reads:
            add(self.w.get(k), force=force_all)
        for k in sreads:
            add(self.w.get(k), force=True)
        for k in writes:
            add(self.w.get(k))
            for sn, vv in self.r.get(k, {}).items():
                add((sn, vv))
        for mk in extra:
            add(mk)
        if dma is not None:
            sem, step = dma, 16
        else:
            sem, step = ESEM[eng], 1
        self.items.append((eng, tuple(waits.items()), fn, sem, step))
        self.val[sem] = self.val.get(sem, 0) + step
        mk = (sem, self.val[sem])
        for k in list(reads) + list(sreads):
            d = self.r.setdefault(k, {})
            if d.get(mk[0], -1) < mk[1]:
                d[mk[0]] = mk[1]
        for k in writes:
            self.w[k] = mk
            self.r[k] = {}
        return mk

    def barrier_mark(self):
        return [(sn, vv) for sn, vv in self.val.items()]

    def dma_fence(self, eng, sem, keys=()):
        v = self.val.get(sem, 0)
        self.items.append((eng, ((sem, v),), None, None, 0))
        for k in keys:
            self.w[k] = (sem, v)
            self.r[k] = {}

    def dma_fence_r(self, eng, sem, rkeys=()):
        v = self.val.get(sem, 0)
        self.items.append((eng, ((sem, v),), None, None, 0))
        for k in rkeys:
            self.r[k] = {sem: v}


def build(debug=False, debug2=False):
    nc = bass.Bass("TRN2", target_bir_lowering=False, debug=False)

    xin = nc.dram_tensor("xin", [SPC, D, T], F32, kind="ExternalInput").ap()
    wqd = nc.dram_tensor("wq", [2048, 1024], F32, kind="ExternalInput").ap()
    wkupd = nc.dram_tensor("wkup", [2048, 64], F32, kind="ExternalInput").ap()
    wvupd = nc.dram_tensor("wvup", [2048, 64], F32, kind="ExternalInput").ap()
    woTd = nc.dram_tensor("woT", [2048, 1024], F32, kind="ExternalInput").ap()
    wkvT2d = nc.dram_tensor("wkvT2", [1024, 128], F32, kind="ExternalInput").ap()
    wTd = nc.dram_tensor("wTd", [1024, 1024], F32, kind="ExternalInput").ap()
    rTd = nc.dram_tensor("rTd", [1024, 8], F32, kind="ExternalInput").ap()
    aTd = nc.dram_tensor("aTd", [1024, 256], F32, kind="ExternalInput").ap()
    bTd = nc.dram_tensor("bTd", [256, 1024], F32, kind="ExternalInput").ap()
    bcold = nc.dram_tensor("bcold", [1024, 1], F32, kind="ExternalInput").ap()
    identd = nc.dram_tensor("identd", [128, 128], F32, kind="ExternalInput").ap()
    onescd = nc.dram_tensor("onescd", [128, 1], F32, kind="ExternalInput").ap()
    onesrd = nc.dram_tensor("onesrd", [1, 128], F32, kind="ExternalInput").ap()
    selLd = nc.dram_tensor("selLd", [8, 512], F32, kind="ExternalInput").ap()
    selEd = nc.dram_tensor("selEd", [8, 256], F32, kind="ExternalInput").ap()
    epsd = nc.dram_tensor("epsd", [128, 1], F32, kind="ExternalInput").ap()
    xout = nc.dram_tensor("xout", [SPC, D, T], F32, kind="ExternalOutput").ap()
    dbg = {}
    if debug2:
        for nm, shp in [("pm_ws", [128, 16]), ("pm_wqk", [128, 4096]),
                        ("pm_wov", [128, 4096]), ("pm_kvl", [128, 2048]),
                        ("pm_wov0", [128, 4096]), ("pm_lnrb", [8, 512])]:
            dbg[nm] = nc.dram_tensor(nm, shp, F32, kind="ExternalOutput").ap()
    if debug:
        for nm, shp in [("d_kvl", [128, T]), ("d_kvt", [128, NKC * 65]),
                        ("d_qt", [128, 4 * QW]), ("d_lr", [8, QW]),
                        ("d_ng", [128, 8]), ("d_on", [128, 4 * QW]),
                        ("d_ab", [128, 8 * HW]), ("d_lnrb", [8, 512]),
                        ("d_gf", [128, 16]), ("d_hs", [128, 2 * HW]),
                        ("d_wqk", [128, 512]), ("d_wov", [128, 1024]),
                        ("d_kvm2", [128, 2]), ("d_gsc", [128, 8]),
                        ("d_kvm_a", [128, 1]), ("d_kvm_b", [128, 1]),
                        ("d_xa", [128, T]), ("d_ltk", [128, 16]),
                        ("d_geg", [128, 8]), ("d_gt2", [128, 8]),
                        ("d_gmsk", [128, 8]), ("d_gem", [128, 8]),
                        ("d_gsc2", [128, 8]),
                        ("d_rr", [128, 512]), ("d_st", [34, 512]),
                        ("d_psb", [128, 512]), ("d_sqs", [128, 512])]:
            dbg[nm] = nc.dram_tensor(nm, shp, F32, kind="ExternalOutput").ap()

    P = Prog()

    with ExitStack() as stk:
        en = stk.enter_context

        def sb(name, shape):
            return en(nc.sbuf_tensor(name, shape, F32))

        xb = [sb(f"x{i}", [128, T]) for i in range(ND)]
        wqkT = [sb(f"wqk{i}", [128, 512]) for i in range(ND)]
        wovT = [sb(f"wov{j}", [128, 1024]) for j in range(NHC)]
        wTs = [sb(f"wt{i}", [128, 1024]) for i in range(ND)]
        aTs = [sb(f"aa{i}", [128, 256]) for i in range(ND)]
        bTs = [sb(f"bb{g}", [128, 1024]) for g in range(2)]
        rTs = [sb(f"rr{i}", [128, 8]) for i in range(ND)]
        bcols = sb("bcols", [128, 8])
        negws = sb("negws", [128, 4])
        negwskv = sb("negwskv", [128, 1])
        negwsW = sb("negwsw", [128, 8])
        negwsA = sb("negwsa", [128, 2])
        negwsR = sb("negwsr", [8, 1])
        kvl = sb("kvl", [128, T])
        kvt = sb("kvt", [128, NKC * 65])
        lnrb = sb("lnrb", [8, 512])
        ident = sb("ident", [128, 128])
        onesc = sb("onesc", [128, 1])
        onesr = sb("onesr", [1, 128])
        selL = sb("selL", [8, 512])
        selE = sb("selE", [8, 256])
        negG = sb("negg", [128, 8])
        kvm = sb("kvm", [128, 1])
        kvm2 = sb("kvm2", [128, 2])
        epsc = sb("epsc", [128, 1])
        qt = [sb(f"q{j}", [128, 512]) for j in range(NHC)]
        pbuf = [sb(f"p{k}", [128, 512]) for k in range(2)]
        pbuf2 = [sb(f"pp{k}", [128, 512]) for k in range(2)]
        pball = [pbuf[0], pbuf[1], pbuf2[0], pbuf2[1]]
        sqs = sb("sqs", [128, 512])
        rrep = sb("rrep", [128, 512])
        brep = sb("brep", [128, 512])
        ab = [sb(f"ab{i}", [128, 512]) for i in range(ND)]
        stat = sb("stat", [34, 512])
        lrows = sb("lrows", [8, 512])
        rl = sb("rl", [8, 512])
        gate8 = rl
        hsb = [pbuf2[0], pbuf2[1]]
        HSK = ['pb2', 'pb3']
        ltk = sb("ltk", [128, 32])
        geg = sb("geg", [128, 8])
        gt2 = sb("gt2", [128, 8])
        gmsk = sb("gmsk", [128, 8])
        gem = sb("gem", [128, 8])
        gsc = sb("gsc", [128, 8])
        gfin = sb("gfin", [128, 32])
        m1t = sb("m1t", [128, 1])
        nm1t = sb("nm1t", [128, 1])
        m2t = sb("m2t", [128, 1])
        lsum = sb("lsum", [128, 1])
        s2t = sb("s2t", [128, 1])
        dent = sb("dent", [128, 1])
        rdent = sb("rdent", [128, 1])
        zcol = sb("zcol", [128, 1])

        ps = [en(nc.psum_tensor(f"ps{k}", [128, 512], F32)) for k in range(8)]
        PK = [f"ps{k}" for k in range(8)]

        sems = {}
        for sn in ('t', 'v', 's', 'g', 'ld', 'ot', 'gd'):
            sems[sn] = en(nc.semaphore(sn))
        block = en(nc.Block())

        def ld(fn, writes):
            return P.emit('Y', fn, (), writes, dma='ld')

        def st(fn, reads):
            return P.emit('Y', fn, reads, (), dma='ot')

        def gd(fn, reads=(), writes=()):
            mk = P.emit('G', fn, reads, writes, dma='gd')
            P.items.append(('G', ((mk[0], mk[1]),), None, None, 0))
            return mk

        ld(lambda E: E.dma_start(out=ident[:], in_=identd[:, :]), ['ident'])
        ld(lambda E: E.dma_start(out=onesc[:], in_=onescd[:, :]), ['onesc'])
        ld(lambda E: E.dma_start(out=onesr[:], in_=onesrd[:, :]), ['onesr'])
        ld(lambda E: E.dma_start(out=selL[:], in_=selLd[:, :]), ['selL'])
        ld(lambda E: E.dma_start(out=selE[:], in_=selEd[:, :]), ['selE'])
        ld(lambda E: E.dma_start(out=epsc[:], in_=epsd[:, :]), ['epsc'])
        def wkv_ap(i):
            return (pbuf[0] if i < 4 else pbuf2[0])[:, (i % 4) * 128:
                                                    (i % 4) * 128 + 128]

        def wkv_key(i):
            return 'pb0' if i < 4 else 'pb2'

        def load_wkv():
            for i in range(ND):
                ld(lambda E, i=i: E.dma_start(
                    out=wkv_ap(i), in_=wkvT2d[i * 128:(i + 1) * 128, :]),
                    [wkv_key(i)])
            P.dma_fence('Y', 'ld', ['pb0', 'pb2'])
        load_wkv()
        for i in range(ND):
            ld(lambda E, i=i: E.dma_start(
                out=wTs[i][:], in_=wTd[i * 128:(i + 1) * 128, :]), [f'wt{i}'])
            ld(lambda E, i=i: E.dma_start(
                out=aTs[i][:], in_=aTd[i * 128:(i + 1) * 128, :]), [f'aa{i}'])
            ld(lambda E, i=i: E.dma_start(
                out=rTs[i][:], in_=rTd[i * 128:(i + 1) * 128, :]), [f'rr{i}'])
        for g in range(2):
            ld(lambda E, g=g: E.dma_start(
                out=bTs[g][:], in_=bTd[g * 128:(g + 1) * 128, :]), [f'bb{g}'])
        for oc in range(ND):
            ld(lambda E, oc=oc: E.dma_start(
                out=bcols[:, oc:oc + 1], in_=bcold[oc * 128:(oc + 1) * 128, :]),
                ['bcols'])
        for i in range(ND):
            for q in range(NQ):
                ld(lambda E, i=i, q=q: E.dma_start(
                    out=xb[i][:, q * QW:(q + 1) * QW],
                    in_=xin[0, i * 128:(i + 1) * 128, q * QW:(q + 1) * QW]),
                    [f'x{i}q{q}'])
        P.dma_fence('Y', 'ld',
                    ['ident', 'onesc', 'onesr', 'selL', 'selE', 'epsc',
                     'bcols', 'bb0', 'bb1']
                    + [f'wt{i}' for i in range(ND)]
                    + [f'aa{i}' for i in range(ND)]
                    + [f'rr{i}' for i in range(ND)]
                    + [f'x{i}q{q}' for i in range(ND) for q in range(NQ)])

        P.emit('V', lambda E: E.memset(stat[:, :], 0.0), (), ['stat'])
        P.emit('V', lambda E: E.memset(lnrb[:, :], 0.0), (), ['lnrb'])
        P.emit('G', lambda E: E.memset(kvt[:, :], 1.0), (), ['kvt'])
        P.emit('V', lambda E: E.memset(lrows[:, :], 1.0), (), ['lrows'])
        P.emit('V', lambda E: E.memset(gsc[:, :], 0.0), (), ['gsc'])
        P.emit('V', lambda E: E.memset(zcol[:, :], 0.0), (), ['zcol'])

        for h in range(H):
            ld(lambda E, h=h: E.dma_start(
                out=qt[0][:, :], in_=wqd[h * 256:h * 256 + 128, 0:512]),
                ['qt0h0', 'qt0h1'])
            ld(lambda E, h=h: E.dma_start(
                out=qt[1][:, :], in_=wqd[h * 256:h * 256 + 128, 512:1024]),
                ['qt1h0', 'qt1h1'])
            ld(lambda E, h=h: E.dma_start(
                out=qt[2][:, :], in_=wqd[h * 256 + 128:h * 256 + 256, 0:512]),
                ['qt2h0', 'qt2h1'])
            ld(lambda E, h=h: E.dma_start(
                out=qt[3][:, :], in_=wqd[h * 256 + 128:h * 256 + 256, 512:1024]),
                ['qt3h0', 'qt3h1'])
            ld(lambda E, h=h: E.dma_start(
                out=pbuf[0][:, 0:64], in_=wkupd[h * 256:h * 256 + 128, :]),
                ['pb0'])
            ld(lambda E, h=h: E.dma_start(
                out=pbuf[0][:, 64:128],
                in_=wkupd[h * 256 + 128:h * 256 + 256, :]), ['pb0'])
            P.dma_fence('Y', 'ld', ['pb0'] + [f'qt{k}h0' for k in range(4)]
                        + [f'qt{k}h1' for k in range(4)])
            for i in range(ND):
                bk = i % 2
                q01 = i // 4
                c0 = (i % 4) * 128
                P.emit('T', lambda E, bk=bk, q01=q01, c0=c0: E.matmul(
                    ps[bk][0:128, 0:64], qt[q01][:, c0:c0 + 128],
                    pbuf[0][:, 0:64], start=True, stop=False),
                    [f'qt{q01}h0', f'qt{q01}h1', 'pb0'], [PK[bk]])
                P.emit('T', lambda E, bk=bk, q01=q01, c0=c0: E.matmul(
                    ps[bk][0:128, 0:64], qt[2 + q01][:, c0:c0 + 128],
                    pbuf[0][:, 64:128], start=False, stop=True),
                    [f'qt{2+q01}h0', f'qt{2+q01}h1', 'pb0'], [PK[bk]])
                P.emit('V', lambda E, bk=bk, i=i, h=h: E.tensor_copy(
                    wqkT[i][:, h * 64:(h + 1) * 64], ps[bk][0:128, 0:64]),
                    [PK[bk]], [f'wqkT{i}'])
        for jp in range(NHC):
            ld(lambda E, jp=jp: E.dma_start(
                out=pbuf[1][:, 0:64], in_=wvupd[jp * 512:jp * 512 + 128, :]),
                ['pb1'])
            ld(lambda E, jp=jp: E.dma_start(
                out=pbuf[1][:, 64:128],
                in_=wvupd[jp * 512 + 128:jp * 512 + 256, :]), ['pb1'])
            ld(lambda E, jp=jp: E.dma_start(
                out=pbuf[1][:, 128:192],
                in_=wvupd[jp * 512 + 256:jp * 512 + 384, :]), ['pb1'])
            ld(lambda E, jp=jp: E.dma_start(
                out=pbuf[1][:, 192:256],
                in_=wvupd[jp * 512 + 384:jp * 512 + 512, :]), ['pb1'])
            P.dma_fence('Y', 'ld', ['pb1'])
            for ohalf in range(2):
                stg = qt if ohalf == 0 else ab
                sgk = (lambda k: [f'qt{k}h0', f'qt{k}h1']) if ohalf == 0 \
                    else (lambda k: [f'ab{k}'])
                for k2 in range(2):
                    ld(lambda E, jp=jp, k2=k2, ohalf=ohalf, stg=stg:
                       E.dma_start(
                        out=stg[k2][:, :],
                        in_=woTd[jp * 512 + k2 * 128:jp * 512 + (k2 + 1) * 128,
                                 ohalf * 512:(ohalf + 1) * 512]),
                        sgk(k2))
                    ld(lambda E, jp=jp, k2=k2, ohalf=ohalf, stg=stg:
                       E.dma_start(
                        out=stg[2 + k2][:, :],
                        in_=woTd[jp * 512 + 256 + k2 * 128:
                                 jp * 512 + 256 + (k2 + 1) * 128,
                                 ohalf * 512:(ohalf + 1) * 512]),
                        sgk(2 + k2))
                P.dma_fence('Y', 'ld',
                            sum((sgk(k) for k in range(4)), []))
                for hf in range(2):
                    bk = 4 + (2 * ohalf + hf) % 4
                    P.emit('T', lambda E, bk=bk, hf=hf, stg=stg: E.matmul(
                        ps[bk][0:64, :], pbuf[1][:, hf * 128:hf * 128 + 64],
                        stg[2 * hf][:, :], start=True, stop=False),
                        ['pb1'] + sgk(2 * hf), [PK[bk]])
                    P.emit('T', lambda E, bk=bk, hf=hf, stg=stg: E.matmul(
                        ps[bk][0:64, :],
                        pbuf[1][:, hf * 128 + 64:hf * 128 + 128],
                        stg[2 * hf + 1][:, :], start=False, stop=True),
                        ['pb1'] + sgk(2 * hf + 1), [PK[bk]])
                    if hf == 0:
                        P.emit('V', lambda E, bk=bk, jp=jp, ohalf=ohalf:
                               E.tensor_copy(
                            wovT[jp][0:64, ohalf * 512:(ohalf + 1) * 512],
                            ps[bk][0:64, :]), [PK[bk]], [f'wovT{jp}'])
                    else:
                        P.emit('V', lambda E, bk=bk: E.tensor_copy(
                            sqs[0:64, :], ps[bk][0:64, :]),
                            [PK[bk]], ['sqsA', 'sqsB'])
                        gd(lambda E, jp=jp, ohalf=ohalf: E.dma_start(
                            out=wovT[jp][64:128, ohalf * 512:(ohalf + 1) * 512],
                            in_=sqs[0:64, :]), ['sqsA', 'sqsB'],
                            [f'wovT{jp}'])
        if debug:
            st(lambda E: E.dma_start(out=dbg['d_wqk'][:, :], in_=wqkT[0][:, :]),
               ['wqkT0'])
            st(lambda E: E.dma_start(out=dbg['d_wov'][:, :], in_=wovT[0][:, :]),
               ['wovT0'])

        if debug2:
            for j in range(NHC):
                st(lambda E, j=j: E.dma_start(
                    out=dbg['pm_wov0'][:, j * 1024:(j + 1) * 1024],
                    in_=wovT[j][:, :]), [f'wovT{j}'])
        def colsum_neg(dst_fn, dst_key, lhs):
            n = len(lhs)
            for i, (lf, rkeys, kp) in enumerate(lhs):
                P.emit('T', lambda E, lf=lf, i=i, n=n, kp=kp: E.matmul(
                    ps[6][0:kp, 0:1], lf(), onesc[:, :],
                    start=(i == 0), stop=(i == n - 1)),
                    list(rkeys) + ['onesc'], [PK[6]])
            P.emit('S', lambda E, kp=lhs[0][2]: E.mul(
                dst_fn(), ps[6][0:kp, 0:1], -1.0), [PK[6]], [dst_key])

        for j in range(NHC):
            colsum_neg(lambda j=j: negws[:, j:j + 1], 'negws',
                       [(lambda i=i, j=j: wqkT[i][:, j * 128:(j + 1) * 128],
                         [f'wqkT{i}'], 128) for i in range(ND)])
        load_wkv()
        colsum_neg(lambda: negwskv[:, 0:1], 'negwskv',
                   [(lambda i=i: wkv_ap(i), [wkv_key(i)], 128)
                    for i in range(ND)])
        for oc in range(ND):
            colsum_neg(lambda oc=oc: negwsW[:, oc:oc + 1], 'negwsW',
                       [(lambda i=i, oc=oc: wTs[i][:, oc * 128:(oc + 1) * 128],
                         [f'wt{i}'], 128) for i in range(ND)])
        for g in range(2):
            colsum_neg(lambda g=g: negwsA[:, g:g + 1], 'negwsA',
                       [(lambda i=i, g=g: aTs[i][:, g * 128:(g + 1) * 128],
                         [f'aa{i}'], 128) for i in range(ND)])
        colsum_neg(lambda: negwsR[:, 0:1], 'negwsR',
                   [(lambda i=i: rTs[i][:, :], [f'rr{i}'], 8)
                    for i in range(ND)])

        def ln_stats(w, bS, bQ, store_q=None, scr=6):
            assert w == QW
            sqr = sqs[0:1, 0:w]
            msq = pbuf[1][0:1, 0:w]
            rrow = qt[0][0:1, 0:w]
            brow = qt[1][0:1, 0:w]
            rk = 'qt0h0'
            bk_ = 'qt1h0'
            P.emit('V', lambda E, w=w: E.tensor_copy(
                stat[0:1, 0:w], ps[bS][0:1, 0:w]), [PK[bS]], ['stat'])
            P.emit('V', lambda E, w=w: E.tensor_copy(
                stat[32:33, 0:w], ps[bQ][32:33, 0:w]), [PK[bQ]], ['stat'])
            P.emit('T', lambda E, w=w, scr=scr: E.matmul(
                ps[scr][0:1, 0:w], ident[0:33, 32:33], stat[0:33, 0:w],
                start=True, stop=True), ['stat', 'ident'], [PK[scr]])
            P.emit('V', lambda E, w=w, sqr=sqr, scr=scr: E.tensor_copy(
                sqr, ps[scr][0:1, 0:w]), [PK[scr]], ['sqsA', 'sqsB'])
            P.emit('S', lambda E, w=w, msq=msq: E.activation(
                msq, stat[0:1, 0:w], AF.Square, scale=INV_D),
                ['stat'], ['pb1'])
            P.emit('V', lambda E, sqr=sqr, msq=msq: E.scalar_tensor_tensor(
                msq, sqr, INV_D, msq, ALU.mult, ALU.subtract),
                [], ['pb1'], sreads=['sqsA', 'sqsB', 'pb1'])
            P.emit('S', lambda E, msq=msq: E.activation(
                msq, msq, AF.Ln, bias=epsc[0:1, 0:1]),
                ['pb1', 'epsc'], ['pb1'])
            P.emit('S', lambda E, msq=msq, rrow=rrow: E.activation(
                rrow, msq, AF.Exp, scale=-0.5),
                ['pb1'], [rk])
            P.emit('V', lambda E, rrow=rrow, brow=brow: E.scalar_tensor_tensor(
                brow, stat[0:1, 0:w], INV_D, rrow,
                ALU.mult, ALU.mult), [rk], [bk_], sreads=['stat'])
            P.emit('T', lambda E, w=w, rrow=rrow, scr=scr: E.matmul(
                ps[scr][:, 0:w], onesr[:, :], rrow, start=True, stop=True),
                [rk, 'onesr'], [PK[scr]])
            P.emit('V', lambda E, w=w, scr=scr: E.tensor_copy(
                rrep[:, 0:w], ps[scr][:, 0:w]), [PK[scr]], ['rrep'])
            P.emit('T', lambda E, w=w, brow=brow, scr=scr: E.matmul(
                ps[scr][:, 0:w], onesr[:, :], brow, start=True, stop=True),
                [bk_, 'onesr'], [PK[scr]])
            P.emit('V', lambda E, w=w, scr=scr: E.tensor_copy(
                brep[:, 0:w], ps[scr][:, 0:w]), [PK[scr]], ['brep'])
            if store_q is not None:
                gd(lambda E, q=store_q, rrow=rrow: E.dma_start(
                    out=lnrb[q:q + 1, :], in_=rrow), [rk], ['lnrb'])
                gd(lambda E, q=store_q, brow=brow: E.dma_start(
                    out=lnrb[4 + q:5 + q, :], in_=brow), [bk_], ['lnrb'])

        for sq_i in range(SPC):
            XK = lambda i, q: f'x{i}q{q}'
            load_wkv()
            for q in range(NQ):
                qs = slice(q * QW, (q + 1) * QW)
                for i in range(ND):
                    sqt = sqs if i % 2 == 0 else pbuf[1]
                    sk = ['sqsA', 'sqsB'] if i % 2 == 0 else ['pb1']
                    P.emit('S', lambda E, i=i, qs=qs, sqt=sqt: E.activation(
                        sqt[:, :], xb[i][:, qs], AF.Square),
                        [XK(i, q)], sk)
                    P.emit('T', lambda E, i=i, qs=qs: E.matmul(
                        ps[0][0:1, 0:QW], onesc[:, :], xb[i][:, qs],
                        start=(i == 0), stop=(i == ND - 1),
                        tile_position=(0, 0)),
                        [XK(i, q), 'onesc'], [PK[0]])
                    P.emit('T', lambda E, sqt=sqt, i=i: E.matmul(
                        ps[1][32:33, 0:QW], onesc[:, :], sqt[:, :],
                        start=(i == 0), stop=(i == ND - 1),
                        tile_position=(0, 32)), sk + ['onesc'], [PK[1]])
                ln_stats(QW, 0, 1, store_q=q)
                for i in range(ND):
                    P.emit('T', lambda E, i=i, qs=qs: E.matmul(
                        ps[2][:, 0:QW], wkv_ap(i), xb[i][:, qs],
                        start=(i == 0), stop=(i == ND - 1)),
                        [wkv_key(i), XK(i, q)], [PK[2]])
                P.emit('V', lambda E, qs=qs: E.tensor_tensor(
                    kvl[:, qs], ps[2][:, 0:QW], rrep[:, :], ALU.mult),
                    [PK[2], 'rrep'], [f'kvl{q}'])
                P.emit('V', lambda E, qs=qs: E.scalar_tensor_tensor(
                    kvl[:, qs], brep[:, :], negwskv[:, 0:1], kvl[:, qs],
                    ALU.mult, ALU.add),
                    [f'kvl{q}'], [f'kvl{q}'], sreads=['brep', 'negwskv'])
            KVLK = [f'kvl{q}' for q in range(NQ)]
            for cb in range(2):
                bk = 3 + cb
                for ci in range(8):
                    c = cb * 8 + ci
                    P.emit('T', lambda E, bk=bk, c=c, ci=ci: E.transpose(
                        ps[bk][:, ci * 64:(ci + 1) * 64],
                        kvl[0:64, c * 128:(c + 1) * 128], ident[0:64, 0:64]),
                        KVLK + ['ident'], [PK[bk]] if ci == 0 else [PK[bk]])
                for ci in range(8):
                    c = cb * 8 + ci
                    P.emit('V', lambda E, bk=bk, c=c, ci=ci: E.tensor_copy(
                        kvt[:, c * 65:c * 65 + 64],
                        ps[bk][:, ci * 64:(ci + 1) * 64]), [PK[bk]], ['kvt'])
            P.emit('V', lambda E: E.memset(kvm2[:, :], 0.0), (), ['kvm2'])
            P.emit('V', lambda E: E.reduce_max(
                kvm2[0:64, 0:1], kvl[0:64, :], axis=AX.X,
                apply_absolute_value=True), KVLK, ['kvm2'])
            P.emit('V', lambda E: E.reduce_max(
                kvm2[64:128, 1:2], kvl[64:128, :], axis=AX.X,
                apply_absolute_value=True), KVLK, ['kvm2'])
            if debug and sq_i == 0:
                st(lambda E: E.dma_start(out=dbg['d_kvl'][:, :], in_=kvl[:, :]),
                   KVLK)
                st(lambda E: E.dma_start(out=dbg['d_kvt'][:, :], in_=kvt[:, :]),
                   ['kvt'])
                st(lambda E: E.dma_start(out=dbg['d_lnrb'][:, :],
                                         in_=lnrb[:, :]), ['lnrb'])

            for q in range(NQ):
                qs = slice(q * QW, (q + 1) * QW)
                P.emit('T', lambda E, q=q: E.matmul(
                    ps[0][0:1, 0:QW], ident[0:8, q:q + 1], lnrb[0:8, :],
                    start=True, stop=True), ['lnrb', 'ident'], [PK[0]])
                P.emit('V', lambda E: E.tensor_copy(
                    qt[0][0:1, :], ps[0][0:1, 0:QW]), [PK[0]], ['qt0h0'])
                P.emit('T', lambda E: E.matmul(
                    ps[1][:, 0:QW], onesr[:, :], qt[0][0:1, :],
                    start=True, stop=True), ['qt0h0', 'onesr'], [PK[1]])
                P.emit('V', lambda E: E.tensor_copy(
                    rrep[:, :], ps[1][:, 0:QW]), [PK[1]], ['rrep'])
                P.emit('T', lambda E, q=q: E.matmul(
                    ps[0][0:1, 0:QW], ident[0:8, 4 + q:5 + q], lnrb[0:8, :],
                    start=True, stop=True), ['lnrb', 'ident'], [PK[0]])
                P.emit('V', lambda E: E.tensor_copy(
                    qt[0][0:1, :], ps[0][0:1, 0:QW]), [PK[0]], ['qt0h0'])
                P.emit('T', lambda E: E.matmul(
                    ps[1][:, 0:QW], onesr[:, :], qt[0][0:1, :],
                    start=True, stop=True), ['qt0h0', 'onesr'], [PK[1]])
                P.emit('V', lambda E: E.tensor_copy(
                    brep[:, :], ps[1][:, 0:QW]), [PK[1]], ['brep'])
                for j in range(NHC):
                    bk = 2 + j % 2
                    for i in range(ND):
                        P.emit('T', lambda E, bk=bk, i=i, j=j, qs=qs: E.matmul(
                            ps[bk][:, 0:QW], wqkT[i][:, j * 128:(j + 1) * 128],
                            xb[i][:, qs], start=(i == 0), stop=(i == ND - 1)),
                            [f'wqkT{i}', XK(i, q)], [PK[bk]])
                    P.emit('V', lambda E, bk=bk, j=j: E.tensor_tensor(
                        qt[j][:, :], ps[bk][:, 0:QW], rrep[:, :], ALU.mult),
                        [PK[bk], 'rrep'], [f'qt{j}h0', f'qt{j}h1'])
                    P.emit('V', lambda E, j=j: E.scalar_tensor_tensor(
                        qt[j][:, :], brep[:, :], negws[:, j:j + 1], qt[j][:, :],
                        ALU.mult, ALU.add),
                        [f'qt{j}h0', f'qt{j}h1'],
                        [f'qt{j}h0', f'qt{j}h1'], sreads=['brep', 'negws'])
                P.emit('V', lambda E: E.memset(ps[5][:, :], -1e30),
                       (), [PK[5]])
                for j in range(NHC):
                    P.emit('S', lambda E, j=j: E.activation(
                        sqs[:, :], qt[j][:, :], AF.Abs),
                        [f'qt{j}h0', f'qt{j}h1'], ['sqsA', 'sqsB'])
                    P.emit('T', lambda E, j=j: E.matmul(
                        ps[5][32 * j:32 * j + 2, 0:QW], kvm2[:, :], sqs[:, :],
                        start=True, stop=True,
                        tile_position=(0, 32 * j)),
                        ['kvm2', 'sqsA', 'sqsB'], [PK[5]])
                if debug and sq_i == 0 and q == 0:
                    P.emit('V', lambda E: E.tensor_copy(
                        pbuf[0][:, :], ps[5][:, 0:QW]), [PK[5]], ['pb0'])
                    st(lambda E: E.dma_start(out=dbg['d_psb'][:, :],
                                             in_=pbuf[0][:, :]), ['pb0'])
                    st(lambda E: E.dma_start(out=dbg['d_sqs'][:, :],
                                             in_=sqs[:, :]), ['sqsA', 'sqsB'])
                    st(lambda E: E.dma_start(out=dbg['d_kvm2'][:, :],
                                             in_=kvm2[:, :]), ['kvm2'])
                P.emit('V', lambda E: E.reduce_max(
                    gsc[:, 0:1], ps[5][:, 0:QW], axis=AX.X, negate=True),
                    [PK[5]], ['gsc'])
                P.emit('S', lambda E: E.mul(gsc[:, 1:2], gsc[:, 0:1], SCALE),
                       ['gsc'], ['gsc'])
                if debug and sq_i == 0 and q == 0:
                    st(lambda E: E.dma_start(out=dbg['d_gsc'][:, :],
                                             in_=gsc[:, :]), ['gsc'])
                for h in range(H):
                    r_h = 32 * (h // 2) + (h % 2)
                    P.emit('T', lambda E, r_h=r_h: E.matmul(
                        ps[0][0:1, 0:1], ident[0:128, r_h:r_h + 1],
                        gsc[0:128, 1:2], start=True, stop=True),
                        ['gsc', 'ident'], [PK[0]])
                    P.emit('V', lambda E, h=h: E.tensor_copy(
                        stat[0:1, 100 + h:101 + h], ps[0][0:1, 0:1]),
                        [PK[0]], ['stat'])
                    P.emit('T', lambda E, h=h: E.matmul(
                        ps[1][:, 0:1], onesr[:, :],
                        stat[0:1, 100 + h:101 + h], start=True, stop=True),
                        ['stat', 'onesr'], [PK[1]])
                    P.emit('V', lambda E, h=h: E.tensor_copy(
                        negG[:, h:h + 1], ps[1][:, 0:1]), [PK[1]], ['negG'])
                if debug and sq_i == 0 and q == 0:
                    st(lambda E: E.dma_start(out=dbg['d_ng'][:, :],
                                             in_=negG[:, :]), ['negG'])
                    for j in range(NHC):
                        st(lambda E, j=j: E.dma_start(
                            out=dbg['d_qt'][:, j * QW:(j + 1) * QW],
                            in_=qt[j][:, :]), [f'qt{j}h0', f'qt{j}h1'])

                for hp in range(NHC):
                    j = hp
                    qke, qko = f'qt{j}h0', f'qt{j}h1'
                    obe = 4 + hp % 2
                    obo = 6 + hp % 2

                    def eS(c, hf, j=j):
                        bk = (c % 2) + 2 * hf
                        qk = f'qt{j}h{hf}'
                        return P.emit('T', lambda E, bk=bk, c=c, hf=hf, j=j:
                                      E.matmul(
                            ps[bk][:, 0:QW],
                            kvl[64 * hf:64 * hf + 64, c * 128:(c + 1) * 128],
                            qt[j][64 * hf:64 * hf + 64, :],
                            start=True, stop=True,
                            tile_position=(64 * hf, 0)),
                            KVLK + [qk], [PK[bk]])

                    def eE(c, hf, hp=hp):
                        bk = (c % 2) + 2 * hf
                        pbi = (c % 2) + 2 * hf
                        h = 2 * hp + hf
                        return P.emit('S', lambda E, bk=bk, pbi=pbi, h=h:
                                      E.activation(
                            pball[pbi][:, :], ps[bk][:, 0:QW], AF.Exp,
                            bias=negG[:, h:h + 1], scale=SCALE),
                            [PK[bk], 'negG'], [f'pb{pbi}'])

                    def eO(c, hf, obe=obe, obo=obo):
                        ob = obe if hf == 0 else obo
                        pbi = (c % 2) + 2 * hf
                        return P.emit('T', lambda E, c=c, ob=ob, pbi=pbi:
                                      E.matmul(
                            ps[ob][0:65, 0:QW], kvt[:, c * 65:(c + 1) * 65],
                            pball[pbi][:, :],
                            start=(c == 0), stop=(c == NKC - 1)),
                            [f'pb{pbi}', 'kvt'], [PK[ob]])

                    for c in range(2):
                        eS(c, 0)
                        eS(c, 1)
                        eE(c, 0)
                        eE(c, 1)
                    for c in range(2, NKC):
                        eO(c - 2, 0)
                        eO(c - 2, 1)
                        eS(c, 0)
                        eS(c, 1)
                        eE(c, 0)
                        eE(c, 1)
                    for c in (NKC - 2, NKC - 1):
                        eO(c, 0)
                        eO(c, 1)
                    P.emit('V', lambda E, obe=obe, j=j: E.tensor_copy(
                        qt[j][0:64, :], ps[obe][0:64, 0:QW]),
                        [PK[obe]], [qke])
                    P.emit('V', lambda E, obe=obe: E.tensor_copy(
                        sqs[64:65, :], ps[obe][64:65, 0:QW]),
                        [PK[obe]], ['sqsA', 'sqsB'])
                    gd(lambda E, he=2 * hp: E.dma_start(
                        out=lrows[he:he + 1, :], in_=sqs[64:65, :]),
                        ['sqsA', 'sqsB'], ['lrows'])
                    P.emit('V', lambda E, obo=obo: E.tensor_copy(
                        sqs[0:64, :], ps[obo][0:64, 0:QW]),
                        [PK[obo]], ['sqsA', 'sqsB'])
                    gd(lambda E, j=j: E.dma_start(
                        out=qt[j][64:128, :], in_=sqs[0:64, :]),
                        ['sqsA', 'sqsB'], [qko])
                    P.emit('V', lambda E, obo=obo: E.tensor_copy(
                        sqs[64:65, :], ps[obo][64:65, 0:QW]),
                        [PK[obo]], ['sqsA', 'sqsB'])
                    gd(lambda E, ho=2 * hp + 1: E.dma_start(
                        out=lrows[ho:ho + 1, :], in_=sqs[64:65, :]),
                        ['sqsA', 'sqsB'], ['lrows'])
                P.emit('V', lambda E: E.reciprocal(rl[0:8, :], lrows[0:8, :]),
                       ['lrows'], ['rl'])
                for j in range(NHC):
                    P.emit('T', lambda E, j=j: E.matmul(
                        ps[4][:, 0:QW], selL[:, j * 128:(j + 1) * 128],
                        rl[0:8, :], start=True, stop=True),
                        ['rl', 'selL'], [PK[4]])
                    P.emit('V', lambda E, j=j: E.tensor_tensor(
                        qt[j][:, :], qt[j][:, :], ps[4][:, 0:QW], ALU.mult),
                        [PK[4], f'qt{j}h0', f'qt{j}h1'],
                        [f'qt{j}h0', f'qt{j}h1'])
                if debug and sq_i == 0 and q == 0:
                    st(lambda E: E.dma_start(out=dbg['d_lr'][:, :],
                                             in_=lrows[0:8, :]), ['lrows'])
                    for j in range(NHC):
                        st(lambda E, j=j: E.dma_start(
                            out=dbg['d_on'][:, j * QW:(j + 1) * QW],
                            in_=qt[j][:, :]), [f'qt{j}h0', f'qt{j}h1'])

                cs = slice(q * QW, (q + 1) * QW)
                for oc in range(ND):
                    bk = 5 + oc % 2
                    sqt = sqs if oc % 2 == 0 else pbuf[1]
                    sk = ['sqsA', 'sqsB'] if oc % 2 == 0 else ['pb1']
                    for j in range(NHC):
                        P.emit('T', lambda E, bk=bk, j=j, oc=oc: E.matmul(
                            ps[bk][:, 0:QW],
                            wovT[j][:, oc * 128:(oc + 1) * 128],
                            qt[j][:, :], start=(j == 0),
                            stop=(j == NHC - 1)),
                            [f'wovT{j}', f'qt{j}h0', f'qt{j}h1'],
                            [PK[bk]])
                    P.emit('S', lambda E, bk=bk, oc=oc: E.activation(
                        ab[oc][:, :], ps[bk][:, 0:QW], AF.Identity),
                        [PK[bk]], [f'ab{oc}'])
                    P.emit('S', lambda E, bk=bk, sqt=sqt: E.activation(
                        sqt[:, :], ps[bk][:, 0:QW], AF.Square),
                        [PK[bk]], sk)
                    P.emit('T', lambda E, oc=oc: E.matmul(
                        ps[0][0:1, 0:QW], onesc[:, :], ab[oc][:, :],
                        start=(oc == 0), stop=(oc == ND - 1),
                        tile_position=(0, 0)),
                        [f'ab{oc}', 'onesc'], [PK[0]])
                    P.emit('T', lambda E, sqt=sqt, oc=oc: E.matmul(
                        ps[1][32:33, 0:QW], onesc[:, :], sqt[:, :],
                        start=(oc == 0), stop=(oc == ND - 1),
                        tile_position=(0, 32)), sk + ['onesc'],
                        [PK[1]])
                ln_stats(QW, 0, 1)
                for i in range(ND):
                    P.emit('V', lambda E, i=i: E.tensor_tensor(
                        ab[i][:, :], ab[i][:, :], rrep[:, :], ALU.mult),
                        ['rrep', f'ab{i}'], [f'ab{i}'])
                    P.emit('V', lambda E, i=i: E.tensor_tensor(
                        ab[i][:, :], ab[i][:, :], brep[:, :],
                        ALU.subtract), ['brep', f'ab{i}'], [f'ab{i}'])
                    P.emit('V', lambda E, i=i, cs=cs: E.tensor_tensor(
                        xb[i][:, cs], xb[i][:, cs], ab[i][:, :], ALU.add),
                        [f'ab{i}', XK(i, q)], [XK(i, q)])
                if debug and sq_i == 0 and q == 0:
                    for oc in range(ND):
                        st(lambda E, oc=oc: E.dma_start(
                            out=dbg['d_ab'][:, oc * HW:(oc + 1) * HW],
                            in_=ab[oc][:, 0:HW]), [f'ab{oc}'])

            if debug and sq_i == 0:
                st(lambda E: E.dma_start(out=dbg['d_xa'][:, :],
                                         in_=xb[0][:, :]),
                   [XK(0, qq) for qq in range(NQ)])
            for q in range(NQ):
                if True:
                    cs = slice(q * QW, (q + 1) * QW)
                    for i in range(ND):
                        sqt = sqs if i % 2 == 0 else pbuf[1]
                        sk = ['sqsA', 'sqsB'] if i % 2 == 0 else ['pb1']
                        P.emit('S', lambda E, i=i, cs=cs, sqt=sqt: E.activation(
                            sqt[:, :], xb[i][:, cs], AF.Square),
                            [XK(i, q)], sk)
                        P.emit('T', lambda E, i=i, cs=cs: E.matmul(
                            ps[0][0:1, 0:QW], onesc[:, :], xb[i][:, cs],
                            start=(i == 0), stop=(i == ND - 1),
                            tile_position=(0, 0)),
                            [XK(i, q), 'onesc'], [PK[0]])
                        P.emit('T', lambda E, sqt=sqt, i=i: E.matmul(
                            ps[1][32:33, 0:QW], onesc[:, :], sqt[:, :],
                            start=(i == 0), stop=(i == ND - 1),
                            tile_position=(0, 32)), sk + ['onesc'], [PK[1]])
                    for i in range(ND):
                        P.emit('T', lambda E, i=i, cs=cs: E.matmul(
                            ps[2][0:8, 0:QW], rTs[i][:, :], xb[i][:, cs],
                            start=(i == 0), stop=(i == ND - 1)),
                            [f'rr{i}', XK(i, q)], [PK[2]])
                    for g in range(2):
                        hb = 6 + g
                        for i in range(ND):
                            P.emit('T', lambda E, i=i, g=g, hb=hb, cs=cs:
                                   E.matmul(
                                ps[hb][:, 0:QW],
                                aTs[i][:, g * 128:(g + 1) * 128],
                                xb[i][:, cs], start=(i == 0),
                                stop=(i == ND - 1)),
                                [f'aa{i}', XK(i, q)], [PK[hb]])
                    for oc in range(2):
                        bk = 4 + oc
                        for i in range(ND):
                            P.emit('T', lambda E, bk=bk, i=i, oc=oc, cs=cs:
                                   E.matmul(
                                ps[bk][:, 0:QW],
                                wTs[i][:, oc * 128:(oc + 1) * 128],
                                xb[i][:, cs], start=(i == 0),
                                stop=(i == ND - 1)),
                                [f'wt{i}', XK(i, q)], [PK[bk]])
                    ln_stats(QW, 0, 1, scr=0)
                    P.emit('V', lambda E: E.tensor_tensor(
                        gate8[0:8, :], ps[2][0:8, 0:QW], rrep[0:8, :],
                        ALU.mult), [PK[2], 'rrep'], ['rl'])
                    P.emit('V', lambda E: E.scalar_tensor_tensor(
                        gate8[0:8, :], brep[0:8, :], negwsR[0:8, 0:1],
                        gate8[0:8, :], ALU.mult, ALU.add),
                        ['rl'], ['rl'], sreads=['brep', 'negwsR'])
                    for tb4 in range(4):
                        P.emit('T', lambda E, tb4=tb4: E.transpose(
                            ps[3][:, tb4 * 8:(tb4 + 1) * 8],
                            gate8[0:8, tb4 * 128:(tb4 + 1) * 128],
                            ident[0:8, 0:8]),
                            ['rl', 'ident'], [PK[3]])
                    P.emit('V', lambda E: E.tensor_copy(
                        ltk[:, :], ps[3][:, 0:32]), [PK[3]], ['ltk'])
                    if debug and sq_i == 0 and q == 0:
                        st(lambda E: E.dma_start(out=dbg['d_ltk'][:, :],
                                                 in_=ltk[:, 0:16]), ['ltk'])
                        st(lambda E: E.dma_start(out=dbg['d_rr'][:, :],
                                                 in_=rrep[:, :]), ['rrep'])
                        st(lambda E: E.dma_start(out=dbg['d_st'][:, :],
                                                 in_=stat[:, :]), ['stat'])
                    for tb in range(4):
                        tsl = slice(tb * 8, (tb + 1) * 8)
                        P.emit('V', lambda E, tsl=tsl: E.reduce_max(
                            m1t[:, 0:1], ltk[:, tsl], axis=AX.X),
                            ['ltk'], ['m1t'])
                        P.emit('V', lambda E, tsl=tsl: E.reduce_max(
                            nm1t[:, 0:1], ltk[:, tsl], axis=AX.X, negate=True),
                            ['ltk'], ['nm1t'])
                        P.emit('S', lambda E, tsl=tsl: E.activation(
                            geg[:, :], ltk[:, tsl], AF.Exp, bias=nm1t[:, 0:1]),
                            ['ltk', 'nm1t'], ['geg'])
                        P.emit('V', lambda E: E.reduce_sum(
                            lsum[:, 0:1], geg[:, :], axis=AX.X),
                            ['geg'], ['lsum'])
                        P.emit('V', lambda E, tsl=tsl: E.tensor_scalar(
                            gt2[:, :], ltk[:, tsl], m1t[:, 0:1], None,
                            ALU.subtract), ['ltk'], ['gt2'], sreads=['m1t'])
                        P.emit('V', lambda E: E.tensor_scalar(
                            gmsk[:, :], gt2[:, :], zcol[:, 0:1], None,
                            ALU.is_ge), ['gt2'], ['gmsk'], sreads=['zcol'])
                        P.emit('V', lambda E, tsl=tsl: E.scalar_tensor_tensor(
                            gt2[:, :], gmsk[:, :], -1e30, ltk[:, tsl],
                            ALU.mult, ALU.add), ['ltk'], ['gt2'],
                            sreads=['gmsk'])
                        P.emit('V', lambda E: E.reduce_max(
                            m2t[:, 0:1], gt2[:, :], axis=AX.X),
                            ['gt2'], ['m2t'])
                        P.emit('V', lambda E, tsl=tsl: E.tensor_scalar(
                            gmsk[:, :], ltk[:, tsl], m2t[:, 0:1], None,
                            ALU.subtract), ['ltk'], ['gmsk'], sreads=['m2t'])
                        P.emit('V', lambda E: E.tensor_scalar(
                            gmsk[:, :], gmsk[:, :], zcol[:, 0:1], None,
                            ALU.is_ge), ['gmsk'], ['gmsk'], sreads=['zcol'])
                        P.emit('V', lambda E: E.tensor_tensor(
                            gem[:, :], geg[:, :], gmsk[:, :], ALU.mult),
                            ['geg', 'gmsk'], ['gem'])
                        P.emit('V', lambda E: E.reduce_sum(
                            s2t[:, 0:1], gem[:, :], axis=AX.X),
                            ['gem'], ['s2t'])
                        P.emit('S', lambda E: E.mul(
                            dent[:, 0:1], lsum[:, 0:1], GEPS),
                            ['lsum'], ['dent'])
                        P.emit('V', lambda E: E.tensor_tensor(
                            dent[:, 0:1], dent[:, 0:1], s2t[:, 0:1], ALU.add),
                            ['dent', 's2t'], ['dent'])
                        P.emit('V', lambda E: E.reciprocal(
                            rdent[:, 0:1], dent[:, 0:1]), ['dent'], ['rdent'])
                        P.emit('V', lambda E, tsl=tsl: E.tensor_scalar(
                            gfin[:, tsl], gem[:, :], rdent[:, 0:1], None,
                            ALU.mult), ['gem'], ['gfin'], sreads=['rdent'])
                        if debug and sq_i == 0 and q == 0 and tb == 0:
                            st(lambda E: E.dma_start(out=dbg['d_geg'][:, :],
                                                     in_=geg[:, :]), ['geg'])
                            st(lambda E: E.dma_start(out=dbg['d_gt2'][:, :],
                                                     in_=gt2[:, :]), ['gt2'])
                            st(lambda E: E.dma_start(out=dbg['d_gmsk'][:, :],
                                                     in_=gmsk[:, :]), ['gmsk'])
                            st(lambda E: E.dma_start(out=dbg['d_gem'][:, :],
                                                     in_=gem[:, :]), ['gem'])
                            st(lambda E: E.dma_start(out=dbg['d_gsc2'][:, :],
                                                     in_=gsc[:, :]), ['gsc'])
                    for tb4 in range(4):
                        P.emit('T', lambda E, tb4=tb4: E.transpose(
                            ps[2][0:8, tb4 * 128:(tb4 + 1) * 128],
                            gfin[:, tb4 * 8:(tb4 + 1) * 8], ident[:, :]),
                            ['gfin', 'ident'], [PK[2]])
                    P.emit('V', lambda E: E.tensor_copy(
                        gate8[0:8, :], ps[2][0:8, 0:QW]), [PK[2]], ['rl'])
                    if debug and sq_i == 0 and q == 0:
                        st(lambda E: E.dma_start(out=dbg['d_gf'][:, :],
                                                 in_=gfin[:, 0:16]), ['gfin'])
                    for g in range(2):
                        gb = 2 + g
                        hb = 6 + g
                        P.emit('T', lambda E, g=g, gb=gb: E.matmul(
                            ps[gb][:, 0:QW], selE[:, g * 128:(g + 1) * 128],
                            gate8[0:8, :], start=True, stop=True),
                            ['rl', 'selE'], [PK[gb]])
                        P.emit('V', lambda E, g=g, hb=hb: E.tensor_tensor(
                            hsb[g][:, :], ps[hb][:, 0:QW], rrep[:, :],
                            ALU.mult), [PK[hb], 'rrep'], [HSK[g]])
                        P.emit('V', lambda E, g=g: E.scalar_tensor_tensor(
                            hsb[g][:, :], brep[:, :], negwsA[:, g:g + 1],
                            hsb[g][:, :], ALU.mult, ALU.add),
                            [HSK[g]], [HSK[g]], sreads=['brep', 'negwsA'])
                        P.emit('V', lambda E, g=g, gb=gb: E.tensor_tensor(
                            hsb[g][:, :], hsb[g][:, :], ps[gb][:, 0:QW],
                            ALU.mult), [PK[gb], HSK[g]], [HSK[g]])
                    if debug and sq_i == 0 and q == 0:
                        for g in range(2):
                            st(lambda E, g=g: E.dma_start(
                                out=dbg['d_hs'][:, g * HW:(g + 1) * HW],
                                in_=hsb[g][:, 0:HW]), [HSK[g]])
                    for oc in range(ND):
                        bk = 4 + oc if oc < 2 else oc % 2
                        sqt = sqs if oc % 2 == 0 else pbuf[1]
                        sk = ['sqsA', 'sqsB'] if oc % 2 == 0 else ['pb1']
                        if oc >= 2:
                            for i in range(ND):
                                P.emit('T', lambda E, bk=bk, i=i, oc=oc, cs=cs:
                                       E.matmul(
                                    ps[bk][:, 0:QW],
                                    wTs[i][:, oc * 128:(oc + 1) * 128],
                                    xb[i][:, cs], start=(i == 0),
                                    stop=(i == ND - 1)),
                                    [f'wt{i}', XK(i, q)], [PK[bk]])
                        P.emit('T', lambda E, oc=oc: E.matmul(
                            ps[7][:, 0:QW], bTs[0][:, oc * 128:(oc + 1) * 128],
                            hsb[0][:, :], start=True, stop=False),
                            ['bb0', 'pb2'], [PK[7]])
                        P.emit('T', lambda E, oc=oc: E.matmul(
                            ps[7][:, 0:QW], bTs[1][:, oc * 128:(oc + 1) * 128],
                            hsb[1][:, :], start=False, stop=True),
                            ['bb1', 'pb3'], [PK[7]])
                        P.emit('V', lambda E, bk=bk, oc=oc: E.tensor_tensor(
                            ab[oc][:, :], ps[bk][:, 0:QW], rrep[:, :],
                            ALU.mult), [PK[bk], 'rrep'], [f'ab{oc}'])
                        P.emit('V', lambda E, oc=oc: E.scalar_tensor_tensor(
                            ab[oc][:, :], brep[:, :], negwsW[:, oc:oc + 1],
                            ab[oc][:, :], ALU.mult, ALU.add),
                            [f'ab{oc}'], [f'ab{oc}'], sreads=['brep', 'negwsW'])
                        P.emit('V', lambda E, oc=oc: E.tensor_tensor(
                            ab[oc][:, :], ab[oc][:, :], ps[7][:, 0:QW],
                            ALU.add), [PK[7], f'ab{oc}'], [f'ab{oc}'])
                        P.emit('S', lambda E, oc=oc: E.activation(
                            ab[oc][:, :], ab[oc][:, :], AF.Identity,
                            bias=bcols[:, oc:oc + 1]),
                            ['bcols', f'ab{oc}'], [f'ab{oc}'])
                        P.emit('S', lambda E, oc=oc, sqt=sqt: E.activation(
                            sqt[:, :], ab[oc][:, :], AF.Square),
                            [f'ab{oc}'], sk)
                        P.emit('T', lambda E, oc=oc: E.matmul(
                            ps[2][0:1, 0:QW], onesc[:, :], ab[oc][:, :],
                            start=(oc == 0), stop=(oc == ND - 1),
                            tile_position=(0, 0)),
                            [f'ab{oc}', 'onesc'], [PK[2]])
                        P.emit('T', lambda E, oc=oc, sqt=sqt: E.matmul(
                            ps[3][32:33, 0:QW], onesc[:, :], sqt[:, :],
                            start=(oc == 0), stop=(oc == ND - 1),
                            tile_position=(0, 32)), sk + ['onesc'],
                            [PK[3]])
                    ln_stats(QW, 2, 3)
                    for i in range(ND):
                        P.emit('V', lambda E, i=i: E.tensor_tensor(
                            ab[i][:, :], ab[i][:, :], rrep[:, :], ALU.mult),
                            ['rrep', f'ab{i}'], [f'ab{i}'])
                        P.emit('V', lambda E, i=i: E.tensor_tensor(
                            ab[i][:, :], ab[i][:, :], brep[:, :],
                            ALU.subtract), ['brep', f'ab{i}'], [f'ab{i}'])
                        P.emit('V', lambda E, i=i, cs=cs: E.tensor_tensor(
                            xb[i][:, cs], xb[i][:, cs], ab[i][:, :], ALU.add),
                            [f'ab{i}', XK(i, q)], [XK(i, q)])
                        st(lambda E, i=i, cs=cs, sq_i=sq_i: E.dma_start(
                            out=xout[sq_i, i * 128:(i + 1) * 128, cs],
                            in_=xb[i][:, cs]), [XK(i, q)])
                    P.dma_fence_r('Y', 'ot',
                                  [XK(i, q) for i in range(ND)])
                    if sq_i + 1 < SPC:
                        for i in range(ND):
                            ld(lambda E, i=i, q=q, sq_i=sq_i: E.dma_start(
                                out=xb[i][:, q * QW:(q + 1) * QW],
                                in_=xin[sq_i + 1, i * 128:(i + 1) * 128,
                                        q * QW:(q + 1) * QW]), [XK(i, q)])
                        P.dma_fence('Y', 'ld', [XK(i, q) for i in range(ND)])

        if debug2:
            st(lambda E: E.dma_start(out=dbg['pm_ws'][:, 0:4], in_=negws[:, :], allow_slow_non_contiguous=True),
               ['negws'])
            st(lambda E: E.dma_start(out=dbg['pm_ws'][:, 4:5],
                                     in_=negwskv[:, :], allow_slow_non_contiguous=True), ['negwskv'])
            st(lambda E: E.dma_start(out=dbg['pm_ws'][:, 5:13],
                                     in_=negwsW[:, :], allow_slow_non_contiguous=True), ['negwsW'])
            st(lambda E: E.dma_start(out=dbg['pm_ws'][:, 13:15],
                                     in_=negwsA[:, :], allow_slow_non_contiguous=True), ['negwsA'])
            for i in range(ND):
                st(lambda E, i=i: E.dma_start(
                    out=dbg['pm_wqk'][:, i * 512:(i + 1) * 512],
                    in_=wqkT[i][:, :]), [f'wqkT{i}'])
            for j in range(NHC):
                st(lambda E, j=j: E.dma_start(
                    out=dbg['pm_wov'][:, j * 1024:(j + 1) * 1024],
                    in_=wovT[j][:, :]), [f'wovT{j}'])
            st(lambda E: E.dma_start(out=dbg['pm_kvl'][:, :], in_=kvl[:, :]),
               [f'kvl{qq}' for qq in range(NQ)])
            st(lambda E: E.dma_start(out=dbg['pm_lnrb'][:, :], in_=lnrb[:, :]),
               ['lnrb'])
        P.items.append(('Y', tuple([('ot', P.val.get('ot', 0)),
                                    ('gd', P.val.get('gd', 0))]), None, None, 0))

        def walker(engname):
            def go(E):
                waited = {}
                for (en_, waits, fn, sem, step) in P.items:
                    if en_ != engname:
                        continue
                    for (sn, vv) in waits:
                        if waited.get(sn, -1) < vv:
                            E.wait_ge(sems[sn], vv)
                            waited[sn] = vv
                    if fn is None:
                        continue
                    ins = fn(E)
                    ins.then_inc(sems[sem], step)
            return go

        @block.sync
        def _(E):
            walker('Y')(E)

        @block.tensor
        def _(E):
            walker('T')(E)

        @block.vector
        def _(E):
            walker('V')(E)

        @block.scalar
        def _(E):
            walker('S')(E)

        @block.gpsimd
        def _(E):
            walker('G')(E)

    return nc


def _host_prep(L0_attn_x, L0_attn_Wq, L0_attn_Wkv_down, L0_attn_Wk_up,
               L0_attn_Wv_up, L0_attn_Wo, L0_ffn_W, L0_ffn_b, L0_ffn_R,
               L0_ffn_A, L0_ffn_B):
    f = np.float32
    x = np.ascontiguousarray(np.asarray(L0_attn_x, f).transpose(0, 2, 1))
    wq = np.ascontiguousarray(np.asarray(L0_attn_Wq, f))
    wkup = np.ascontiguousarray(np.asarray(L0_attn_Wk_up, f))
    wvup = np.ascontiguousarray(np.asarray(L0_attn_Wv_up, f))
    woT = np.ascontiguousarray(np.asarray(L0_attn_Wo, f).T)
    wkvT = np.asarray(L0_attn_Wkv_down, f).T
    wkvT2 = np.ascontiguousarray(np.concatenate([wkvT, wkvT], axis=1))
    wT = np.ascontiguousarray(np.asarray(L0_ffn_W, f).T)
    rT = np.ascontiguousarray(np.asarray(L0_ffn_R, f).T)
    A = np.asarray(L0_ffn_A, f)
    aT = np.zeros((1024, 256), f)
    for ex in range(8):
        g, ei = divmod(ex, 4)
        aT[:, g * 128 + ei * 32: g * 128 + ei * 32 + 16] = A[ex].T
    Bw = np.asarray(L0_ffn_B, f)
    bT = np.zeros((256, 1024), f)
    for ex in range(8):
        g, ei = divmod(ex, 4)
        bT[g * 128 + ei * 32: g * 128 + ei * 32 + 16, :] = Bw[ex].T
    bcol = np.ascontiguousarray(np.asarray(L0_ffn_b, f).reshape(1024, 1))
    identc = np.eye(128, dtype=f)
    onesc = np.ones((128, 1), f)
    onesr = np.ones((1, 128), f)
    selL = np.zeros((8, 512), f)
    for j in range(4):
        selL[2 * j, j * 128:j * 128 + 64] = 1.0
        selL[2 * j + 1, j * 128 + 64:j * 128 + 128] = 1.0
    selE = np.zeros((8, 256), f)
    for g in range(2):
        for ex in range(4):
            selE[g * 4 + ex, g * 128 + ex * 32:g * 128 + (ex + 1) * 32] = 1.0
    base = dict(wq=wq, wkup=wkup, wvup=wvup, woT=woT, wkvT2=wkvT2, wTd=wT,
                rTd=rT, aTd=aT, bTd=bT, bcold=bcol, identd=identc,
                onescd=onesc, onesrd=onesr, selLd=selL, selEd=selE,
                epsd=np.full((128, 1), LNEPS, f))
    in_maps = []
    for c in range(NCORES):
        m = dict(base)
        m["xin"] = np.ascontiguousarray(x[c * SPC:(c + 1) * SPC])
        in_maps.append(m)
    return in_maps


def kernel(**inputs):
    debug = bool(inputs.pop("_debug", False))
    debug2 = bool(inputs.pop("_debug2", False))
    in_maps = _host_prep(**inputs)
    nc = build(debug=debug, debug2=debug2)
    res = run_bass_kernel_spmd(nc, in_maps, core_ids=list(range(NCORES)))
    outs = []
    for c in range(NCORES):
        xo = np.asarray(res.results[c]["xout"])
        outs.append(xo.transpose(0, 2, 1))
    full = np.ascontiguousarray(np.concatenate(outs, axis=0).astype(np.float32))
    if debug or debug2:
        kernel._dbg = res.results
    return full


# version 09
# speedup vs baseline: 1.1427x; 1.0334x over previous
import numpy as np
from contextlib import ExitStack
import concourse.bass as bass
import concourse.mybir as mybir
from concourse.bass_utils import run_bass_kernel_spmd

F32 = mybir.dt.float32
AF = mybir.ActivationFunctionType
ALU = mybir.AluOpType
AX = mybir.AxisListType

NCORES = 8
T, D = 2048, 1024
SPC = 4
NQ = 4
QW = 512
HW = 256
ND = 8
NHC = 4
NKC = 16
H = 8
SCALE = 0.0625
LNEPS = 1e-5
GEPS = 1e-6
INV_D = 1.0 / 1024.0

ESEM = {'T': 't', 'V': 'v', 'S': 's', 'G': 'g'}


class Prog:

    def __init__(self):
        self.items = []
        self.val = {}
        self.w = {}
        self.r = {}

    def emit(self, eng, fn, reads=(), writes=(), extra=(), dma=None,
             sreads=()):
        waits = {}

        def add(mk, force=False):
            if mk is None:
                return
            sn, vv = mk
            if not force and eng in ESEM and sn == ESEM[eng]:
                return
            if waits.get(sn, -1) < vv:
                waits[sn] = vv

        force_all = eng in ('V', 'S')
        for k in reads:
            add(self.w.get(k), force=force_all)
        for k in sreads:
            add(self.w.get(k), force=True)
        for k in writes:
            add(self.w.get(k))
            for sn, vv in self.r.get(k, {}).items():
                add((sn, vv))
        for mk in extra:
            add(mk)
        if dma is not None:
            sem, step = dma, 16
        else:
            sem, step = ESEM[eng], 1
        self.items.append((eng, tuple(waits.items()), fn, sem, step))
        self.val[sem] = self.val.get(sem, 0) + step
        mk = (sem, self.val[sem])
        for k in list(reads) + list(sreads):
            d = self.r.setdefault(k, {})
            if d.get(mk[0], -1) < mk[1]:
                d[mk[0]] = mk[1]
        for k in writes:
            self.w[k] = mk
            self.r[k] = {}
        return mk

    def barrier_mark(self):
        return [(sn, vv) for sn, vv in self.val.items()]

    def dma_fence(self, eng, sem, keys=()):
        v = self.val.get(sem, 0)
        self.items.append((eng, ((sem, v),), None, None, 0))
        for k in keys:
            self.w[k] = (sem, v)
            self.r[k] = {}

    def dma_fence_r(self, eng, sem, rkeys=()):
        v = self.val.get(sem, 0)
        self.items.append((eng, ((sem, v),), None, None, 0))
        for k in rkeys:
            self.r[k] = {sem: v}


def build(debug=False, debug2=False):
    nc = bass.Bass("TRN2", target_bir_lowering=False, debug=False)

    xin = nc.dram_tensor("xin", [SPC, D, T], F32, kind="ExternalInput").ap()
    wqd = nc.dram_tensor("wq", [2048, 1024], F32, kind="ExternalInput").ap()
    wkupd = nc.dram_tensor("wkup", [2048, 64], F32, kind="ExternalInput").ap()
    wvupd = nc.dram_tensor("wvup", [2048, 64], F32, kind="ExternalInput").ap()
    woTd = nc.dram_tensor("woT", [2048, 1024], F32, kind="ExternalInput").ap()
    wkvT2d = nc.dram_tensor("wkvT2", [1024, 128], F32, kind="ExternalInput").ap()
    wTd = nc.dram_tensor("wTd", [1024, 1024], F32, kind="ExternalInput").ap()
    rTd = nc.dram_tensor("rTd", [1024, 8], F32, kind="ExternalInput").ap()
    aTd = nc.dram_tensor("aTd", [1024, 256], F32, kind="ExternalInput").ap()
    bTd = nc.dram_tensor("bTd", [256, 1024], F32, kind="ExternalInput").ap()
    bcold = nc.dram_tensor("bcold", [1024, 1], F32, kind="ExternalInput").ap()
    identd = nc.dram_tensor("identd", [128, 128], F32, kind="ExternalInput").ap()
    onescd = nc.dram_tensor("onescd", [128, 1], F32, kind="ExternalInput").ap()
    onesrd = nc.dram_tensor("onesrd", [1, 128], F32, kind="ExternalInput").ap()
    selLd = nc.dram_tensor("selLd", [8, 512], F32, kind="ExternalInput").ap()
    selEd = nc.dram_tensor("selEd", [8, 256], F32, kind="ExternalInput").ap()
    epsd = nc.dram_tensor("epsd", [128, 1], F32, kind="ExternalInput").ap()
    xout = nc.dram_tensor("xout", [SPC, D, T], F32, kind="ExternalOutput").ap()
    dbg = {}
    if debug2:
        for nm, shp in [("pm_ws", [128, 16]), ("pm_wqk", [128, 4096]),
                        ("pm_wov", [128, 4096]), ("pm_kvl", [128, 2048]),
                        ("pm_wov0", [128, 4096]), ("pm_lnrb", [8, 512])]:
            dbg[nm] = nc.dram_tensor(nm, shp, F32, kind="ExternalOutput").ap()
    if debug:
        for nm, shp in [("d_kvl", [128, T]), ("d_kvt", [128, NKC * 65]),
                        ("d_qt", [128, 4 * QW]), ("d_lr", [8, QW]),
                        ("d_ng", [128, 8]), ("d_on", [128, 4 * QW]),
                        ("d_ab", [128, 8 * HW]), ("d_lnrb", [8, 512]),
                        ("d_gf", [128, 16]), ("d_hs", [128, 2 * HW]),
                        ("d_wqk", [128, 512]), ("d_wov", [128, 1024]),
                        ("d_kvm2", [128, 2]), ("d_gsc", [128, 8]),
                        ("d_kvm_a", [128, 1]), ("d_kvm_b", [128, 1]),
                        ("d_xa", [128, T]), ("d_ltk", [128, 16]),
                        ("d_geg", [128, 8]), ("d_gt2", [128, 8]),
                        ("d_gmsk", [128, 8]), ("d_gem", [128, 8]),
                        ("d_gsc2", [128, 8]),
                        ("d_rr", [128, 512]), ("d_st", [34, 512]),
                        ("d_psb", [128, 512]), ("d_sqs", [128, 512])]:
            dbg[nm] = nc.dram_tensor(nm, shp, F32, kind="ExternalOutput").ap()

    P = Prog()

    with ExitStack() as stk:
        en = stk.enter_context

        def sb(name, shape):
            return en(nc.sbuf_tensor(name, shape, F32))

        xb = [sb(f"x{i}", [128, T]) for i in range(ND)]
        wqkT = [sb(f"wqk{i}", [128, 512]) for i in range(ND)]
        wovT = [sb(f"wov{j}", [128, 1024]) for j in range(NHC)]
        wTs = [sb(f"wt{i}", [128, 1024]) for i in range(ND)]
        aTs = [sb(f"aa{i}", [128, 256]) for i in range(ND)]
        bTs = [sb(f"bb{g}", [128, 1024]) for g in range(2)]
        rTs = [sb(f"rr{i}", [128, 8]) for i in range(ND)]
        bcols = sb("bcols", [128, 8])
        negws = sb("negws", [128, 4])
        negwskv = sb("negwskv", [128, 1])
        negwsW = sb("negwsw", [128, 8])
        negwsA = sb("negwsa", [128, 2])
        negwsR = sb("negwsr", [8, 1])
        kvl = sb("kvl", [128, T])
        kvt = sb("kvt", [128, NKC * 65])
        lnrb = sb("lnrb", [8, 512])
        ident = sb("ident", [128, 128])
        onesc = sb("onesc", [128, 1])
        onesr = sb("onesr", [1, 128])
        selL = sb("selL", [8, 512])
        selE = sb("selE", [8, 256])
        negG = sb("negg", [128, 8])
        kvm = sb("kvm", [128, 1])
        kvm2 = sb("kvm2", [128, 2])
        epsc = sb("epsc", [128, 1])
        qt = [sb(f"q{j}", [128, 512]) for j in range(NHC)]
        pbuf = [sb(f"p{k}", [128, 512]) for k in range(2)]
        pbuf2 = [sb(f"pp{k}", [128, 512]) for k in range(2)]
        pball = [pbuf[0], pbuf[1], pbuf2[0], pbuf2[1]]
        sqs = sb("sqs", [128, 512])
        rrep = sb("rrep", [128, 512])
        brep = sb("brep", [128, 512])
        ab = [sb(f"ab{i}", [128, 512]) for i in range(ND)]
        stat = sb("stat", [34, 512])
        lrows = sb("lrows", [8, 512])
        rl = sb("rl", [8, 512])
        gate8 = rl
        hsb = [pbuf2[0], pbuf2[1]]
        HSK = ['pb2', 'pb3']
        ltk = sb("ltk", [128, 32])
        geg = sb("geg", [128, 8])
        gt2 = sb("gt2", [128, 8])
        gmsk = sb("gmsk", [128, 8])
        gem = sb("gem", [128, 8])
        gsc = sb("gsc", [128, 8])
        gfin = sb("gfin", [128, 32])
        m1t = sb("m1t", [128, 1])
        nm1t = sb("nm1t", [128, 1])
        m2t = sb("m2t", [128, 1])
        lsum = sb("lsum", [128, 1])
        s2t = sb("s2t", [128, 1])
        dent = sb("dent", [128, 1])
        rdent = sb("rdent", [128, 1])
        zcol = sb("zcol", [128, 1])

        ps = [en(nc.psum_tensor(f"ps{k}", [128, 512], F32)) for k in range(8)]
        PK = [f"ps{k}" for k in range(8)]

        sems = {}
        for sn in ('t', 'v', 's', 'g', 'ld', 'ot', 'gd'):
            sems[sn] = en(nc.semaphore(sn))
        block = en(nc.Block())

        def ld(fn, writes):
            return P.emit('Y', fn, (), writes, dma='ld')

        def st(fn, reads):
            return P.emit('Y', fn, reads, (), dma='ot')

        def gd(fn, reads=(), writes=()):
            mk = P.emit('G', fn, reads, writes, dma='gd')
            P.items.append(('G', ((mk[0], mk[1]),), None, None, 0))
            return mk

        ld(lambda E: E.dma_start(out=ident[:], in_=identd[:, :]), ['ident'])
        ld(lambda E: E.dma_start(out=onesc[:], in_=onescd[:, :]), ['onesc'])
        ld(lambda E: E.dma_start(out=onesr[:], in_=onesrd[:, :]), ['onesr'])
        ld(lambda E: E.dma_start(out=selL[:], in_=selLd[:, :]), ['selL'])
        ld(lambda E: E.dma_start(out=selE[:], in_=selEd[:, :]), ['selE'])
        ld(lambda E: E.dma_start(out=epsc[:], in_=epsd[:, :]), ['epsc'])
        def wkv_ap(i):
            return (pbuf[0] if i < 4 else pbuf2[0])[:, (i % 4) * 128:
                                                    (i % 4) * 128 + 128]

        def wkv_key(i):
            return 'pb0' if i < 4 else 'pb2'

        def load_wkv():
            for i in range(ND):
                ld(lambda E, i=i: E.dma_start(
                    out=wkv_ap(i), in_=wkvT2d[i * 128:(i + 1) * 128, :]),
                    [wkv_key(i)])
            P.dma_fence('Y', 'ld', ['pb0', 'pb2'])
        load_wkv()
        for i in range(ND):
            ld(lambda E, i=i: E.dma_start(
                out=wTs[i][:], in_=wTd[i * 128:(i + 1) * 128, :]), [f'wt{i}'])
            ld(lambda E, i=i: E.dma_start(
                out=aTs[i][:], in_=aTd[i * 128:(i + 1) * 128, :]), [f'aa{i}'])
            ld(lambda E, i=i: E.dma_start(
                out=rTs[i][:], in_=rTd[i * 128:(i + 1) * 128, :]), [f'rr{i}'])
        for g in range(2):
            ld(lambda E, g=g: E.dma_start(
                out=bTs[g][:], in_=bTd[g * 128:(g + 1) * 128, :]), [f'bb{g}'])
        for oc in range(ND):
            ld(lambda E, oc=oc: E.dma_start(
                out=bcols[:, oc:oc + 1], in_=bcold[oc * 128:(oc + 1) * 128, :]),
                ['bcols'])
        for i in range(ND):
            for q in range(NQ):
                ld(lambda E, i=i, q=q: E.dma_start(
                    out=xb[i][:, q * QW:(q + 1) * QW],
                    in_=xin[0, i * 128:(i + 1) * 128, q * QW:(q + 1) * QW]),
                    [f'x{i}q{q}'])
        P.dma_fence('Y', 'ld',
                    ['ident', 'onesc', 'onesr', 'selL', 'selE', 'epsc',
                     'bcols', 'bb0', 'bb1']
                    + [f'wt{i}' for i in range(ND)]
                    + [f'aa{i}' for i in range(ND)]
                    + [f'rr{i}' for i in range(ND)]
                    + [f'x{i}q{q}' for i in range(ND) for q in range(NQ)])

        P.emit('V', lambda E: E.memset(stat[:, :], 0.0), (), ['stat'])
        P.emit('V', lambda E: E.memset(lnrb[:, :], 0.0), (), ['lnrb'])
        P.emit('G', lambda E: E.memset(kvt[:, :], 1.0), (), ['kvt'])
        P.emit('V', lambda E: E.memset(lrows[:, :], 1.0), (), ['lrows'])
        P.emit('V', lambda E: E.memset(gsc[:, :], 0.0), (), ['gsc'])
        P.emit('V', lambda E: E.memset(zcol[:, :], 0.0), (), ['zcol'])

        for h in range(H):
            ld(lambda E, h=h: E.dma_start(
                out=qt[0][:, :], in_=wqd[h * 256:h * 256 + 128, 0:512]),
                ['qt0h0', 'qt0h1'])
            ld(lambda E, h=h: E.dma_start(
                out=qt[1][:, :], in_=wqd[h * 256:h * 256 + 128, 512:1024]),
                ['qt1h0', 'qt1h1'])
            ld(lambda E, h=h: E.dma_start(
                out=qt[2][:, :], in_=wqd[h * 256 + 128:h * 256 + 256, 0:512]),
                ['qt2h0', 'qt2h1'])
            ld(lambda E, h=h: E.dma_start(
                out=qt[3][:, :], in_=wqd[h * 256 + 128:h * 256 + 256, 512:1024]),
                ['qt3h0', 'qt3h1'])
            ld(lambda E, h=h: E.dma_start(
                out=pbuf[0][:, 0:64], in_=wkupd[h * 256:h * 256 + 128, :]),
                ['pb0'])
            ld(lambda E, h=h: E.dma_start(
                out=pbuf[0][:, 64:128],
                in_=wkupd[h * 256 + 128:h * 256 + 256, :]), ['pb0'])
            P.dma_fence('Y', 'ld', ['pb0'] + [f'qt{k}h0' for k in range(4)]
                        + [f'qt{k}h1' for k in range(4)])
            for i in range(ND):
                bk = i % 2
                q01 = i // 4
                c0 = (i % 4) * 128
                P.emit('T', lambda E, bk=bk, q01=q01, c0=c0: E.matmul(
                    ps[bk][0:128, 0:64], qt[q01][:, c0:c0 + 128],
                    pbuf[0][:, 0:64], start=True, stop=False),
                    [f'qt{q01}h0', f'qt{q01}h1', 'pb0'], [PK[bk]])
                P.emit('T', lambda E, bk=bk, q01=q01, c0=c0: E.matmul(
                    ps[bk][0:128, 0:64], qt[2 + q01][:, c0:c0 + 128],
                    pbuf[0][:, 64:128], start=False, stop=True),
                    [f'qt{2+q01}h0', f'qt{2+q01}h1', 'pb0'], [PK[bk]])
                P.emit('V', lambda E, bk=bk, i=i, h=h: E.tensor_copy(
                    wqkT[i][:, h * 64:(h + 1) * 64], ps[bk][0:128, 0:64]),
                    [PK[bk]], [f'wqkT{i}'])
        for jp in range(NHC):
            ld(lambda E, jp=jp: E.dma_start(
                out=pbuf[1][:, 0:64], in_=wvupd[jp * 512:jp * 512 + 128, :]),
                ['pb1'])
            ld(lambda E, jp=jp: E.dma_start(
                out=pbuf[1][:, 64:128],
                in_=wvupd[jp * 512 + 128:jp * 512 + 256, :]), ['pb1'])
            ld(lambda E, jp=jp: E.dma_start(
                out=pbuf[1][:, 128:192],
                in_=wvupd[jp * 512 + 256:jp * 512 + 384, :]), ['pb1'])
            ld(lambda E, jp=jp: E.dma_start(
                out=pbuf[1][:, 192:256],
                in_=wvupd[jp * 512 + 384:jp * 512 + 512, :]), ['pb1'])
            P.dma_fence('Y', 'ld', ['pb1'])
            for ohalf in range(2):
                stg = qt if ohalf == 0 else ab
                sgk = (lambda k: [f'qt{k}h0', f'qt{k}h1']) if ohalf == 0 \
                    else (lambda k: [f'ab{k}'])
                for k2 in range(2):
                    ld(lambda E, jp=jp, k2=k2, ohalf=ohalf, stg=stg:
                       E.dma_start(
                        out=stg[k2][:, :],
                        in_=woTd[jp * 512 + k2 * 128:jp * 512 + (k2 + 1) * 128,
                                 ohalf * 512:(ohalf + 1) * 512]),
                        sgk(k2))
                    ld(lambda E, jp=jp, k2=k2, ohalf=ohalf, stg=stg:
                       E.dma_start(
                        out=stg[2 + k2][:, :],
                        in_=woTd[jp * 512 + 256 + k2 * 128:
                                 jp * 512 + 256 + (k2 + 1) * 128,
                                 ohalf * 512:(ohalf + 1) * 512]),
                        sgk(2 + k2))
                P.dma_fence('Y', 'ld',
                            sum((sgk(k) for k in range(4)), []))
                for hf in range(2):
                    bk = 4 + (2 * ohalf + hf) % 4
                    P.emit('T', lambda E, bk=bk, hf=hf, stg=stg: E.matmul(
                        ps[bk][0:64, :], pbuf[1][:, hf * 128:hf * 128 + 64],
                        stg[2 * hf][:, :], start=True, stop=False),
                        ['pb1'] + sgk(2 * hf), [PK[bk]])
                    P.emit('T', lambda E, bk=bk, hf=hf, stg=stg: E.matmul(
                        ps[bk][0:64, :],
                        pbuf[1][:, hf * 128 + 64:hf * 128 + 128],
                        stg[2 * hf + 1][:, :], start=False, stop=True),
                        ['pb1'] + sgk(2 * hf + 1), [PK[bk]])
                    if hf == 0:
                        P.emit('V', lambda E, bk=bk, jp=jp, ohalf=ohalf:
                               E.tensor_copy(
                            wovT[jp][0:64, ohalf * 512:(ohalf + 1) * 512],
                            ps[bk][0:64, :]), [PK[bk]], [f'wovT{jp}'])
                    else:
                        P.emit('V', lambda E, bk=bk: E.tensor_copy(
                            sqs[0:64, :], ps[bk][0:64, :]),
                            [PK[bk]], ['sqsA', 'sqsB'])
                        gd(lambda E, jp=jp, ohalf=ohalf: E.dma_start(
                            out=wovT[jp][64:128, ohalf * 512:(ohalf + 1) * 512],
                            in_=sqs[0:64, :]), ['sqsA', 'sqsB'],
                            [f'wovT{jp}'])
        if debug:
            st(lambda E: E.dma_start(out=dbg['d_wqk'][:, :], in_=wqkT[0][:, :]),
               ['wqkT0'])
            st(lambda E: E.dma_start(out=dbg['d_wov'][:, :], in_=wovT[0][:, :]),
               ['wovT0'])

        if debug2:
            for j in range(NHC):
                st(lambda E, j=j: E.dma_start(
                    out=dbg['pm_wov0'][:, j * 1024:(j + 1) * 1024],
                    in_=wovT[j][:, :]), [f'wovT{j}'])
        def colsum_neg(dst_fn, dst_key, lhs):
            n = len(lhs)
            for i, (lf, rkeys, kp) in enumerate(lhs):
                P.emit('T', lambda E, lf=lf, i=i, n=n, kp=kp: E.matmul(
                    ps[6][0:kp, 0:1], lf(), onesc[:, :],
                    start=(i == 0), stop=(i == n - 1)),
                    list(rkeys) + ['onesc'], [PK[6]])
            P.emit('S', lambda E, kp=lhs[0][2]: E.mul(
                dst_fn(), ps[6][0:kp, 0:1], -1.0), [PK[6]], [dst_key])

        for j in range(NHC):
            colsum_neg(lambda j=j: negws[:, j:j + 1], 'negws',
                       [(lambda i=i, j=j: wqkT[i][:, j * 128:(j + 1) * 128],
                         [f'wqkT{i}'], 128) for i in range(ND)])
        load_wkv()
        colsum_neg(lambda: negwskv[:, 0:1], 'negwskv',
                   [(lambda i=i: wkv_ap(i), [wkv_key(i)], 128)
                    for i in range(ND)])
        for oc in range(ND):
            colsum_neg(lambda oc=oc: negwsW[:, oc:oc + 1], 'negwsW',
                       [(lambda i=i, oc=oc: wTs[i][:, oc * 128:(oc + 1) * 128],
                         [f'wt{i}'], 128) for i in range(ND)])
        for g in range(2):
            colsum_neg(lambda g=g: negwsA[:, g:g + 1], 'negwsA',
                       [(lambda i=i, g=g: aTs[i][:, g * 128:(g + 1) * 128],
                         [f'aa{i}'], 128) for i in range(ND)])
        colsum_neg(lambda: negwsR[:, 0:1], 'negwsR',
                   [(lambda i=i: rTs[i][:, :], [f'rr{i}'], 8)
                    for i in range(ND)])

        def ln_stats(w, bS, bQ, store_q=None, scr=6):
            assert w == QW
            sqr = sqs[0:1, 0:w]
            msq = pbuf[1][0:1, 0:w]
            rrow = qt[0][0:1, 0:w]
            brow = qt[1][0:1, 0:w]
            rk = 'qt0h0'
            bk_ = 'qt1h0'
            P.emit('V', lambda E, w=w: E.tensor_copy(
                stat[0:1, 0:w], ps[bS][0:1, 0:w]), [PK[bS]], ['stat'])
            P.emit('V', lambda E, w=w: E.tensor_copy(
                stat[32:33, 0:w], ps[bQ][32:33, 0:w]), [PK[bQ]], ['stat'])
            P.emit('T', lambda E, w=w, scr=scr: E.matmul(
                ps[scr][0:1, 0:w], ident[0:33, 32:33], stat[0:33, 0:w],
                start=True, stop=True), ['stat', 'ident'], [PK[scr]])
            P.emit('V', lambda E, w=w, sqr=sqr, scr=scr: E.tensor_copy(
                sqr, ps[scr][0:1, 0:w]), [PK[scr]], ['sqsA', 'sqsB'])
            P.emit('S', lambda E, w=w, msq=msq: E.activation(
                msq, stat[0:1, 0:w], AF.Square, scale=INV_D),
                ['stat'], ['pb1'])
            P.emit('V', lambda E, sqr=sqr, msq=msq: E.scalar_tensor_tensor(
                msq, sqr, INV_D, msq, ALU.mult, ALU.subtract),
                [], ['pb1'], sreads=['sqsA', 'sqsB', 'pb1'])
            P.emit('S', lambda E, msq=msq: E.activation(
                msq, msq, AF.Ln, bias=epsc[0:1, 0:1]),
                ['pb1', 'epsc'], ['pb1'])
            P.emit('S', lambda E, msq=msq, rrow=rrow: E.activation(
                rrow, msq, AF.Exp, scale=-0.5),
                ['pb1'], [rk])
            P.emit('V', lambda E, rrow=rrow, brow=brow: E.scalar_tensor_tensor(
                brow, stat[0:1, 0:w], INV_D, rrow,
                ALU.mult, ALU.mult), [rk], [bk_], sreads=['stat'])
            P.emit('T', lambda E, w=w, rrow=rrow, scr=scr: E.matmul(
                ps[scr][:, 0:w], onesr[:, :], rrow, start=True, stop=True),
                [rk, 'onesr'], [PK[scr]])
            P.emit('V', lambda E, w=w, scr=scr: E.tensor_copy(
                rrep[:, 0:w], ps[scr][:, 0:w]), [PK[scr]], ['rrep'])
            P.emit('T', lambda E, w=w, brow=brow, scr=scr: E.matmul(
                ps[scr][:, 0:w], onesr[:, :], brow, start=True, stop=True),
                [bk_, 'onesr'], [PK[scr]])
            P.emit('V', lambda E, w=w, scr=scr: E.tensor_copy(
                brep[:, 0:w], ps[scr][:, 0:w]), [PK[scr]], ['brep'])
            if store_q is not None:
                gd(lambda E, q=store_q, rrow=rrow: E.dma_start(
                    out=lnrb[q:q + 1, :], in_=rrow), [rk], ['lnrb'])
                gd(lambda E, q=store_q, brow=brow: E.dma_start(
                    out=lnrb[4 + q:5 + q, :], in_=brow), [bk_], ['lnrb'])

        for sq_i in range(SPC):
            XK = lambda i, q: f'x{i}q{q}'
            load_wkv()
            for q in range(NQ):
                qs = slice(q * QW, (q + 1) * QW)
                for i in range(ND):
                    sqt = sqs if i % 2 == 0 else pbuf[1]
                    sk = ['sqsA', 'sqsB'] if i % 2 == 0 else ['pb1']
                    P.emit('S', lambda E, i=i, qs=qs, sqt=sqt: E.activation(
                        sqt[:, :], xb[i][:, qs], AF.Square),
                        [XK(i, q)], sk)
                    P.emit('T', lambda E, i=i, qs=qs: E.matmul(
                        ps[0][0:1, 0:QW], onesc[:, :], xb[i][:, qs],
                        start=(i == 0), stop=(i == ND - 1),
                        tile_position=(0, 0)),
                        [XK(i, q), 'onesc'], [PK[0]])
                    P.emit('T', lambda E, sqt=sqt, i=i: E.matmul(
                        ps[1][32:33, 0:QW], onesc[:, :], sqt[:, :],
                        start=(i == 0), stop=(i == ND - 1),
                        tile_position=(0, 32)), sk + ['onesc'], [PK[1]])
                ln_stats(QW, 0, 1, store_q=q)
                for i in range(ND):
                    P.emit('T', lambda E, i=i, qs=qs: E.matmul(
                        ps[2][:, 0:QW], wkv_ap(i), xb[i][:, qs],
                        start=(i == 0), stop=(i == ND - 1)),
                        [wkv_key(i), XK(i, q)], [PK[2]])
                P.emit('V', lambda E, qs=qs: E.tensor_tensor(
                    kvl[:, qs], ps[2][:, 0:QW], rrep[:, :], ALU.mult),
                    [PK[2], 'rrep'], [f'kvl{q}'])
                P.emit('V', lambda E, qs=qs: E.scalar_tensor_tensor(
                    kvl[:, qs], brep[:, :], negwskv[:, 0:1], kvl[:, qs],
                    ALU.mult, ALU.add),
                    [f'kvl{q}'], [f'kvl{q}'], sreads=['brep', 'negwskv'])
            KVLK = [f'kvl{q}' for q in range(NQ)]
            for cb in range(2):
                bk = 3 + cb
                for ci in range(8):
                    c = cb * 8 + ci
                    P.emit('T', lambda E, bk=bk, c=c, ci=ci: E.transpose(
                        ps[bk][:, ci * 64:(ci + 1) * 64],
                        kvl[0:64, c * 128:(c + 1) * 128], ident[0:64, 0:64]),
                        KVLK + ['ident'], [PK[bk]] if ci == 0 else [PK[bk]])
                for ci in range(8):
                    c = cb * 8 + ci
                    P.emit('V', lambda E, bk=bk, c=c, ci=ci: E.tensor_copy(
                        kvt[:, c * 65:c * 65 + 64],
                        ps[bk][:, ci * 64:(ci + 1) * 64]), [PK[bk]], ['kvt'])
            P.emit('V', lambda E: E.memset(kvm2[:, :], 0.0), (), ['kvm2'])
            P.emit('V', lambda E: E.reduce_max(
                kvm2[0:64, 0:1], kvl[0:64, :], axis=AX.X,
                apply_absolute_value=True), KVLK, ['kvm2'])
            P.emit('V', lambda E: E.reduce_max(
                kvm2[64:128, 1:2], kvl[64:128, :], axis=AX.X,
                apply_absolute_value=True), KVLK, ['kvm2'])
            if debug and sq_i == 0:
                st(lambda E: E.dma_start(out=dbg['d_kvl'][:, :], in_=kvl[:, :]),
                   KVLK)
                st(lambda E: E.dma_start(out=dbg['d_kvt'][:, :], in_=kvt[:, :]),
                   ['kvt'])
                st(lambda E: E.dma_start(out=dbg['d_lnrb'][:, :],
                                         in_=lnrb[:, :]), ['lnrb'])

            for q in range(NQ):
                qs = slice(q * QW, (q + 1) * QW)
                for j in range(NHC):
                    bk = 2 + j
                    for i in range(ND):
                        P.emit('T', lambda E, bk=bk, i=i, j=j, qs=qs: E.matmul(
                            ps[bk][:, 0:QW], wqkT[i][:, j * 128:(j + 1) * 128],
                            xb[i][:, qs], start=(i == 0), stop=(i == ND - 1)),
                            [f'wqkT{i}', XK(i, q)], [PK[bk]])
                P.emit('T', lambda E, q=q: E.matmul(
                    ps[0][0:1, 0:QW], ident[0:8, q:q + 1], lnrb[0:8, :],
                    start=True, stop=True), ['lnrb', 'ident'], [PK[0]])
                P.emit('V', lambda E: E.tensor_copy(
                    qt[0][0:1, :], ps[0][0:1, 0:QW]), [PK[0]], ['qt0h0'])
                P.emit('T', lambda E: E.matmul(
                    ps[1][:, 0:QW], onesr[:, :], qt[0][0:1, :],
                    start=True, stop=True), ['qt0h0', 'onesr'], [PK[1]])
                P.emit('V', lambda E: E.tensor_copy(
                    rrep[:, :], ps[1][:, 0:QW]), [PK[1]], ['rrep'])
                P.emit('T', lambda E, q=q: E.matmul(
                    ps[0][0:1, 0:QW], ident[0:8, 4 + q:5 + q], lnrb[0:8, :],
                    start=True, stop=True), ['lnrb', 'ident'], [PK[0]])
                P.emit('V', lambda E: E.tensor_copy(
                    qt[0][0:1, :], ps[0][0:1, 0:QW]), [PK[0]], ['qt0h0'])
                P.emit('T', lambda E: E.matmul(
                    ps[1][:, 0:QW], onesr[:, :], qt[0][0:1, :],
                    start=True, stop=True), ['qt0h0', 'onesr'], [PK[1]])
                P.emit('V', lambda E: E.tensor_copy(
                    brep[:, :], ps[1][:, 0:QW]), [PK[1]], ['brep'])
                for j in range(NHC):
                    bk = 2 + j
                    P.emit('V', lambda E, bk=bk, j=j: E.tensor_tensor(
                        qt[j][:, :], ps[bk][:, 0:QW], rrep[:, :], ALU.mult),
                        [PK[bk], 'rrep'], [f'qt{j}h0', f'qt{j}h1'])
                    P.emit('V', lambda E, j=j: E.scalar_tensor_tensor(
                        qt[j][:, :], brep[:, :], negws[:, j:j + 1], qt[j][:, :],
                        ALU.mult, ALU.add),
                        [f'qt{j}h0', f'qt{j}h1'],
                        [f'qt{j}h0', f'qt{j}h1'], sreads=['brep', 'negws'])
                P.emit('V', lambda E: E.memset(ps[5][:, :], -1e30),
                       (), [PK[5]])
                for j in range(NHC):
                    P.emit('S', lambda E, j=j: E.activation(
                        sqs[:, :], qt[j][:, :], AF.Abs),
                        [f'qt{j}h0', f'qt{j}h1'], ['sqsA', 'sqsB'])
                    P.emit('T', lambda E, j=j: E.matmul(
                        ps[5][32 * j:32 * j + 2, 0:QW], kvm2[:, :], sqs[:, :],
                        start=True, stop=True,
                        tile_position=(0, 32 * j)),
                        ['kvm2', 'sqsA', 'sqsB'], [PK[5]])
                if debug and sq_i == 0 and q == 0:
                    P.emit('V', lambda E: E.tensor_copy(
                        pbuf[0][:, :], ps[5][:, 0:QW]), [PK[5]], ['pb0'])
                    st(lambda E: E.dma_start(out=dbg['d_psb'][:, :],
                                             in_=pbuf[0][:, :]), ['pb0'])
                    st(lambda E: E.dma_start(out=dbg['d_sqs'][:, :],
                                             in_=sqs[:, :]), ['sqsA', 'sqsB'])
                    st(lambda E: E.dma_start(out=dbg['d_kvm2'][:, :],
                                             in_=kvm2[:, :]), ['kvm2'])
                P.emit('V', lambda E: E.reduce_max(
                    gsc[:, 0:1], ps[5][:, 0:QW], axis=AX.X, negate=True),
                    [PK[5]], ['gsc'])
                P.emit('S', lambda E: E.mul(gsc[:, 1:2], gsc[:, 0:1], SCALE),
                       ['gsc'], ['gsc'])
                if debug and sq_i == 0 and q == 0:
                    st(lambda E: E.dma_start(out=dbg['d_gsc'][:, :],
                                             in_=gsc[:, :]), ['gsc'])
                for h in range(H):
                    r_h = 32 * (h // 2) + (h % 2)
                    P.emit('T', lambda E, r_h=r_h: E.matmul(
                        ps[4][0:1, 0:1], ident[0:128, r_h:r_h + 1],
                        gsc[0:128, 1:2], start=True, stop=True),
                        ['gsc', 'ident'], [PK[4]])
                    P.emit('V', lambda E, h=h: E.tensor_copy(
                        stat[0:1, 100 + h:101 + h], ps[4][0:1, 0:1]),
                        [PK[4]], ['stat'])
                    P.emit('T', lambda E, h=h: E.matmul(
                        ps[5][:, 0:1], onesr[:, :],
                        stat[0:1, 100 + h:101 + h], start=True, stop=True),
                        ['stat', 'onesr'], [PK[5]])
                    P.emit('V', lambda E, h=h: E.tensor_copy(
                        negG[:, h:h + 1], ps[5][:, 0:1]), [PK[5]],
                        [f'negG{h}'])
                if debug and sq_i == 0 and q == 0:
                    st(lambda E: E.dma_start(out=dbg['d_ng'][:, :],
                                             in_=negG[:, :]),
                       [f'negG{h}' for h in range(H)])
                    for j in range(NHC):
                        st(lambda E, j=j: E.dma_start(
                            out=dbg['d_qt'][:, j * QW:(j + 1) * QW],
                            in_=qt[j][:, :]), [f'qt{j}h0', f'qt{j}h1'])

                for hp in range(NHC):
                    j = hp
                    qke, qko = f'qt{j}h0', f'qt{j}h1'
                    obe = 4 + hp % 2
                    obo = 6 + hp % 2

                    def eS(c, hf, j=j):
                        bk = (c % 2) + 2 * hf
                        qk = f'qt{j}h{hf}'
                        return P.emit('T', lambda E, bk=bk, c=c, hf=hf, j=j:
                                      E.matmul(
                            ps[bk][:, 0:QW],
                            kvl[64 * hf:64 * hf + 64, c * 128:(c + 1) * 128],
                            qt[j][64 * hf:64 * hf + 64, :],
                            start=True, stop=True,
                            tile_position=(64 * hf, 0)),
                            KVLK + [qk], [PK[bk]])

                    def eE(c, hf, hp=hp):
                        bk = (c % 2) + 2 * hf
                        pbi = (c % 2) + 2 * hf
                        h = 2 * hp + hf
                        return P.emit('S', lambda E, bk=bk, pbi=pbi, h=h:
                                      E.activation(
                            pball[pbi][:, :], ps[bk][:, 0:QW], AF.Exp,
                            bias=negG[:, h:h + 1], scale=SCALE),
                            [PK[bk], f'negG{h}'], [f'pb{pbi}'])

                    def eO(c, hf, obe=obe, obo=obo):
                        ob = obe if hf == 0 else obo
                        pbi = (c % 2) + 2 * hf
                        return P.emit('T', lambda E, c=c, ob=ob, pbi=pbi:
                                      E.matmul(
                            ps[ob][0:65, 0:QW], kvt[:, c * 65:(c + 1) * 65],
                            pball[pbi][:, :],
                            start=(c == 0), stop=(c == NKC - 1)),
                            [f'pb{pbi}', 'kvt'], [PK[ob]])

                    for c in range(2):
                        eS(c, 0)
                        eS(c, 1)
                        eE(c, 0)
                        eE(c, 1)
                    for c in range(2, NKC):
                        eO(c - 2, 0)
                        eO(c - 2, 1)
                        eS(c, 0)
                        eS(c, 1)
                        eE(c, 0)
                        eE(c, 1)
                    for c in (NKC - 2, NKC - 1):
                        eO(c, 0)
                        eO(c, 1)
                    P.emit('V', lambda E, obe=obe, j=j: E.tensor_copy(
                        qt[j][0:64, :], ps[obe][0:64, 0:QW]),
                        [PK[obe]], [qke])
                    P.emit('V', lambda E, obe=obe: E.tensor_copy(
                        sqs[64:65, :], ps[obe][64:65, 0:QW]),
                        [PK[obe]], ['sqsA', 'sqsB'])
                    gd(lambda E, he=2 * hp: E.dma_start(
                        out=lrows[he:he + 1, :], in_=sqs[64:65, :]),
                        ['sqsA', 'sqsB'], ['lrows'])
                    P.emit('V', lambda E, obo=obo: E.tensor_copy(
                        sqs[0:64, :], ps[obo][0:64, 0:QW]),
                        [PK[obo]], ['sqsA', 'sqsB'])
                    gd(lambda E, j=j: E.dma_start(
                        out=qt[j][64:128, :], in_=sqs[0:64, :]),
                        ['sqsA', 'sqsB'], [qko])
                    P.emit('V', lambda E, obo=obo: E.tensor_copy(
                        sqs[64:65, :], ps[obo][64:65, 0:QW]),
                        [PK[obo]], ['sqsA', 'sqsB'])
                    gd(lambda E, ho=2 * hp + 1: E.dma_start(
                        out=lrows[ho:ho + 1, :], in_=sqs[64:65, :]),
                        ['sqsA', 'sqsB'], ['lrows'])
                P.emit('V', lambda E: E.reciprocal(rl[0:8, :], lrows[0:8, :]),
                       ['lrows'], ['rl'])
                for j in range(NHC):
                    P.emit('T', lambda E, j=j: E.matmul(
                        ps[4][:, 0:QW], selL[:, j * 128:(j + 1) * 128],
                        rl[0:8, :], start=True, stop=True),
                        ['rl', 'selL'], [PK[4]])
                    P.emit('V', lambda E, j=j: E.tensor_tensor(
                        qt[j][:, :], qt[j][:, :], ps[4][:, 0:QW], ALU.mult),
                        [PK[4], f'qt{j}h0', f'qt{j}h1'],
                        [f'qt{j}h0', f'qt{j}h1'])
                if debug and sq_i == 0 and q == 0:
                    st(lambda E: E.dma_start(out=dbg['d_lr'][:, :],
                                             in_=lrows[0:8, :]), ['lrows'])
                    for j in range(NHC):
                        st(lambda E, j=j: E.dma_start(
                            out=dbg['d_on'][:, j * QW:(j + 1) * QW],
                            in_=qt[j][:, :]), [f'qt{j}h0', f'qt{j}h1'])

                cs = slice(q * QW, (q + 1) * QW)
                for oc in range(ND):
                    bk = 5 + oc % 2
                    sqt = sqs if oc % 2 == 0 else pbuf[1]
                    sk = ['sqsA', 'sqsB'] if oc % 2 == 0 else ['pb1']
                    for j in range(NHC):
                        P.emit('T', lambda E, bk=bk, j=j, oc=oc: E.matmul(
                            ps[bk][:, 0:QW],
                            wovT[j][:, oc * 128:(oc + 1) * 128],
                            qt[j][:, :], start=(j == 0),
                            stop=(j == NHC - 1)),
                            [f'wovT{j}', f'qt{j}h0', f'qt{j}h1'],
                            [PK[bk]])
                    P.emit('S', lambda E, bk=bk, oc=oc: E.activation(
                        ab[oc][:, :], ps[bk][:, 0:QW], AF.Identity),
                        [PK[bk]], [f'ab{oc}'])
                    P.emit('S', lambda E, bk=bk, sqt=sqt: E.activation(
                        sqt[:, :], ps[bk][:, 0:QW], AF.Square),
                        [PK[bk]], sk)
                    P.emit('T', lambda E, oc=oc: E.matmul(
                        ps[0][0:1, 0:QW], onesc[:, :], ab[oc][:, :],
                        start=(oc == 0), stop=(oc == ND - 1),
                        tile_position=(0, 0)),
                        [f'ab{oc}', 'onesc'], [PK[0]])
                    P.emit('T', lambda E, sqt=sqt, oc=oc: E.matmul(
                        ps[1][32:33, 0:QW], onesc[:, :], sqt[:, :],
                        start=(oc == 0), stop=(oc == ND - 1),
                        tile_position=(0, 32)), sk + ['onesc'],
                        [PK[1]])
                ln_stats(QW, 0, 1)
                for i in range(ND):
                    P.emit('V', lambda E, i=i: E.tensor_tensor(
                        ab[i][:, :], ab[i][:, :], rrep[:, :], ALU.mult),
                        ['rrep', f'ab{i}'], [f'ab{i}'])
                    P.emit('V', lambda E, i=i: E.tensor_tensor(
                        ab[i][:, :], ab[i][:, :], brep[:, :],
                        ALU.subtract), ['brep', f'ab{i}'], [f'ab{i}'])
                    P.emit('V', lambda E, i=i, cs=cs: E.tensor_tensor(
                        xb[i][:, cs], xb[i][:, cs], ab[i][:, :], ALU.add),
                        [f'ab{i}', XK(i, q)], [XK(i, q)])
                if debug and sq_i == 0 and q == 0:
                    for oc in range(ND):
                        st(lambda E, oc=oc: E.dma_start(
                            out=dbg['d_ab'][:, oc * HW:(oc + 1) * HW],
                            in_=ab[oc][:, 0:HW]), [f'ab{oc}'])

            if debug and sq_i == 0:
                st(lambda E: E.dma_start(out=dbg['d_xa'][:, :],
                                         in_=xb[0][:, :]),
                   [XK(0, qq) for qq in range(NQ)])
            for q in range(NQ):
                if True:
                    cs = slice(q * QW, (q + 1) * QW)
                    for i in range(ND):
                        sqt = sqs if i % 2 == 0 else pbuf[1]
                        sk = ['sqsA', 'sqsB'] if i % 2 == 0 else ['pb1']
                        P.emit('S', lambda E, i=i, cs=cs, sqt=sqt: E.activation(
                            sqt[:, :], xb[i][:, cs], AF.Square),
                            [XK(i, q)], sk)
                        P.emit('T', lambda E, i=i, cs=cs: E.matmul(
                            ps[0][0:1, 0:QW], onesc[:, :], xb[i][:, cs],
                            start=(i == 0), stop=(i == ND - 1),
                            tile_position=(0, 0)),
                            [XK(i, q), 'onesc'], [PK[0]])
                        P.emit('T', lambda E, sqt=sqt, i=i: E.matmul(
                            ps[1][32:33, 0:QW], onesc[:, :], sqt[:, :],
                            start=(i == 0), stop=(i == ND - 1),
                            tile_position=(0, 32)), sk + ['onesc'], [PK[1]])
                    for i in range(ND):
                        P.emit('T', lambda E, i=i, cs=cs: E.matmul(
                            ps[2][0:8, 0:QW], rTs[i][:, :], xb[i][:, cs],
                            start=(i == 0), stop=(i == ND - 1)),
                            [f'rr{i}', XK(i, q)], [PK[2]])
                    for g in range(2):
                        hb = 6 + g
                        for i in range(ND):
                            P.emit('T', lambda E, i=i, g=g, hb=hb, cs=cs:
                                   E.matmul(
                                ps[hb][:, 0:QW],
                                aTs[i][:, g * 128:(g + 1) * 128],
                                xb[i][:, cs], start=(i == 0),
                                stop=(i == ND - 1)),
                                [f'aa{i}', XK(i, q)], [PK[hb]])
                    for oc in range(2):
                        bk = 4 + oc
                        for i in range(ND):
                            P.emit('T', lambda E, bk=bk, i=i, oc=oc, cs=cs:
                                   E.matmul(
                                ps[bk][:, 0:QW],
                                wTs[i][:, oc * 128:(oc + 1) * 128],
                                xb[i][:, cs], start=(i == 0),
                                stop=(i == ND - 1)),
                                [f'wt{i}', XK(i, q)], [PK[bk]])
                    ln_stats(QW, 0, 1, scr=0)
                    P.emit('V', lambda E: E.tensor_tensor(
                        gate8[0:8, :], ps[2][0:8, 0:QW], rrep[0:8, :],
                        ALU.mult), [PK[2], 'rrep'], ['rl'])
                    P.emit('V', lambda E: E.scalar_tensor_tensor(
                        gate8[0:8, :], brep[0:8, :], negwsR[0:8, 0:1],
                        gate8[0:8, :], ALU.mult, ALU.add),
                        ['rl'], ['rl'], sreads=['brep', 'negwsR'])
                    for tb4 in range(4):
                        P.emit('T', lambda E, tb4=tb4: E.transpose(
                            ps[3][:, tb4 * 8:(tb4 + 1) * 8],
                            gate8[0:8, tb4 * 128:(tb4 + 1) * 128],
                            ident[0:8, 0:8]),
                            ['rl', 'ident'], [PK[3]])
                    P.emit('V', lambda E: E.tensor_copy(
                        ltk[:, :], ps[3][:, 0:32]), [PK[3]], ['ltk'])
                    if debug and sq_i == 0 and q == 0:
                        st(lambda E: E.dma_start(out=dbg['d_ltk'][:, :],
                                                 in_=ltk[:, 0:16]), ['ltk'])
                        st(lambda E: E.dma_start(out=dbg['d_rr'][:, :],
                                                 in_=rrep[:, :]), ['rrep'])
                        st(lambda E: E.dma_start(out=dbg['d_st'][:, :],
                                                 in_=stat[:, :]), ['stat'])
                    for tb in range(4):
                        tsl = slice(tb * 8, (tb + 1) * 8)
                        P.emit('V', lambda E, tsl=tsl: E.reduce_max(
                            m1t[:, 0:1], ltk[:, tsl], axis=AX.X),
                            ['ltk'], ['m1t'])
                        P.emit('V', lambda E, tsl=tsl: E.reduce_max(
                            nm1t[:, 0:1], ltk[:, tsl], axis=AX.X, negate=True),
                            ['ltk'], ['nm1t'])
                        P.emit('S', lambda E, tsl=tsl: E.activation(
                            geg[:, :], ltk[:, tsl], AF.Exp, bias=nm1t[:, 0:1]),
                            ['ltk', 'nm1t'], ['geg'])
                        P.emit('V', lambda E: E.reduce_sum(
                            lsum[:, 0:1], geg[:, :], axis=AX.X),
                            ['geg'], ['lsum'])
                        P.emit('V', lambda E, tsl=tsl: E.tensor_scalar(
                            gt2[:, :], ltk[:, tsl], m1t[:, 0:1], None,
                            ALU.subtract), ['ltk'], ['gt2'], sreads=['m1t'])
                        P.emit('V', lambda E: E.tensor_scalar(
                            gmsk[:, :], gt2[:, :], zcol[:, 0:1], None,
                            ALU.is_ge), ['gt2'], ['gmsk'], sreads=['zcol'])
                        P.emit('V', lambda E, tsl=tsl: E.scalar_tensor_tensor(
                            gt2[:, :], gmsk[:, :], -1e30, ltk[:, tsl],
                            ALU.mult, ALU.add), ['ltk'], ['gt2'],
                            sreads=['gmsk'])
                        P.emit('V', lambda E: E.reduce_max(
                            m2t[:, 0:1], gt2[:, :], axis=AX.X),
                            ['gt2'], ['m2t'])
                        P.emit('V', lambda E, tsl=tsl: E.tensor_scalar(
                            gmsk[:, :], ltk[:, tsl], m2t[:, 0:1], None,
                            ALU.subtract), ['ltk'], ['gmsk'], sreads=['m2t'])
                        P.emit('V', lambda E: E.tensor_scalar(
                            gmsk[:, :], gmsk[:, :], zcol[:, 0:1], None,
                            ALU.is_ge), ['gmsk'], ['gmsk'], sreads=['zcol'])
                        P.emit('V', lambda E: E.tensor_tensor(
                            gem[:, :], geg[:, :], gmsk[:, :], ALU.mult),
                            ['geg', 'gmsk'], ['gem'])
                        P.emit('V', lambda E: E.reduce_sum(
                            s2t[:, 0:1], gem[:, :], axis=AX.X),
                            ['gem'], ['s2t'])
                        P.emit('S', lambda E: E.mul(
                            dent[:, 0:1], lsum[:, 0:1], GEPS),
                            ['lsum'], ['dent'])
                        P.emit('V', lambda E: E.tensor_tensor(
                            dent[:, 0:1], dent[:, 0:1], s2t[:, 0:1], ALU.add),
                            ['dent', 's2t'], ['dent'])
                        P.emit('V', lambda E: E.reciprocal(
                            rdent[:, 0:1], dent[:, 0:1]), ['dent'], ['rdent'])
                        P.emit('V', lambda E, tsl=tsl: E.tensor_scalar(
                            gfin[:, tsl], gem[:, :], rdent[:, 0:1], None,
                            ALU.mult), ['gem'], ['gfin'], sreads=['rdent'])
                        if debug and sq_i == 0 and q == 0 and tb == 0:
                            st(lambda E: E.dma_start(out=dbg['d_geg'][:, :],
                                                     in_=geg[:, :]), ['geg'])
                            st(lambda E: E.dma_start(out=dbg['d_gt2'][:, :],
                                                     in_=gt2[:, :]), ['gt2'])
                            st(lambda E: E.dma_start(out=dbg['d_gmsk'][:, :],
                                                     in_=gmsk[:, :]), ['gmsk'])
                            st(lambda E: E.dma_start(out=dbg['d_gem'][:, :],
                                                     in_=gem[:, :]), ['gem'])
                            st(lambda E: E.dma_start(out=dbg['d_gsc2'][:, :],
                                                     in_=gsc[:, :]), ['gsc'])
                    for tb4 in range(4):
                        P.emit('T', lambda E, tb4=tb4: E.transpose(
                            ps[2][0:8, tb4 * 128:(tb4 + 1) * 128],
                            gfin[:, tb4 * 8:(tb4 + 1) * 8], ident[:, :]),
                            ['gfin', 'ident'], [PK[2]])
                    P.emit('V', lambda E: E.tensor_copy(
                        gate8[0:8, :], ps[2][0:8, 0:QW]), [PK[2]], ['rl'])
                    if debug and sq_i == 0 and q == 0:
                        st(lambda E: E.dma_start(out=dbg['d_gf'][:, :],
                                                 in_=gfin[:, 0:16]), ['gfin'])
                    for g in range(2):
                        gb = 2 + g
                        hb = 6 + g
                        P.emit('T', lambda E, g=g, gb=gb: E.matmul(
                            ps[gb][:, 0:QW], selE[:, g * 128:(g + 1) * 128],
                            gate8[0:8, :], start=True, stop=True),
                            ['rl', 'selE'], [PK[gb]])
                        P.emit('V', lambda E, g=g, hb=hb: E.tensor_tensor(
                            hsb[g][:, :], ps[hb][:, 0:QW], rrep[:, :],
                            ALU.mult), [PK[hb], 'rrep'], [HSK[g]])
                        P.emit('V', lambda E, g=g: E.scalar_tensor_tensor(
                            hsb[g][:, :], brep[:, :], negwsA[:, g:g + 1],
                            hsb[g][:, :], ALU.mult, ALU.add),
                            [HSK[g]], [HSK[g]], sreads=['brep', 'negwsA'])
                        P.emit('V', lambda E, g=g, gb=gb: E.tensor_tensor(
                            hsb[g][:, :], hsb[g][:, :], ps[gb][:, 0:QW],
                            ALU.mult), [PK[gb], HSK[g]], [HSK[g]])
                    if debug and sq_i == 0 and q == 0:
                        for g in range(2):
                            st(lambda E, g=g: E.dma_start(
                                out=dbg['d_hs'][:, g * HW:(g + 1) * HW],
                                in_=hsb[g][:, 0:HW]), [HSK[g]])
                    for oc in range(ND):
                        bk = 4 + oc if oc < 2 else oc % 2
                        sqt = sqs if oc % 2 == 0 else pbuf[1]
                        sk = ['sqsA', 'sqsB'] if oc % 2 == 0 else ['pb1']
                        if oc >= 2:
                            for i in range(ND):
                                P.emit('T', lambda E, bk=bk, i=i, oc=oc, cs=cs:
                                       E.matmul(
                                    ps[bk][:, 0:QW],
                                    wTs[i][:, oc * 128:(oc + 1) * 128],
                                    xb[i][:, cs], start=(i == 0),
                                    stop=(i == ND - 1)),
                                    [f'wt{i}', XK(i, q)], [PK[bk]])
                        P.emit('T', lambda E, oc=oc: E.matmul(
                            ps[7][:, 0:QW], bTs[0][:, oc * 128:(oc + 1) * 128],
                            hsb[0][:, :], start=True, stop=False),
                            ['bb0', 'pb2'], [PK[7]])
                        P.emit('T', lambda E, oc=oc: E.matmul(
                            ps[7][:, 0:QW], bTs[1][:, oc * 128:(oc + 1) * 128],
                            hsb[1][:, :], start=False, stop=True),
                            ['bb1', 'pb3'], [PK[7]])
                        P.emit('V', lambda E, bk=bk, oc=oc: E.tensor_tensor(
                            ab[oc][:, :], ps[bk][:, 0:QW], rrep[:, :],
                            ALU.mult), [PK[bk], 'rrep'], [f'ab{oc}'])
                        P.emit('V', lambda E, oc=oc: E.scalar_tensor_tensor(
                            ab[oc][:, :], brep[:, :], negwsW[:, oc:oc + 1],
                            ab[oc][:, :], ALU.mult, ALU.add),
                            [f'ab{oc}'], [f'ab{oc}'], sreads=['brep', 'negwsW'])
                        P.emit('V', lambda E, oc=oc: E.tensor_tensor(
                            ab[oc][:, :], ab[oc][:, :], ps[7][:, 0:QW],
                            ALU.add), [PK[7], f'ab{oc}'], [f'ab{oc}'])
                        P.emit('S', lambda E, oc=oc: E.activation(
                            ab[oc][:, :], ab[oc][:, :], AF.Identity,
                            bias=bcols[:, oc:oc + 1]),
                            ['bcols', f'ab{oc}'], [f'ab{oc}'])
                        P.emit('S', lambda E, oc=oc, sqt=sqt: E.activation(
                            sqt[:, :], ab[oc][:, :], AF.Square),
                            [f'ab{oc}'], sk)
                        P.emit('T', lambda E, oc=oc: E.matmul(
                            ps[2][0:1, 0:QW], onesc[:, :], ab[oc][:, :],
                            start=(oc == 0), stop=(oc == ND - 1),
                            tile_position=(0, 0)),
                            [f'ab{oc}', 'onesc'], [PK[2]])
                        P.emit('T', lambda E, oc=oc, sqt=sqt: E.matmul(
                            ps[3][32:33, 0:QW], onesc[:, :], sqt[:, :],
                            start=(oc == 0), stop=(oc == ND - 1),
                            tile_position=(0, 32)), sk + ['onesc'],
                            [PK[3]])
                    ln_stats(QW, 2, 3)
                    for i in range(ND):
                        P.emit('V', lambda E, i=i: E.tensor_tensor(
                            ab[i][:, :], ab[i][:, :], rrep[:, :], ALU.mult),
                            ['rrep', f'ab{i}'], [f'ab{i}'])
                        P.emit('V', lambda E, i=i: E.tensor_tensor(
                            ab[i][:, :], ab[i][:, :], brep[:, :],
                            ALU.subtract), ['brep', f'ab{i}'], [f'ab{i}'])
                        P.emit('V', lambda E, i=i, cs=cs: E.tensor_tensor(
                            xb[i][:, cs], xb[i][:, cs], ab[i][:, :], ALU.add),
                            [f'ab{i}', XK(i, q)], [XK(i, q)])
                        st(lambda E, i=i, cs=cs, sq_i=sq_i: E.dma_start(
                            out=xout[sq_i, i * 128:(i + 1) * 128, cs],
                            in_=xb[i][:, cs]), [XK(i, q)])
                    P.dma_fence_r('Y', 'ot',
                                  [XK(i, q) for i in range(ND)])
                    if sq_i + 1 < SPC:
                        for i in range(ND):
                            ld(lambda E, i=i, q=q, sq_i=sq_i: E.dma_start(
                                out=xb[i][:, q * QW:(q + 1) * QW],
                                in_=xin[sq_i + 1, i * 128:(i + 1) * 128,
                                        q * QW:(q + 1) * QW]), [XK(i, q)])
                        P.dma_fence('Y', 'ld', [XK(i, q) for i in range(ND)])

        if debug2:
            st(lambda E: E.dma_start(out=dbg['pm_ws'][:, 0:4], in_=negws[:, :], allow_slow_non_contiguous=True),
               ['negws'])
            st(lambda E: E.dma_start(out=dbg['pm_ws'][:, 4:5],
                                     in_=negwskv[:, :], allow_slow_non_contiguous=True), ['negwskv'])
            st(lambda E: E.dma_start(out=dbg['pm_ws'][:, 5:13],
                                     in_=negwsW[:, :], allow_slow_non_contiguous=True), ['negwsW'])
            st(lambda E: E.dma_start(out=dbg['pm_ws'][:, 13:15],
                                     in_=negwsA[:, :], allow_slow_non_contiguous=True), ['negwsA'])
            for i in range(ND):
                st(lambda E, i=i: E.dma_start(
                    out=dbg['pm_wqk'][:, i * 512:(i + 1) * 512],
                    in_=wqkT[i][:, :]), [f'wqkT{i}'])
            for j in range(NHC):
                st(lambda E, j=j: E.dma_start(
                    out=dbg['pm_wov'][:, j * 1024:(j + 1) * 1024],
                    in_=wovT[j][:, :]), [f'wovT{j}'])
            st(lambda E: E.dma_start(out=dbg['pm_kvl'][:, :], in_=kvl[:, :]),
               [f'kvl{qq}' for qq in range(NQ)])
            st(lambda E: E.dma_start(out=dbg['pm_lnrb'][:, :], in_=lnrb[:, :]),
               ['lnrb'])
        P.items.append(('Y', tuple([('ot', P.val.get('ot', 0)),
                                    ('gd', P.val.get('gd', 0))]), None, None, 0))

        def walker(engname):
            def go(E):
                waited = {}
                for (en_, waits, fn, sem, step) in P.items:
                    if en_ != engname:
                        continue
                    for (sn, vv) in waits:
                        if waited.get(sn, -1) < vv:
                            E.wait_ge(sems[sn], vv)
                            waited[sn] = vv
                    if fn is None:
                        continue
                    ins = fn(E)
                    ins.then_inc(sems[sem], step)
            return go

        @block.sync
        def _(E):
            walker('Y')(E)

        @block.tensor
        def _(E):
            walker('T')(E)

        @block.vector
        def _(E):
            walker('V')(E)

        @block.scalar
        def _(E):
            walker('S')(E)

        @block.gpsimd
        def _(E):
            walker('G')(E)

    return nc


def _host_prep(L0_attn_x, L0_attn_Wq, L0_attn_Wkv_down, L0_attn_Wk_up,
               L0_attn_Wv_up, L0_attn_Wo, L0_ffn_W, L0_ffn_b, L0_ffn_R,
               L0_ffn_A, L0_ffn_B):
    f = np.float32
    x = np.ascontiguousarray(np.asarray(L0_attn_x, f).transpose(0, 2, 1))
    wq = np.ascontiguousarray(np.asarray(L0_attn_Wq, f))
    wkup = np.ascontiguousarray(np.asarray(L0_attn_Wk_up, f))
    wvup = np.ascontiguousarray(np.asarray(L0_attn_Wv_up, f))
    woT = np.ascontiguousarray(np.asarray(L0_attn_Wo, f).T)
    wkvT = np.asarray(L0_attn_Wkv_down, f).T
    wkvT2 = np.ascontiguousarray(np.concatenate([wkvT, wkvT], axis=1))
    wT = np.ascontiguousarray(np.asarray(L0_ffn_W, f).T)
    rT = np.ascontiguousarray(np.asarray(L0_ffn_R, f).T)
    A = np.asarray(L0_ffn_A, f)
    aT = np.zeros((1024, 256), f)
    for ex in range(8):
        g, ei = divmod(ex, 4)
        aT[:, g * 128 + ei * 32: g * 128 + ei * 32 + 16] = A[ex].T
    Bw = np.asarray(L0_ffn_B, f)
    bT = np.zeros((256, 1024), f)
    for ex in range(8):
        g, ei = divmod(ex, 4)
        bT[g * 128 + ei * 32: g * 128 + ei * 32 + 16, :] = Bw[ex].T
    bcol = np.ascontiguousarray(np.asarray(L0_ffn_b, f).reshape(1024, 1))
    identc = np.eye(128, dtype=f)
    onesc = np.ones((128, 1), f)
    onesr = np.ones((1, 128), f)
    selL = np.zeros((8, 512), f)
    for j in range(4):
        selL[2 * j, j * 128:j * 128 + 64] = 1.0
        selL[2 * j + 1, j * 128 + 64:j * 128 + 128] = 1.0
    selE = np.zeros((8, 256), f)
    for g in range(2):
        for ex in range(4):
            selE[g * 4 + ex, g * 128 + ex * 32:g * 128 + (ex + 1) * 32] = 1.0
    base = dict(wq=wq, wkup=wkup, wvup=wvup, woT=woT, wkvT2=wkvT2, wTd=wT,
                rTd=rT, aTd=aT, bTd=bT, bcold=bcol, identd=identc,
                onescd=onesc, onesrd=onesr, selLd=selL, selEd=selE,
                epsd=np.full((128, 1), LNEPS, f))
    in_maps = []
    for c in range(NCORES):
        m = dict(base)
        m["xin"] = np.ascontiguousarray(x[c * SPC:(c + 1) * SPC])
        in_maps.append(m)
    return in_maps


def kernel(**inputs):
    debug = bool(inputs.pop("_debug", False))
    debug2 = bool(inputs.pop("_debug2", False))
    in_maps = _host_prep(**inputs)
    nc = build(debug=debug, debug2=debug2)
    res = run_bass_kernel_spmd(nc, in_maps, core_ids=list(range(NCORES)))
    outs = []
    for c in range(NCORES):
        xo = np.asarray(res.results[c]["xout"])
        outs.append(xo.transpose(0, 2, 1))
    full = np.ascontiguousarray(np.concatenate(outs, axis=0).astype(np.float32))
    if debug or debug2:
        kernel._dbg = res.results
    return full


# version 10
# speedup vs baseline: 1.1531x; 1.0091x over previous
import numpy as np
from contextlib import ExitStack
import concourse.bass as bass
import concourse.mybir as mybir
from concourse.bass_utils import run_bass_kernel_spmd

F32 = mybir.dt.float32
AF = mybir.ActivationFunctionType
ALU = mybir.AluOpType
AX = mybir.AxisListType

NCORES = 8
T, D = 2048, 1024
SPC = 4
NQ = 4
QW = 512
HW = 256
ND = 8
NHC = 4
NKC = 16
H = 8
SCALE = 0.0625
LNEPS = 1e-5
GEPS = 1e-6
INV_D = 1.0 / 1024.0

ESEM = {'T': 't', 'V': 'v', 'S': 's', 'G': 'g'}


class Prog:

    def __init__(self):
        self.items = []
        self.val = {}
        self.w = {}
        self.r = {}

    def emit(self, eng, fn, reads=(), writes=(), extra=(), dma=None,
             sreads=()):
        waits = {}

        def add(mk, force=False):
            if mk is None:
                return
            sn, vv = mk
            if not force and eng in ESEM and sn == ESEM[eng]:
                return
            if waits.get(sn, -1) < vv:
                waits[sn] = vv

        force_all = eng in ('V', 'S')
        for k in reads:
            add(self.w.get(k), force=force_all)
        for k in sreads:
            add(self.w.get(k), force=True)
        for k in writes:
            add(self.w.get(k))
            for sn, vv in self.r.get(k, {}).items():
                add((sn, vv))
        for mk in extra:
            add(mk)
        if dma is not None:
            sem, step = dma, 16
        else:
            sem, step = ESEM[eng], 1
        self.items.append((eng, tuple(waits.items()), fn, sem, step))
        self.val[sem] = self.val.get(sem, 0) + step
        mk = (sem, self.val[sem])
        for k in list(reads) + list(sreads):
            d = self.r.setdefault(k, {})
            if d.get(mk[0], -1) < mk[1]:
                d[mk[0]] = mk[1]
        for k in writes:
            self.w[k] = mk
            self.r[k] = {}
        return mk

    def barrier_mark(self):
        return [(sn, vv) for sn, vv in self.val.items()]

    def dma_fence(self, eng, sem, keys=()):
        v = self.val.get(sem, 0)
        self.items.append((eng, ((sem, v),), None, None, 0))
        for k in keys:
            self.w[k] = (sem, v)
            self.r[k] = {}

    def dma_fence_r(self, eng, sem, rkeys=()):
        v = self.val.get(sem, 0)
        self.items.append((eng, ((sem, v),), None, None, 0))
        for k in rkeys:
            self.r[k] = {sem: v}


def build(debug=False, debug2=False):
    nc = bass.Bass("TRN2", target_bir_lowering=False, debug=False)

    xin = nc.dram_tensor("xin", [SPC, D, T], F32, kind="ExternalInput").ap()
    wqd = nc.dram_tensor("wq", [2048, 1024], F32, kind="ExternalInput").ap()
    wkupd = nc.dram_tensor("wkup", [2048, 64], F32, kind="ExternalInput").ap()
    wvupd = nc.dram_tensor("wvup", [2048, 64], F32, kind="ExternalInput").ap()
    woTd = nc.dram_tensor("woT", [2048, 1024], F32, kind="ExternalInput").ap()
    wkvT2d = nc.dram_tensor("wkvT2", [1024, 128], F32, kind="ExternalInput").ap()
    wTd = nc.dram_tensor("wTd", [1024, 1024], F32, kind="ExternalInput").ap()
    rTd = nc.dram_tensor("rTd", [1024, 8], F32, kind="ExternalInput").ap()
    aTd = nc.dram_tensor("aTd", [1024, 256], F32, kind="ExternalInput").ap()
    bTd = nc.dram_tensor("bTd", [256, 1024], F32, kind="ExternalInput").ap()
    bcold = nc.dram_tensor("bcold", [1024, 1], F32, kind="ExternalInput").ap()
    identd = nc.dram_tensor("identd", [128, 128], F32, kind="ExternalInput").ap()
    onescd = nc.dram_tensor("onescd", [128, 1], F32, kind="ExternalInput").ap()
    onesrd = nc.dram_tensor("onesrd", [1, 128], F32, kind="ExternalInput").ap()
    selLd = nc.dram_tensor("selLd", [8, 512], F32, kind="ExternalInput").ap()
    selEd = nc.dram_tensor("selEd", [8, 256], F32, kind="ExternalInput").ap()
    epsd = nc.dram_tensor("epsd", [128, 1], F32, kind="ExternalInput").ap()
    xout = nc.dram_tensor("xout", [SPC, D, T], F32, kind="ExternalOutput").ap()
    dbg = {}
    if debug2:
        for nm, shp in [("pm_ws", [128, 16]), ("pm_wqk", [128, 4096]),
                        ("pm_wov", [128, 4096]), ("pm_kvl", [128, 2048]),
                        ("pm_wov0", [128, 4096]), ("pm_lnrb", [8, 512])]:
            dbg[nm] = nc.dram_tensor(nm, shp, F32, kind="ExternalOutput").ap()
    if debug:
        for nm, shp in [("d_kvl", [128, T]), ("d_kvt", [128, NKC * 65]),
                        ("d_qt", [128, 4 * QW]), ("d_lr", [8, QW]),
                        ("d_ng", [128, 8]), ("d_on", [128, 4 * QW]),
                        ("d_ab", [128, 8 * HW]), ("d_lnrb", [8, 512]),
                        ("d_gf", [128, 16]), ("d_hs", [128, 2 * HW]),
                        ("d_wqk", [128, 512]), ("d_wov", [128, 1024]),
                        ("d_kvm2", [128, 2]), ("d_gsc", [128, 8]),
                        ("d_kvm_a", [128, 1]), ("d_kvm_b", [128, 1]),
                        ("d_xa", [128, T]), ("d_ltk", [128, 16]),
                        ("d_geg", [128, 8]), ("d_gt2", [128, 8]),
                        ("d_gmsk", [128, 8]), ("d_gem", [128, 8]),
                        ("d_gsc2", [128, 8]),
                        ("d_rr", [128, 512]), ("d_st", [34, 512]),
                        ("d_psb", [128, 512]), ("d_sqs", [128, 512])]:
            dbg[nm] = nc.dram_tensor(nm, shp, F32, kind="ExternalOutput").ap()

    P = Prog()

    with ExitStack() as stk:
        en = stk.enter_context

        def sb(name, shape):
            return en(nc.sbuf_tensor(name, shape, F32))

        xb = [sb(f"x{i}", [128, T]) for i in range(ND)]
        wqkT = [sb(f"wqk{i}", [128, 512]) for i in range(ND)]
        wovT = [sb(f"wov{j}", [128, 1024]) for j in range(NHC)]
        wTs = [sb(f"wt{i}", [128, 1024]) for i in range(ND)]
        aTs = [sb(f"aa{i}", [128, 256]) for i in range(ND)]
        bTs = [sb(f"bb{g}", [128, 1024]) for g in range(2)]
        rTs = [sb(f"rr{i}", [128, 8]) for i in range(ND)]
        bcols = sb("bcols", [128, 8])
        negws = sb("negws", [128, 4])
        negwskv = sb("negwskv", [128, 1])
        negwsW = sb("negwsw", [128, 8])
        negwsA = sb("negwsa", [128, 2])
        negwsR = sb("negwsr", [8, 1])
        kvl = sb("kvl", [128, T])
        kvt = sb("kvt", [128, NKC * 65])
        lnrb = sb("lnrb", [8, 512])
        ident = sb("ident", [128, 128])
        onesc = sb("onesc", [128, 1])
        onesr = sb("onesr", [1, 128])
        selL = sb("selL", [8, 512])
        selE = sb("selE", [8, 256])
        negG = sb("negg", [128, 8])
        kvm = sb("kvm", [128, 1])
        kvm2 = sb("kvm2", [128, 2])
        epsc = sb("epsc", [128, 1])
        qt = [sb(f"q{j}", [128, 512]) for j in range(NHC)]
        pbuf = [sb(f"p{k}", [128, 512]) for k in range(2)]
        pbuf2 = [sb(f"pp{k}", [128, 512]) for k in range(2)]
        pball = [pbuf[0], pbuf[1], pbuf2[0], pbuf2[1]]
        sqs = sb("sqs", [128, 512])
        rrep = sb("rrep", [128, 512])
        brep = sb("brep", [128, 512])
        ab = [sb(f"ab{i}", [128, 512]) for i in range(ND)]
        stat = sb("stat", [34, 512])
        lrows = sb("lrows", [8, 512])
        rl = sb("rl", [8, 512])
        gate8 = rl
        hsb = [pbuf2[0], pbuf2[1]]
        HSK = ['pb2', 'pb3']
        ltk = sb("ltk", [128, 32])
        geg = sb("geg", [128, 8])
        gt2 = sb("gt2", [128, 8])
        gmsk = sb("gmsk", [128, 8])
        gem = sb("gem", [128, 8])
        gsc = sb("gsc", [128, 8])
        gfin = sb("gfin", [128, 32])
        m1t = sb("m1t", [128, 1])
        nm1t = sb("nm1t", [128, 1])
        m2t = sb("m2t", [128, 1])
        lsum = sb("lsum", [128, 1])
        s2t = sb("s2t", [128, 1])
        dent = sb("dent", [128, 1])
        rdent = sb("rdent", [128, 1])
        zcol = sb("zcol", [128, 1])

        ps = [en(nc.psum_tensor(f"ps{k}", [128, 512], F32)) for k in range(8)]
        PK = [f"ps{k}" for k in range(8)]

        sems = {}
        for sn in ('t', 'v', 's', 'g', 'ld', 'ot', 'gd'):
            sems[sn] = en(nc.semaphore(sn))
        block = en(nc.Block())

        def ld(fn, writes):
            return P.emit('Y', fn, (), writes, dma='ld')

        def st(fn, reads):
            return P.emit('Y', fn, reads, (), dma='ot')

        def gd(fn, reads=(), writes=()):
            mk = P.emit('G', fn, reads, writes, dma='gd')
            P.items.append(('G', ((mk[0], mk[1]),), None, None, 0))
            return mk

        ld(lambda E: E.dma_start(out=ident[:], in_=identd[:, :]), ['ident'])
        ld(lambda E: E.dma_start(out=onesc[:], in_=onescd[:, :]), ['onesc'])
        ld(lambda E: E.dma_start(out=onesr[:], in_=onesrd[:, :]), ['onesr'])
        ld(lambda E: E.dma_start(out=selL[:], in_=selLd[:, :]), ['selL'])
        ld(lambda E: E.dma_start(out=selE[:], in_=selEd[:, :]), ['selE'])
        ld(lambda E: E.dma_start(out=epsc[:], in_=epsd[:, :]), ['epsc'])
        def wkv_ap(i):
            return (pbuf[0] if i < 4 else pbuf2[0])[:, (i % 4) * 128:
                                                    (i % 4) * 128 + 128]

        def wkv_key(i):
            return 'pb0' if i < 4 else 'pb2'

        def load_wkv():
            for i in range(ND):
                ld(lambda E, i=i: E.dma_start(
                    out=wkv_ap(i), in_=wkvT2d[i * 128:(i + 1) * 128, :]),
                    [wkv_key(i)])
            P.dma_fence('Y', 'ld', ['pb0', 'pb2'])
        load_wkv()
        for i in range(ND):
            ld(lambda E, i=i: E.dma_start(
                out=wTs[i][:], in_=wTd[i * 128:(i + 1) * 128, :]), [f'wt{i}'])
            ld(lambda E, i=i: E.dma_start(
                out=aTs[i][:], in_=aTd[i * 128:(i + 1) * 128, :]), [f'aa{i}'])
            ld(lambda E, i=i: E.dma_start(
                out=rTs[i][:], in_=rTd[i * 128:(i + 1) * 128, :]), [f'rr{i}'])
        for g in range(2):
            ld(lambda E, g=g: E.dma_start(
                out=bTs[g][:], in_=bTd[g * 128:(g + 1) * 128, :]), [f'bb{g}'])
        for oc in range(ND):
            ld(lambda E, oc=oc: E.dma_start(
                out=bcols[:, oc:oc + 1], in_=bcold[oc * 128:(oc + 1) * 128, :]),
                ['bcols'])
        for i in range(ND):
            for q in range(NQ):
                ld(lambda E, i=i, q=q: E.dma_start(
                    out=xb[i][:, q * QW:(q + 1) * QW],
                    in_=xin[0, i * 128:(i + 1) * 128, q * QW:(q + 1) * QW]),
                    [f'x{i}q{q}'])
        P.dma_fence('Y', 'ld',
                    ['ident', 'onesc', 'onesr', 'selL', 'selE', 'epsc',
                     'bcols', 'bb0', 'bb1']
                    + [f'wt{i}' for i in range(ND)]
                    + [f'aa{i}' for i in range(ND)]
                    + [f'rr{i}' for i in range(ND)]
                    + [f'x{i}q{q}' for i in range(ND) for q in range(NQ)])

        P.emit('V', lambda E: E.memset(stat[:, :], 0.0), (), ['stat'])
        P.emit('V', lambda E: E.memset(lnrb[:, :], 0.0), (), ['lnrb'])
        P.emit('G', lambda E: E.memset(kvt[:, :], 1.0), (), ['kvt'])
        P.emit('V', lambda E: E.memset(lrows[:, :], 1.0), (), ['lrows'])
        P.emit('V', lambda E: E.memset(gsc[:, :], 0.0), (), ['gsc'])
        P.emit('V', lambda E: E.memset(zcol[:, :], 0.0), (), ['zcol'])

        for h in range(H):
            ld(lambda E, h=h: E.dma_start(
                out=qt[0][:, :], in_=wqd[h * 256:h * 256 + 128, 0:512]),
                ['qt0h0', 'qt0h1'])
            ld(lambda E, h=h: E.dma_start(
                out=qt[1][:, :], in_=wqd[h * 256:h * 256 + 128, 512:1024]),
                ['qt1h0', 'qt1h1'])
            ld(lambda E, h=h: E.dma_start(
                out=qt[2][:, :], in_=wqd[h * 256 + 128:h * 256 + 256, 0:512]),
                ['qt2h0', 'qt2h1'])
            ld(lambda E, h=h: E.dma_start(
                out=qt[3][:, :], in_=wqd[h * 256 + 128:h * 256 + 256, 512:1024]),
                ['qt3h0', 'qt3h1'])
            ld(lambda E, h=h: E.dma_start(
                out=pbuf[0][:, 0:64], in_=wkupd[h * 256:h * 256 + 128, :]),
                ['pb0'])
            ld(lambda E, h=h: E.dma_start(
                out=pbuf[0][:, 64:128],
                in_=wkupd[h * 256 + 128:h * 256 + 256, :]), ['pb0'])
            P.dma_fence('Y', 'ld', ['pb0'] + [f'qt{k}h0' for k in range(4)]
                        + [f'qt{k}h1' for k in range(4)])
            for i in range(ND):
                bk = i % 2
                q01 = i // 4
                c0 = (i % 4) * 128
                P.emit('T', lambda E, bk=bk, q01=q01, c0=c0: E.matmul(
                    ps[bk][0:128, 0:64], qt[q01][:, c0:c0 + 128],
                    pbuf[0][:, 0:64], start=True, stop=False),
                    [f'qt{q01}h0', f'qt{q01}h1', 'pb0'], [PK[bk]])
                P.emit('T', lambda E, bk=bk, q01=q01, c0=c0: E.matmul(
                    ps[bk][0:128, 0:64], qt[2 + q01][:, c0:c0 + 128],
                    pbuf[0][:, 64:128], start=False, stop=True),
                    [f'qt{2+q01}h0', f'qt{2+q01}h1', 'pb0'], [PK[bk]])
                P.emit('V', lambda E, bk=bk, i=i, h=h: E.tensor_copy(
                    wqkT[i][:, h * 64:(h + 1) * 64], ps[bk][0:128, 0:64]),
                    [PK[bk]], [f'wqkT{i}'])
        for jp in range(NHC):
            ld(lambda E, jp=jp: E.dma_start(
                out=pbuf[1][:, 0:64], in_=wvupd[jp * 512:jp * 512 + 128, :]),
                ['pb1'])
            ld(lambda E, jp=jp: E.dma_start(
                out=pbuf[1][:, 64:128],
                in_=wvupd[jp * 512 + 128:jp * 512 + 256, :]), ['pb1'])
            ld(lambda E, jp=jp: E.dma_start(
                out=pbuf[1][:, 128:192],
                in_=wvupd[jp * 512 + 256:jp * 512 + 384, :]), ['pb1'])
            ld(lambda E, jp=jp: E.dma_start(
                out=pbuf[1][:, 192:256],
                in_=wvupd[jp * 512 + 384:jp * 512 + 512, :]), ['pb1'])
            P.dma_fence('Y', 'ld', ['pb1'])
            for ohalf in range(2):
                stg = qt if ohalf == 0 else ab
                sgk = (lambda k: [f'qt{k}h0', f'qt{k}h1']) if ohalf == 0 \
                    else (lambda k: [f'ab{k}'])
                for k2 in range(2):
                    ld(lambda E, jp=jp, k2=k2, ohalf=ohalf, stg=stg:
                       E.dma_start(
                        out=stg[k2][:, :],
                        in_=woTd[jp * 512 + k2 * 128:jp * 512 + (k2 + 1) * 128,
                                 ohalf * 512:(ohalf + 1) * 512]),
                        sgk(k2))
                    ld(lambda E, jp=jp, k2=k2, ohalf=ohalf, stg=stg:
                       E.dma_start(
                        out=stg[2 + k2][:, :],
                        in_=woTd[jp * 512 + 256 + k2 * 128:
                                 jp * 512 + 256 + (k2 + 1) * 128,
                                 ohalf * 512:(ohalf + 1) * 512]),
                        sgk(2 + k2))
                P.dma_fence('Y', 'ld',
                            sum((sgk(k) for k in range(4)), []))
                for hf in range(2):
                    bk = 4 + (2 * ohalf + hf) % 4
                    P.emit('T', lambda E, bk=bk, hf=hf, stg=stg: E.matmul(
                        ps[bk][0:64, :], pbuf[1][:, hf * 128:hf * 128 + 64],
                        stg[2 * hf][:, :], start=True, stop=False),
                        ['pb1'] + sgk(2 * hf), [PK[bk]])
                    P.emit('T', lambda E, bk=bk, hf=hf, stg=stg: E.matmul(
                        ps[bk][0:64, :],
                        pbuf[1][:, hf * 128 + 64:hf * 128 + 128],
                        stg[2 * hf + 1][:, :], start=False, stop=True),
                        ['pb1'] + sgk(2 * hf + 1), [PK[bk]])
                    if hf == 0:
                        P.emit('V', lambda E, bk=bk, jp=jp, ohalf=ohalf:
                               E.tensor_copy(
                            wovT[jp][0:64, ohalf * 512:(ohalf + 1) * 512],
                            ps[bk][0:64, :]), [PK[bk]], [f'wovT{jp}'])
                    else:
                        P.emit('V', lambda E, bk=bk: E.tensor_copy(
                            sqs[0:64, :], ps[bk][0:64, :]),
                            [PK[bk]], ['sqsA', 'sqsB'])
                        gd(lambda E, jp=jp, ohalf=ohalf: E.dma_start(
                            out=wovT[jp][64:128, ohalf * 512:(ohalf + 1) * 512],
                            in_=sqs[0:64, :]), ['sqsA', 'sqsB'],
                            [f'wovT{jp}'])
        if debug:
            st(lambda E: E.dma_start(out=dbg['d_wqk'][:, :], in_=wqkT[0][:, :]),
               ['wqkT0'])
            st(lambda E: E.dma_start(out=dbg['d_wov'][:, :], in_=wovT[0][:, :]),
               ['wovT0'])

        if debug2:
            for j in range(NHC):
                st(lambda E, j=j: E.dma_start(
                    out=dbg['pm_wov0'][:, j * 1024:(j + 1) * 1024],
                    in_=wovT[j][:, :]), [f'wovT{j}'])
        def colsum_neg(dst_fn, dst_key, lhs):
            n = len(lhs)
            for i, (lf, rkeys, kp) in enumerate(lhs):
                P.emit('T', lambda E, lf=lf, i=i, n=n, kp=kp: E.matmul(
                    ps[6][0:kp, 0:1], lf(), onesc[:, :],
                    start=(i == 0), stop=(i == n - 1)),
                    list(rkeys) + ['onesc'], [PK[6]])
            P.emit('S', lambda E, kp=lhs[0][2]: E.mul(
                dst_fn(), ps[6][0:kp, 0:1], -1.0), [PK[6]], [dst_key])

        for j in range(NHC):
            colsum_neg(lambda j=j: negws[:, j:j + 1], 'negws',
                       [(lambda i=i, j=j: wqkT[i][:, j * 128:(j + 1) * 128],
                         [f'wqkT{i}'], 128) for i in range(ND)])
        load_wkv()
        colsum_neg(lambda: negwskv[:, 0:1], 'negwskv',
                   [(lambda i=i: wkv_ap(i), [wkv_key(i)], 128)
                    for i in range(ND)])
        for oc in range(ND):
            colsum_neg(lambda oc=oc: negwsW[:, oc:oc + 1], 'negwsW',
                       [(lambda i=i, oc=oc: wTs[i][:, oc * 128:(oc + 1) * 128],
                         [f'wt{i}'], 128) for i in range(ND)])
        for g in range(2):
            colsum_neg(lambda g=g: negwsA[:, g:g + 1], 'negwsA',
                       [(lambda i=i, g=g: aTs[i][:, g * 128:(g + 1) * 128],
                         [f'aa{i}'], 128) for i in range(ND)])
        colsum_neg(lambda: negwsR[:, 0:1], 'negwsR',
                   [(lambda i=i: rTs[i][:, :], [f'rr{i}'], 8)
                    for i in range(ND)])

        def ln_stats(w, bS, bQ, store_q=None, scr=6):
            assert w == QW
            sqr = sqs[0:1, 0:w]
            msq = pbuf[1][0:1, 0:w]
            rrow = qt[0][0:1, 0:w]
            brow = qt[1][0:1, 0:w]
            rk = 'qt0h0'
            bk_ = 'qt1h0'
            P.emit('V', lambda E, w=w: E.tensor_copy(
                stat[0:1, 0:w], ps[bS][0:1, 0:w]), [PK[bS]], ['stat'])
            P.emit('V', lambda E, w=w: E.tensor_copy(
                stat[32:33, 0:w], ps[bQ][32:33, 0:w]), [PK[bQ]], ['stat'])
            P.emit('T', lambda E, w=w, scr=scr: E.matmul(
                ps[scr][0:1, 0:w], ident[0:33, 32:33], stat[0:33, 0:w],
                start=True, stop=True), ['stat', 'ident'], [PK[scr]])
            P.emit('V', lambda E, w=w, sqr=sqr, scr=scr: E.tensor_copy(
                sqr, ps[scr][0:1, 0:w]), [PK[scr]], ['sqsA', 'sqsB'])
            P.emit('S', lambda E, w=w, msq=msq: E.activation(
                msq, stat[0:1, 0:w], AF.Square, scale=INV_D),
                ['stat'], ['pb1'])
            P.emit('V', lambda E, sqr=sqr, msq=msq: E.scalar_tensor_tensor(
                msq, sqr, INV_D, msq, ALU.mult, ALU.subtract),
                [], ['pb1'], sreads=['sqsA', 'sqsB', 'pb1'])
            P.emit('S', lambda E, msq=msq: E.activation(
                msq, msq, AF.Ln, bias=epsc[0:1, 0:1]),
                ['pb1', 'epsc'], ['pb1'])
            P.emit('S', lambda E, msq=msq, rrow=rrow: E.activation(
                rrow, msq, AF.Exp, scale=-0.5),
                ['pb1'], [rk])
            P.emit('V', lambda E, rrow=rrow, brow=brow: E.scalar_tensor_tensor(
                brow, stat[0:1, 0:w], INV_D, rrow,
                ALU.mult, ALU.mult), [rk], [bk_], sreads=['stat'])
            P.emit('T', lambda E, w=w, rrow=rrow, scr=scr: E.matmul(
                ps[scr][:, 0:w], onesr[:, :], rrow, start=True, stop=True),
                [rk, 'onesr'], [PK[scr]])
            P.emit('V', lambda E, w=w, scr=scr: E.tensor_copy(
                rrep[:, 0:w], ps[scr][:, 0:w]), [PK[scr]], ['rrep'])
            P.emit('T', lambda E, w=w, brow=brow, scr=scr: E.matmul(
                ps[scr][:, 0:w], onesr[:, :], brow, start=True, stop=True),
                [bk_, 'onesr'], [PK[scr]])
            P.emit('V', lambda E, w=w, scr=scr: E.tensor_copy(
                brep[:, 0:w], ps[scr][:, 0:w]), [PK[scr]], ['brep'])
            if store_q is not None:
                gd(lambda E, q=store_q, rrow=rrow: E.dma_start(
                    out=lnrb[q:q + 1, :], in_=rrow), [rk], ['lnrb'])
                gd(lambda E, q=store_q, brow=brow: E.dma_start(
                    out=lnrb[4 + q:5 + q, :], in_=brow), [bk_], ['lnrb'])

        for sq_i in range(SPC):
            XK = lambda i, q: f'x{i}q{q}'
            load_wkv()
            for q in range(NQ):
                qs = slice(q * QW, (q + 1) * QW)
                for i in range(ND):
                    sqt = sqs if i % 2 == 0 else pbuf[1]
                    sk = ['sqsA', 'sqsB'] if i % 2 == 0 else ['pb1']
                    P.emit('S', lambda E, i=i, qs=qs, sqt=sqt: E.activation(
                        sqt[:, :], xb[i][:, qs], AF.Square),
                        [XK(i, q)], sk)
                    P.emit('T', lambda E, i=i, qs=qs: E.matmul(
                        ps[0][0:1, 0:QW], onesc[:, :], xb[i][:, qs],
                        start=(i == 0), stop=(i == ND - 1),
                        tile_position=(0, 0)),
                        [XK(i, q), 'onesc'], [PK[0]])
                    P.emit('T', lambda E, sqt=sqt, i=i: E.matmul(
                        ps[1][32:33, 0:QW], onesc[:, :], sqt[:, :],
                        start=(i == 0), stop=(i == ND - 1),
                        tile_position=(0, 32)), sk + ['onesc'], [PK[1]])
                ln_stats(QW, 0, 1, store_q=q)
                for i in range(ND):
                    P.emit('T', lambda E, i=i, qs=qs: E.matmul(
                        ps[2][:, 0:QW], wkv_ap(i), xb[i][:, qs],
                        start=(i == 0), stop=(i == ND - 1)),
                        [wkv_key(i), XK(i, q)], [PK[2]])
                P.emit('V', lambda E, qs=qs: E.tensor_tensor(
                    kvl[:, qs], ps[2][:, 0:QW], rrep[:, :], ALU.mult),
                    [PK[2], 'rrep'], [f'kvl{q}'])
                P.emit('V', lambda E, qs=qs: E.scalar_tensor_tensor(
                    kvl[:, qs], brep[:, :], negwskv[:, 0:1], kvl[:, qs],
                    ALU.mult, ALU.add),
                    [f'kvl{q}'], [f'kvl{q}'], sreads=['brep', 'negwskv'])
            KVLK = [f'kvl{q}' for q in range(NQ)]
            for cb in range(2):
                bk = 3 + cb
                for ci in range(8):
                    c = cb * 8 + ci
                    P.emit('T', lambda E, bk=bk, c=c, ci=ci: E.transpose(
                        ps[bk][:, ci * 64:(ci + 1) * 64],
                        kvl[0:64, c * 128:(c + 1) * 128], ident[0:64, 0:64]),
                        KVLK + ['ident'], [PK[bk]] if ci == 0 else [PK[bk]])
                for ci in range(8):
                    c = cb * 8 + ci
                    P.emit('V', lambda E, bk=bk, c=c, ci=ci: E.tensor_copy(
                        kvt[:, c * 65:c * 65 + 64],
                        ps[bk][:, ci * 64:(ci + 1) * 64]), [PK[bk]], ['kvt'])
            P.emit('V', lambda E: E.memset(kvm2[:, :], 0.0), (), ['kvm2'])
            P.emit('V', lambda E: E.reduce_max(
                kvm2[0:64, 0:1], kvl[0:64, :], axis=AX.X,
                apply_absolute_value=True), KVLK, ['kvm2'])
            P.emit('V', lambda E: E.reduce_max(
                kvm2[64:128, 1:2], kvl[64:128, :], axis=AX.X,
                apply_absolute_value=True), KVLK, ['kvm2'])
            if debug and sq_i == 0:
                st(lambda E: E.dma_start(out=dbg['d_kvl'][:, :], in_=kvl[:, :]),
                   KVLK)
                st(lambda E: E.dma_start(out=dbg['d_kvt'][:, :], in_=kvt[:, :]),
                   ['kvt'])
                st(lambda E: E.dma_start(out=dbg['d_lnrb'][:, :],
                                         in_=lnrb[:, :]), ['lnrb'])

            for q in range(NQ):
                qs = slice(q * QW, (q + 1) * QW)
                for j in range(2):
                    bk = 2 + j % 2
                    for i in range(ND):
                        P.emit('T', lambda E, bk=bk, i=i, j=j, qs=qs: E.matmul(
                            ps[bk][:, 0:QW], wqkT[i][:, j * 128:(j + 1) * 128],
                            xb[i][:, qs], start=(i == 0), stop=(i == ND - 1)),
                            [f'wqkT{i}', XK(i, q)], [PK[bk]])
                P.emit('T', lambda E, q=q: E.matmul(
                    ps[0][0:1, 0:QW], ident[0:8, q:q + 1], lnrb[0:8, :],
                    start=True, stop=True), ['lnrb', 'ident'], [PK[0]])
                P.emit('V', lambda E: E.tensor_copy(
                    qt[0][0:1, :], ps[0][0:1, 0:QW]), [PK[0]], ['qt0h0'])
                P.emit('T', lambda E: E.matmul(
                    ps[1][:, 0:QW], onesr[:, :], qt[0][0:1, :],
                    start=True, stop=True), ['qt0h0', 'onesr'], [PK[1]])
                P.emit('V', lambda E: E.tensor_copy(
                    rrep[:, :], ps[1][:, 0:QW]), [PK[1]], ['rrep'])
                P.emit('T', lambda E, q=q: E.matmul(
                    ps[0][0:1, 0:QW], ident[0:8, 4 + q:5 + q], lnrb[0:8, :],
                    start=True, stop=True), ['lnrb', 'ident'], [PK[0]])
                P.emit('V', lambda E: E.tensor_copy(
                    qt[0][0:1, :], ps[0][0:1, 0:QW]), [PK[0]], ['qt0h0'])
                P.emit('T', lambda E: E.matmul(
                    ps[1][:, 0:QW], onesr[:, :], qt[0][0:1, :],
                    start=True, stop=True), ['qt0h0', 'onesr'], [PK[1]])
                P.emit('V', lambda E: E.tensor_copy(
                    brep[:, :], ps[1][:, 0:QW]), [PK[1]], ['brep'])
                for j in range(NHC):
                    bk = 2 + j % 2
                    if j >= 2:
                        for i in range(ND):
                            P.emit('T', lambda E, bk=bk, i=i, j=j, qs=qs:
                                   E.matmul(
                                ps[bk][:, 0:QW],
                                wqkT[i][:, j * 128:(j + 1) * 128],
                                xb[i][:, qs], start=(i == 0),
                                stop=(i == ND - 1)),
                                [f'wqkT{i}', XK(i, q)], [PK[bk]])
                    P.emit('V', lambda E, bk=bk, j=j: E.tensor_tensor(
                        qt[j][:, :], ps[bk][:, 0:QW], rrep[:, :], ALU.mult),
                        [PK[bk], 'rrep'], [f'qt{j}h0', f'qt{j}h1'])
                    P.emit('V', lambda E, j=j: E.scalar_tensor_tensor(
                        qt[j][:, :], brep[:, :], negws[:, j:j + 1], qt[j][:, :],
                        ALU.mult, ALU.add),
                        [f'qt{j}h0', f'qt{j}h1'],
                        [f'qt{j}h0', f'qt{j}h1'], sreads=['brep', 'negws'])
                P.emit('V', lambda E: E.memset(ps[5][:, :], -1e30),
                       (), [PK[5]])
                for j in range(NHC):
                    P.emit('S', lambda E, j=j: E.activation(
                        sqs[:, :], qt[j][:, :], AF.Abs),
                        [f'qt{j}h0', f'qt{j}h1'], ['sqsA', 'sqsB'])
                    P.emit('T', lambda E, j=j: E.matmul(
                        ps[5][32 * j:32 * j + 2, 0:QW], kvm2[:, :], sqs[:, :],
                        start=True, stop=True,
                        tile_position=(0, 32 * j)),
                        ['kvm2', 'sqsA', 'sqsB'], [PK[5]])
                if debug and sq_i == 0 and q == 0:
                    P.emit('V', lambda E: E.tensor_copy(
                        pbuf[0][:, :], ps[5][:, 0:QW]), [PK[5]], ['pb0'])
                    st(lambda E: E.dma_start(out=dbg['d_psb'][:, :],
                                             in_=pbuf[0][:, :]), ['pb0'])
                    st(lambda E: E.dma_start(out=dbg['d_sqs'][:, :],
                                             in_=sqs[:, :]), ['sqsA', 'sqsB'])
                    st(lambda E: E.dma_start(out=dbg['d_kvm2'][:, :],
                                             in_=kvm2[:, :]), ['kvm2'])
                P.emit('V', lambda E: E.reduce_max(
                    gsc[:, 0:1], ps[5][:, 0:QW], axis=AX.X, negate=True),
                    [PK[5]], ['gsc'])
                P.emit('S', lambda E: E.mul(gsc[:, 1:2], gsc[:, 0:1], SCALE),
                       ['gsc'], ['gsc'])
                if debug and sq_i == 0 and q == 0:
                    st(lambda E: E.dma_start(out=dbg['d_gsc'][:, :],
                                             in_=gsc[:, :]), ['gsc'])
                for h in range(H):
                    r_h = 32 * (h // 2) + (h % 2)
                    eb = h % 2
                    rb = 6 + h % 2
                    P.emit('T', lambda E, r_h=r_h, eb=eb: E.matmul(
                        ps[eb][0:1, 0:1], ident[0:128, r_h:r_h + 1],
                        gsc[0:128, 1:2], start=True, stop=True),
                        ['gsc', 'ident'], [PK[eb]])
                    P.emit('V', lambda E, h=h, eb=eb: E.tensor_copy(
                        stat[0:1, 100 + h:101 + h], ps[eb][0:1, 0:1]),
                        [PK[eb]], [f'gt{h}' if False else 'stat'])
                for h in range(H):
                    rb = 6 + h % 2
                    P.emit('T', lambda E, h=h, rb=rb: E.matmul(
                        ps[rb][:, 0:1], onesr[:, :],
                        stat[0:1, 100 + h:101 + h], start=True, stop=True),
                        ['stat', 'onesr'], [PK[rb]])
                    P.emit('V', lambda E, h=h, rb=rb: E.tensor_copy(
                        negG[:, h:h + 1], ps[rb][:, 0:1]), [PK[rb]], ['negG'])
                if debug and sq_i == 0 and q == 0:
                    st(lambda E: E.dma_start(out=dbg['d_ng'][:, :],
                                             in_=negG[:, :]), ['negG'])
                    for j in range(NHC):
                        st(lambda E, j=j: E.dma_start(
                            out=dbg['d_qt'][:, j * QW:(j + 1) * QW],
                            in_=qt[j][:, :]), [f'qt{j}h0', f'qt{j}h1'])

                for hp in range(NHC):
                    j = hp
                    qke, qko = f'qt{j}h0', f'qt{j}h1'
                    obe = 4 + hp % 2
                    obo = 6 + hp % 2

                    def eS(c, hf, j=j):
                        bk = (c % 2) + 2 * hf
                        qk = f'qt{j}h{hf}'
                        return P.emit('T', lambda E, bk=bk, c=c, hf=hf, j=j:
                                      E.matmul(
                            ps[bk][:, 0:QW],
                            kvl[64 * hf:64 * hf + 64, c * 128:(c + 1) * 128],
                            qt[j][64 * hf:64 * hf + 64, :],
                            start=True, stop=True,
                            tile_position=(64 * hf, 0)),
                            KVLK + [qk], [PK[bk]])

                    def eE(c, hf, hp=hp):
                        bk = (c % 2) + 2 * hf
                        pbi = (c % 2) + 2 * hf
                        h = 2 * hp + hf
                        return P.emit('S', lambda E, bk=bk, pbi=pbi, h=h:
                                      E.activation(
                            pball[pbi][:, :], ps[bk][:, 0:QW], AF.Exp,
                            bias=negG[:, h:h + 1], scale=SCALE),
                            [PK[bk], 'negG'], [f'pb{pbi}'])

                    def eO(c, hf, obe=obe, obo=obo):
                        ob = obe if hf == 0 else obo
                        pbi = (c % 2) + 2 * hf
                        return P.emit('T', lambda E, c=c, ob=ob, pbi=pbi:
                                      E.matmul(
                            ps[ob][0:65, 0:QW], kvt[:, c * 65:(c + 1) * 65],
                            pball[pbi][:, :],
                            start=(c == 0), stop=(c == NKC - 1)),
                            [f'pb{pbi}', 'kvt'], [PK[ob]])

                    for c in range(2):
                        eS(c, 0)
                        eS(c, 1)
                        eE(c, 0)
                        eE(c, 1)
                    for c in range(2, NKC):
                        eO(c - 2, 0)
                        eO(c - 2, 1)
                        eS(c, 0)
                        eS(c, 1)
                        eE(c, 0)
                        eE(c, 1)
                    for c in (NKC - 2, NKC - 1):
                        eO(c, 0)
                        eO(c, 1)
                    P.emit('V', lambda E, obe=obe, j=j: E.tensor_copy(
                        qt[j][0:64, :], ps[obe][0:64, 0:QW]),
                        [PK[obe]], [qke])
                    P.emit('V', lambda E, obe=obe: E.tensor_copy(
                        sqs[64:65, :], ps[obe][64:65, 0:QW]),
                        [PK[obe]], ['sqsA', 'sqsB'])
                    gd(lambda E, he=2 * hp: E.dma_start(
                        out=lrows[he:he + 1, :], in_=sqs[64:65, :]),
                        ['sqsA', 'sqsB'], ['lrows'])
                    P.emit('V', lambda E, obo=obo: E.tensor_copy(
                        sqs[0:64, :], ps[obo][0:64, 0:QW]),
                        [PK[obo]], ['sqsA', 'sqsB'])
                    gd(lambda E, j=j: E.dma_start(
                        out=qt[j][64:128, :], in_=sqs[0:64, :]),
                        ['sqsA', 'sqsB'], [qko])
                    P.emit('V', lambda E, obo=obo: E.tensor_copy(
                        sqs[64:65, :], ps[obo][64:65, 0:QW]),
                        [PK[obo]], ['sqsA', 'sqsB'])
                    gd(lambda E, ho=2 * hp + 1: E.dma_start(
                        out=lrows[ho:ho + 1, :], in_=sqs[64:65, :]),
                        ['sqsA', 'sqsB'], ['lrows'])
                P.emit('V', lambda E: E.reciprocal(rl[0:8, :], lrows[0:8, :]),
                       ['lrows'], ['rl'])
                for j in range(NHC):
                    P.emit('T', lambda E, j=j: E.matmul(
                        ps[4][:, 0:QW], selL[:, j * 128:(j + 1) * 128],
                        rl[0:8, :], start=True, stop=True),
                        ['rl', 'selL'], [PK[4]])
                    P.emit('V', lambda E, j=j: E.tensor_tensor(
                        qt[j][:, :], qt[j][:, :], ps[4][:, 0:QW], ALU.mult),
                        [PK[4], f'qt{j}h0', f'qt{j}h1'],
                        [f'qt{j}h0', f'qt{j}h1'])
                if debug and sq_i == 0 and q == 0:
                    st(lambda E: E.dma_start(out=dbg['d_lr'][:, :],
                                             in_=lrows[0:8, :]), ['lrows'])
                    for j in range(NHC):
                        st(lambda E, j=j: E.dma_start(
                            out=dbg['d_on'][:, j * QW:(j + 1) * QW],
                            in_=qt[j][:, :]), [f'qt{j}h0', f'qt{j}h1'])

                cs = slice(q * QW, (q + 1) * QW)
                for oc in range(ND):
                    bk = 5 + oc % 2
                    sqt = sqs if oc % 2 == 0 else pbuf[1]
                    sk = ['sqsA', 'sqsB'] if oc % 2 == 0 else ['pb1']
                    for j in range(NHC):
                        P.emit('T', lambda E, bk=bk, j=j, oc=oc: E.matmul(
                            ps[bk][:, 0:QW],
                            wovT[j][:, oc * 128:(oc + 1) * 128],
                            qt[j][:, :], start=(j == 0),
                            stop=(j == NHC - 1)),
                            [f'wovT{j}', f'qt{j}h0', f'qt{j}h1'],
                            [PK[bk]])
                    P.emit('S', lambda E, bk=bk, oc=oc: E.activation(
                        ab[oc][:, :], ps[bk][:, 0:QW], AF.Identity),
                        [PK[bk]], [f'ab{oc}'])
                    P.emit('S', lambda E, bk=bk, sqt=sqt: E.activation(
                        sqt[:, :], ps[bk][:, 0:QW], AF.Square),
                        [PK[bk]], sk)
                    P.emit('T', lambda E, oc=oc: E.matmul(
                        ps[0][0:1, 0:QW], onesc[:, :], ab[oc][:, :],
                        start=(oc == 0), stop=(oc == ND - 1),
                        tile_position=(0, 0)),
                        [f'ab{oc}', 'onesc'], [PK[0]])
                    P.emit('T', lambda E, sqt=sqt, oc=oc: E.matmul(
                        ps[1][32:33, 0:QW], onesc[:, :], sqt[:, :],
                        start=(oc == 0), stop=(oc == ND - 1),
                        tile_position=(0, 32)), sk + ['onesc'],
                        [PK[1]])
                ln_stats(QW, 0, 1)
                for i in range(ND):
                    P.emit('V', lambda E, i=i: E.tensor_tensor(
                        ab[i][:, :], ab[i][:, :], rrep[:, :], ALU.mult),
                        ['rrep', f'ab{i}'], [f'ab{i}'])
                    P.emit('V', lambda E, i=i: E.tensor_tensor(
                        ab[i][:, :], ab[i][:, :], brep[:, :],
                        ALU.subtract), ['brep', f'ab{i}'], [f'ab{i}'])
                    P.emit('V', lambda E, i=i, cs=cs: E.tensor_tensor(
                        xb[i][:, cs], xb[i][:, cs], ab[i][:, :], ALU.add),
                        [f'ab{i}', XK(i, q)], [XK(i, q)])
                if debug and sq_i == 0 and q == 0:
                    for oc in range(ND):
                        st(lambda E, oc=oc: E.dma_start(
                            out=dbg['d_ab'][:, oc * HW:(oc + 1) * HW],
                            in_=ab[oc][:, 0:HW]), [f'ab{oc}'])

            if debug and sq_i == 0:
                st(lambda E: E.dma_start(out=dbg['d_xa'][:, :],
                                         in_=xb[0][:, :]),
                   [XK(0, qq) for qq in range(NQ)])
            for q in range(NQ):
                if True:
                    cs = slice(q * QW, (q + 1) * QW)
                    for i in range(ND):
                        sqt = sqs if i % 2 == 0 else pbuf[1]
                        sk = ['sqsA', 'sqsB'] if i % 2 == 0 else ['pb1']
                        P.emit('S', lambda E, i=i, cs=cs, sqt=sqt: E.activation(
                            sqt[:, :], xb[i][:, cs], AF.Square),
                            [XK(i, q)], sk)
                        P.emit('T', lambda E, i=i, cs=cs: E.matmul(
                            ps[0][0:1, 0:QW], onesc[:, :], xb[i][:, cs],
                            start=(i == 0), stop=(i == ND - 1),
                            tile_position=(0, 0)),
                            [XK(i, q), 'onesc'], [PK[0]])
                        P.emit('T', lambda E, sqt=sqt, i=i: E.matmul(
                            ps[1][32:33, 0:QW], onesc[:, :], sqt[:, :],
                            start=(i == 0), stop=(i == ND - 1),
                            tile_position=(0, 32)), sk + ['onesc'], [PK[1]])
                    for i in range(ND):
                        P.emit('T', lambda E, i=i, cs=cs: E.matmul(
                            ps[2][0:8, 0:QW], rTs[i][:, :], xb[i][:, cs],
                            start=(i == 0), stop=(i == ND - 1)),
                            [f'rr{i}', XK(i, q)], [PK[2]])
                    for g in range(2):
                        hb = 6 + g
                        for i in range(ND):
                            P.emit('T', lambda E, i=i, g=g, hb=hb, cs=cs:
                                   E.matmul(
                                ps[hb][:, 0:QW],
                                aTs[i][:, g * 128:(g + 1) * 128],
                                xb[i][:, cs], start=(i == 0),
                                stop=(i == ND - 1)),
                                [f'aa{i}', XK(i, q)], [PK[hb]])
                    for oc in range(2):
                        bk = 4 + oc
                        for i in range(ND):
                            P.emit('T', lambda E, bk=bk, i=i, oc=oc, cs=cs:
                                   E.matmul(
                                ps[bk][:, 0:QW],
                                wTs[i][:, oc * 128:(oc + 1) * 128],
                                xb[i][:, cs], start=(i == 0),
                                stop=(i == ND - 1)),
                                [f'wt{i}', XK(i, q)], [PK[bk]])
                    ln_stats(QW, 0, 1, scr=0)
                    P.emit('V', lambda E: E.tensor_tensor(
                        gate8[0:8, :], ps[2][0:8, 0:QW], rrep[0:8, :],
                        ALU.mult), [PK[2], 'rrep'], ['rl'])
                    P.emit('V', lambda E: E.scalar_tensor_tensor(
                        gate8[0:8, :], brep[0:8, :], negwsR[0:8, 0:1],
                        gate8[0:8, :], ALU.mult, ALU.add),
                        ['rl'], ['rl'], sreads=['brep', 'negwsR'])
                    for tb4 in range(4):
                        P.emit('T', lambda E, tb4=tb4: E.transpose(
                            ps[3][:, tb4 * 8:(tb4 + 1) * 8],
                            gate8[0:8, tb4 * 128:(tb4 + 1) * 128],
                            ident[0:8, 0:8]),
                            ['rl', 'ident'], [PK[3]])
                    P.emit('V', lambda E: E.tensor_copy(
                        ltk[:, :], ps[3][:, 0:32]), [PK[3]], ['ltk'])
                    if debug and sq_i == 0 and q == 0:
                        st(lambda E: E.dma_start(out=dbg['d_ltk'][:, :],
                                                 in_=ltk[:, 0:16]), ['ltk'])
                        st(lambda E: E.dma_start(out=dbg['d_rr'][:, :],
                                                 in_=rrep[:, :]), ['rrep'])
                        st(lambda E: E.dma_start(out=dbg['d_st'][:, :],
                                                 in_=stat[:, :]), ['stat'])
                    for tb in range(4):
                        tsl = slice(tb * 8, (tb + 1) * 8)
                        P.emit('V', lambda E, tsl=tsl: E.reduce_max(
                            m1t[:, 0:1], ltk[:, tsl], axis=AX.X),
                            ['ltk'], ['m1t'])
                        P.emit('V', lambda E, tsl=tsl: E.reduce_max(
                            nm1t[:, 0:1], ltk[:, tsl], axis=AX.X, negate=True),
                            ['ltk'], ['nm1t'])
                        P.emit('S', lambda E, tsl=tsl: E.activation(
                            geg[:, :], ltk[:, tsl], AF.Exp, bias=nm1t[:, 0:1]),
                            ['ltk', 'nm1t'], ['geg'])
                        P.emit('V', lambda E: E.reduce_sum(
                            lsum[:, 0:1], geg[:, :], axis=AX.X),
                            ['geg'], ['lsum'])
                        P.emit('V', lambda E, tsl=tsl: E.tensor_scalar(
                            gt2[:, :], ltk[:, tsl], m1t[:, 0:1], None,
                            ALU.subtract), ['ltk'], ['gt2'], sreads=['m1t'])
                        P.emit('V', lambda E: E.tensor_scalar(
                            gmsk[:, :], gt2[:, :], zcol[:, 0:1], None,
                            ALU.is_ge), ['gt2'], ['gmsk'], sreads=['zcol'])
                        P.emit('V', lambda E, tsl=tsl: E.scalar_tensor_tensor(
                            gt2[:, :], gmsk[:, :], -1e30, ltk[:, tsl],
                            ALU.mult, ALU.add), ['ltk'], ['gt2'],
                            sreads=['gmsk'])
                        P.emit('V', lambda E: E.reduce_max(
                            m2t[:, 0:1], gt2[:, :], axis=AX.X),
                            ['gt2'], ['m2t'])
                        P.emit('V', lambda E, tsl=tsl: E.tensor_scalar(
                            gmsk[:, :], ltk[:, tsl], m2t[:, 0:1], None,
                            ALU.subtract), ['ltk'], ['gmsk'], sreads=['m2t'])
                        P.emit('V', lambda E: E.tensor_scalar(
                            gmsk[:, :], gmsk[:, :], zcol[:, 0:1], None,
                            ALU.is_ge), ['gmsk'], ['gmsk'], sreads=['zcol'])
                        P.emit('V', lambda E: E.tensor_tensor(
                            gem[:, :], geg[:, :], gmsk[:, :], ALU.mult),
                            ['geg', 'gmsk'], ['gem'])
                        P.emit('V', lambda E: E.reduce_sum(
                            s2t[:, 0:1], gem[:, :], axis=AX.X),
                            ['gem'], ['s2t'])
                        P.emit('S', lambda E: E.mul(
                            dent[:, 0:1], lsum[:, 0:1], GEPS),
                            ['lsum'], ['dent'])
                        P.emit('V', lambda E: E.tensor_tensor(
                            dent[:, 0:1], dent[:, 0:1], s2t[:, 0:1], ALU.add),
                            ['dent', 's2t'], ['dent'])
                        P.emit('V', lambda E: E.reciprocal(
                            rdent[:, 0:1], dent[:, 0:1]), ['dent'], ['rdent'])
                        P.emit('V', lambda E, tsl=tsl: E.tensor_scalar(
                            gfin[:, tsl], gem[:, :], rdent[:, 0:1], None,
                            ALU.mult), ['gem'], ['gfin'], sreads=['rdent'])
                        if debug and sq_i == 0 and q == 0 and tb == 0:
                            st(lambda E: E.dma_start(out=dbg['d_geg'][:, :],
                                                     in_=geg[:, :]), ['geg'])
                            st(lambda E: E.dma_start(out=dbg['d_gt2'][:, :],
                                                     in_=gt2[:, :]), ['gt2'])
                            st(lambda E: E.dma_start(out=dbg['d_gmsk'][:, :],
                                                     in_=gmsk[:, :]), ['gmsk'])
                            st(lambda E: E.dma_start(out=dbg['d_gem'][:, :],
                                                     in_=gem[:, :]), ['gem'])
                            st(lambda E: E.dma_start(out=dbg['d_gsc2'][:, :],
                                                     in_=gsc[:, :]), ['gsc'])
                    for tb4 in range(4):
                        P.emit('T', lambda E, tb4=tb4: E.transpose(
                            ps[2][0:8, tb4 * 128:(tb4 + 1) * 128],
                            gfin[:, tb4 * 8:(tb4 + 1) * 8], ident[:, :]),
                            ['gfin', 'ident'], [PK[2]])
                    P.emit('V', lambda E: E.tensor_copy(
                        gate8[0:8, :], ps[2][0:8, 0:QW]), [PK[2]], ['rl'])
                    if debug and sq_i == 0 and q == 0:
                        st(lambda E: E.dma_start(out=dbg['d_gf'][:, :],
                                                 in_=gfin[:, 0:16]), ['gfin'])
                    for g in range(2):
                        gb = 2 + g
                        hb = 6 + g
                        P.emit('T', lambda E, g=g, gb=gb: E.matmul(
                            ps[gb][:, 0:QW], selE[:, g * 128:(g + 1) * 128],
                            gate8[0:8, :], start=True, stop=True),
                            ['rl', 'selE'], [PK[gb]])
                        P.emit('V', lambda E, g=g, hb=hb: E.tensor_tensor(
                            hsb[g][:, :], ps[hb][:, 0:QW], rrep[:, :],
                            ALU.mult), [PK[hb], 'rrep'], [HSK[g]])
                        P.emit('V', lambda E, g=g: E.scalar_tensor_tensor(
                            hsb[g][:, :], brep[:, :], negwsA[:, g:g + 1],
                            hsb[g][:, :], ALU.mult, ALU.add),
                            [HSK[g]], [HSK[g]], sreads=['brep', 'negwsA'])
                        P.emit('V', lambda E, g=g, gb=gb: E.tensor_tensor(
                            hsb[g][:, :], hsb[g][:, :], ps[gb][:, 0:QW],
                            ALU.mult), [PK[gb], HSK[g]], [HSK[g]])
                    if debug and sq_i == 0 and q == 0:
                        for g in range(2):
                            st(lambda E, g=g: E.dma_start(
                                out=dbg['d_hs'][:, g * HW:(g + 1) * HW],
                                in_=hsb[g][:, 0:HW]), [HSK[g]])
                    for oc in range(ND):
                        bk = 4 + oc if oc < 2 else oc % 2
                        sqt = sqs if oc % 2 == 0 else pbuf[1]
                        sk = ['sqsA', 'sqsB'] if oc % 2 == 0 else ['pb1']
                        if oc >= 2:
                            for i in range(ND):
                                P.emit('T', lambda E, bk=bk, i=i, oc=oc, cs=cs:
                                       E.matmul(
                                    ps[bk][:, 0:QW],
                                    wTs[i][:, oc * 128:(oc + 1) * 128],
                                    xb[i][:, cs], start=(i == 0),
                                    stop=(i == ND - 1)),
                                    [f'wt{i}', XK(i, q)], [PK[bk]])
                        P.emit('T', lambda E, oc=oc: E.matmul(
                            ps[7][:, 0:QW], bTs[0][:, oc * 128:(oc + 1) * 128],
                            hsb[0][:, :], start=True, stop=False),
                            ['bb0', 'pb2'], [PK[7]])
                        P.emit('T', lambda E, oc=oc: E.matmul(
                            ps[7][:, 0:QW], bTs[1][:, oc * 128:(oc + 1) * 128],
                            hsb[1][:, :], start=False, stop=True),
                            ['bb1', 'pb3'], [PK[7]])
                        P.emit('V', lambda E, bk=bk, oc=oc: E.tensor_tensor(
                            ab[oc][:, :], ps[bk][:, 0:QW], rrep[:, :],
                            ALU.mult), [PK[bk], 'rrep'], [f'ab{oc}'])
                        P.emit('V', lambda E, oc=oc: E.scalar_tensor_tensor(
                            ab[oc][:, :], brep[:, :], negwsW[:, oc:oc + 1],
                            ab[oc][:, :], ALU.mult, ALU.add),
                            [f'ab{oc}'], [f'ab{oc}'], sreads=['brep', 'negwsW'])
                        P.emit('V', lambda E, oc=oc: E.tensor_tensor(
                            ab[oc][:, :], ab[oc][:, :], ps[7][:, 0:QW],
                            ALU.add), [PK[7], f'ab{oc}'], [f'ab{oc}'])
                        P.emit('S', lambda E, oc=oc: E.activation(
                            ab[oc][:, :], ab[oc][:, :], AF.Identity,
                            bias=bcols[:, oc:oc + 1]),
                            ['bcols', f'ab{oc}'], [f'ab{oc}'])
                        P.emit('S', lambda E, oc=oc, sqt=sqt: E.activation(
                            sqt[:, :], ab[oc][:, :], AF.Square),
                            [f'ab{oc}'], sk)
                        P.emit('T', lambda E, oc=oc: E.matmul(
                            ps[2][0:1, 0:QW], onesc[:, :], ab[oc][:, :],
                            start=(oc == 0), stop=(oc == ND - 1),
                            tile_position=(0, 0)),
                            [f'ab{oc}', 'onesc'], [PK[2]])
                        P.emit('T', lambda E, oc=oc, sqt=sqt: E.matmul(
                            ps[3][32:33, 0:QW], onesc[:, :], sqt[:, :],
                            start=(oc == 0), stop=(oc == ND - 1),
                            tile_position=(0, 32)), sk + ['onesc'],
                            [PK[3]])
                    ln_stats(QW, 2, 3)
                    for i in range(ND):
                        P.emit('V', lambda E, i=i: E.tensor_tensor(
                            ab[i][:, :], ab[i][:, :], rrep[:, :], ALU.mult),
                            ['rrep', f'ab{i}'], [f'ab{i}'])
                        P.emit('V', lambda E, i=i: E.tensor_tensor(
                            ab[i][:, :], ab[i][:, :], brep[:, :],
                            ALU.subtract), ['brep', f'ab{i}'], [f'ab{i}'])
                        P.emit('V', lambda E, i=i, cs=cs: E.tensor_tensor(
                            xb[i][:, cs], xb[i][:, cs], ab[i][:, :], ALU.add),
                            [f'ab{i}', XK(i, q)], [XK(i, q)])
                        st(lambda E, i=i, cs=cs, sq_i=sq_i: E.dma_start(
                            out=xout[sq_i, i * 128:(i + 1) * 128, cs],
                            in_=xb[i][:, cs]), [XK(i, q)])
                    P.dma_fence_r('Y', 'ot',
                                  [XK(i, q) for i in range(ND)])
                    if sq_i + 1 < SPC:
                        for i in range(ND):
                            ld(lambda E, i=i, q=q, sq_i=sq_i: E.dma_start(
                                out=xb[i][:, q * QW:(q + 1) * QW],
                                in_=xin[sq_i + 1, i * 128:(i + 1) * 128,
                                        q * QW:(q + 1) * QW]), [XK(i, q)])
                        P.dma_fence('Y', 'ld', [XK(i, q) for i in range(ND)])

        if debug2:
            st(lambda E: E.dma_start(out=dbg['pm_ws'][:, 0:4], in_=negws[:, :], allow_slow_non_contiguous=True),
               ['negws'])
            st(lambda E: E.dma_start(out=dbg['pm_ws'][:, 4:5],
                                     in_=negwskv[:, :], allow_slow_non_contiguous=True), ['negwskv'])
            st(lambda E: E.dma_start(out=dbg['pm_ws'][:, 5:13],
                                     in_=negwsW[:, :], allow_slow_non_contiguous=True), ['negwsW'])
            st(lambda E: E.dma_start(out=dbg['pm_ws'][:, 13:15],
                                     in_=negwsA[:, :], allow_slow_non_contiguous=True), ['negwsA'])
            for i in range(ND):
                st(lambda E, i=i: E.dma_start(
                    out=dbg['pm_wqk'][:, i * 512:(i + 1) * 512],
                    in_=wqkT[i][:, :]), [f'wqkT{i}'])
            for j in range(NHC):
                st(lambda E, j=j: E.dma_start(
                    out=dbg['pm_wov'][:, j * 1024:(j + 1) * 1024],
                    in_=wovT[j][:, :]), [f'wovT{j}'])
            st(lambda E: E.dma_start(out=dbg['pm_kvl'][:, :], in_=kvl[:, :]),
               [f'kvl{qq}' for qq in range(NQ)])
            st(lambda E: E.dma_start(out=dbg['pm_lnrb'][:, :], in_=lnrb[:, :]),
               ['lnrb'])
        P.items.append(('Y', tuple([('ot', P.val.get('ot', 0)),
                                    ('gd', P.val.get('gd', 0))]), None, None, 0))

        def walker(engname):
            def go(E):
                waited = {}
                for (en_, waits, fn, sem, step) in P.items:
                    if en_ != engname:
                        continue
                    for (sn, vv) in waits:
                        if waited.get(sn, -1) < vv:
                            E.wait_ge(sems[sn], vv)
                            waited[sn] = vv
                    if fn is None:
                        continue
                    ins = fn(E)
                    ins.then_inc(sems[sem], step)
            return go

        @block.sync
        def _(E):
            walker('Y')(E)

        @block.tensor
        def _(E):
            walker('T')(E)

        @block.vector
        def _(E):
            walker('V')(E)

        @block.scalar
        def _(E):
            walker('S')(E)

        @block.gpsimd
        def _(E):
            walker('G')(E)

    return nc


def _host_prep(L0_attn_x, L0_attn_Wq, L0_attn_Wkv_down, L0_attn_Wk_up,
               L0_attn_Wv_up, L0_attn_Wo, L0_ffn_W, L0_ffn_b, L0_ffn_R,
               L0_ffn_A, L0_ffn_B):
    f = np.float32
    x = np.ascontiguousarray(np.asarray(L0_attn_x, f).transpose(0, 2, 1))
    wq = np.ascontiguousarray(np.asarray(L0_attn_Wq, f))
    wkup = np.ascontiguousarray(np.asarray(L0_attn_Wk_up, f))
    wvup = np.ascontiguousarray(np.asarray(L0_attn_Wv_up, f))
    woT = np.ascontiguousarray(np.asarray(L0_attn_Wo, f).T)
    wkvT = np.asarray(L0_attn_Wkv_down, f).T
    wkvT2 = np.ascontiguousarray(np.concatenate([wkvT, wkvT], axis=1))
    wT = np.ascontiguousarray(np.asarray(L0_ffn_W, f).T)
    rT = np.ascontiguousarray(np.asarray(L0_ffn_R, f).T)
    A = np.asarray(L0_ffn_A, f)
    aT = np.zeros((1024, 256), f)
    for ex in range(8):
        g, ei = divmod(ex, 4)
        aT[:, g * 128 + ei * 32: g * 128 + ei * 32 + 16] = A[ex].T
    Bw = np.asarray(L0_ffn_B, f)
    bT = np.zeros((256, 1024), f)
    for ex in range(8):
        g, ei = divmod(ex, 4)
        bT[g * 128 + ei * 32: g * 128 + ei * 32 + 16, :] = Bw[ex].T
    bcol = np.ascontiguousarray(np.asarray(L0_ffn_b, f).reshape(1024, 1))
    identc = np.eye(128, dtype=f)
    onesc = np.ones((128, 1), f)
    onesr = np.ones((1, 128), f)
    selL = np.zeros((8, 512), f)
    for j in range(4):
        selL[2 * j, j * 128:j * 128 + 64] = 1.0
        selL[2 * j + 1, j * 128 + 64:j * 128 + 128] = 1.0
    selE = np.zeros((8, 256), f)
    for g in range(2):
        for ex in range(4):
            selE[g * 4 + ex, g * 128 + ex * 32:g * 128 + (ex + 1) * 32] = 1.0
    base = dict(wq=wq, wkup=wkup, wvup=wvup, woT=woT, wkvT2=wkvT2, wTd=wT,
                rTd=rT, aTd=aT, bTd=bT, bcold=bcol, identd=identc,
                onescd=onesc, onesrd=onesr, selLd=selL, selEd=selE,
                epsd=np.full((128, 1), LNEPS, f))
    in_maps = []
    for c in range(NCORES):
        m = dict(base)
        m["xin"] = np.ascontiguousarray(x[c * SPC:(c + 1) * SPC])
        in_maps.append(m)
    return in_maps


def kernel(**inputs):
    debug = bool(inputs.pop("_debug", False))
    debug2 = bool(inputs.pop("_debug2", False))
    in_maps = _host_prep(**inputs)
    nc = build(debug=debug, debug2=debug2)
    res = run_bass_kernel_spmd(nc, in_maps, core_ids=list(range(NCORES)))
    outs = []
    for c in range(NCORES):
        xo = np.asarray(res.results[c]["xout"])
        outs.append(xo.transpose(0, 2, 1))
    full = np.ascontiguousarray(np.concatenate(outs, axis=0).astype(np.float32))
    if debug or debug2:
        kernel._dbg = res.results
    return full
